# version 29
# baseline (speedup 1.0000x reference)
import numpy as np
import ml_dtypes
import concourse.bass as bass
import concourse.mybir as mybir
from concourse.bass_utils import run_bass_kernel_spmd
from concourse.alu_op_type import AluOpType
from contextlib import ExitStack

F32 = mybir.dt.float32
BF16 = mybir.dt.bfloat16
U8 = mybir.dt.uint8
AF = mybir.ActivationFunctionType
AX = mybir.AxisListType.X
OP = AluOpType

D = 2048
HD = 128
HLOC = 8
KVLOC = 2
RANK = 16
RMS_EPS = 1.1920929e-07
LN_EPS = 1e-5
INV_SQRT_HD = float(HD) ** -0.5

SEMS = (["sDq", "sDs", "sDg", "sPE", "sSC", "sDV", "sGP", "sCC", "sPH"]
        + [f"c{i}" for i in range(12)])
ENGS = ["sq", "sc", "dv", "gp", "pe"]


def later(*evs):
    out = []
    for e in evs:
        if e is None:
            continue
        if isinstance(e, list):
            out.extend(x for x in e if x is not None)
        else:
            out.append(e)
    return out


class Planner:
    def __init__(self):
        self.recs = {e: [] for e in ENGS}
        self.cnt = {s: 0 for s in SEMS}
        self.gate_no = 0
        self.last_gate = {e: 0 for e in ENGS}

    def op(self, eng, fn, w=(), inc=None, n=1):
        waits = [x for x in w if x is not None and x[1] > 0]
        if self.last_gate[eng] < self.gate_no:
            waits.insert(0, ("sPH", self.gate_no))
            self.last_gate[eng] = self.gate_no
        self.recs[eng].append((fn, waits, inc, n))
        if inc is not None:
            self.cnt[inc] += n
            return (inc, self.cnt[inc])
        return None

    def dma(self, q, dst, src, w=(), ch=None):
        sem = ch or {"sq": "sDq", "sc": "sDs", "gp": "sDg", "dv": "sDg"}[q]
        return self.op(q, lambda e, d=dst, s=src: e.dma_start(out=d, in_=s),
                       w=w, inc=sem, n=16)

    def gate(self):
        waits = [(s, self.cnt[s]) for s in SEMS if s != "sPH" and self.cnt[s] > 0]
        self.recs["gp"].append((lambda e: e.nop(), waits, "sPH", 1))
        self.cnt["sPH"] += 1
        self.gate_no = self.cnt["sPH"]
        self.last_gate["gp"] = self.gate_no

    def emit(self, block, sems):
        eng_of = {"sq": block.sync, "sc": block.scalar, "dv": block.vector,
                  "gp": block.gpsimd, "pe": block.tensor}
        for name in ENGS:
            recs = self.recs[name]

            def make(recs=recs):
                def body(e):
                    for fn, waits, inc, n in recs:
                        for s, v in waits:
                            e.wait_ge(sems[s], v)
                        inst = fn(e)
                        if inc is not None:
                            inst.then_inc(sems[inc], n)
                return body
            eng_of[name](make())


class LN:

    def __init__(self, P, nc, es, tag, eps_ap, alloc_sq=True):
        self.P = P
        self.eps = eps_ap
        self.stm = es.enter_context(nc.sbuf_tensor(f"stm_{tag}", [128, 4], F32))
        self.sq = (es.enter_context(nc.sbuf_tensor(f"sq_{tag}", [128, D], F32))
                   if alloc_sq else None)
        self.free = None

    def norm(self, src_ap, out_ap, w_src, w_out, ND, sq_ap=None):
        P, stm = self.P, self.stm
        sq = sq_ap if sq_ap is not None else self.sq[:]
        v1 = P.op("dv", lambda e: e.reduce_sum(stm[:, 0:1], src_ap, axis=AX),
                  w=later(*w_src, self.free), inc="sDV")
        v2 = P.op("sc", lambda e: e.square(sq, src_ap),
                  w=later(*w_src, self.free, *w_out), inc="sSC")
        v3 = P.op("dv", lambda e: e.reduce_sum(stm[:, 1:2], sq, axis=AX),
                  w=[v2], inc="sDV")
        P.op("dv", lambda e: e.drain())
        P.op("dv", lambda e: e.tensor_scalar(stm[:, 2:3], stm[:, 0:1],
                                             1.0 / ND, None, OP.mult))
        P.op("dv", lambda e: e.tensor_scalar(stm[:, 1:2], stm[:, 1:2],
                                             1.0 / ND, None, OP.mult))
        P.op("dv", lambda e: e.drain())
        P.op("dv", lambda e: e.tensor_tensor(stm[:, 0:1], stm[:, 2:3],
                                             stm[:, 2:3], OP.mult))
        P.op("dv", lambda e: e.drain())
        v5 = P.op("dv", lambda e: e.tensor_tensor(stm[:, 1:2], stm[:, 1:2],
                                                  stm[:, 0:1], OP.subtract), inc="sDV")
        v6 = P.op("sc", lambda e: e.activation(stm[:, 0:1], stm[:, 1:2],
                                               AF.Ln, bias=self.eps), w=[v5], inc="sSC")
        P.op("sc", lambda e: e.drain())
        v7 = P.op("sc", lambda e: e.activation(stm[:, 3:4], stm[:, 0:1],
                                               AF.Exp, scale=-0.5), inc="sSC")
        v8 = P.op("dv", lambda e, o=out_ap: e.tensor_scalar(
            o, src_ap, stm[:, 2:3], stm[:, 3:4], OP.subtract, OP.mult),
            w=later(v7, *w_out), inc="sDV")
        self.free = [v8, v7]
        return v8, [v8, v2]


def build(T, I):
    To = T // 2
    NQC = T // 128
    NPASS = T // 512
    NIC = I // 128
    NOCT = NIC // 8
    FP = 512
    NFP = To // FP
    FTC = FP // 128
    NTC = To // 128

    nc = bass.Bass("TRN2", target_bir_lowering=False, debug=False)
    din = lambda nm, sh, d=F32: nc.dram_tensor(nm, sh, d, kind="ExternalInput").ap()
    xT = din("xT", [D, T])
    x_own = din("x_own", [To, D])
    wq = din("wq", [16, 128, HLOC * HD])
    wk = din("wk", [16, 128, KVLOC * HD])
    wv = din("wv", [16, 128, KVLOC * HD])
    wp = din("wp", [HLOC, 128, D])
    qa = din("qa", [16, 128, RANK])
    ka = din("ka", [16, 128, RANK])
    va = din("va", [16, 128, RANK])
    pa = din("pa", [HLOC, 128, RANK])
    qb = din("qb", [RANK, HLOC * HD])
    kb = din("kb", [RANK, KVLOC * HD])
    vb = din("vb", [RANK, KVLOC * HD])
    pb = din("pb", [RANK, D])
    qg = din("qg", [1, HLOC])
    rope = din("rope", [64, T], BF16)
    ropeb = din("ropeb", [64, T], BF16)
    maskt = din("maskt", [128, 128])
    iden = din("iden", [128, 128], BF16)
    gt8 = din("gt8", [NIC, 128, D], U8)
    ut8 = din("ut8", [NIC, 128, D], U8)
    dt8 = din("dt8", [NIC, 128, D], U8)
    gsc = din("gsc", [128, NIC])
    usc = din("usc", [128, NIC])
    dscT = din("dscT", [128, D])
    epst = din("epst", [128, 2])
    out_d = nc.dram_tensor("out", [To, D], F32, kind="ExternalOutput").ap()
    mq_dram = nc.dram_tensor("mq_dram", [HLOC * NQC, 128], BF16).ap()
    cc_in = nc.dram_tensor("cc_in", [T, D], BF16)
    gbf_d = nc.dram_tensor("gbf_d", [NIC, 128, D], BF16).ap()
    ubf_d = nc.dram_tensor("ubf_d", [NIC, 128, D], BF16).ap()
    dbf_d = nc.dram_tensor("dbf_d", [NIC, 128, D], BF16).ap()
    cc_out = nc.dram_tensor("cc_out", [To, D], BF16)

    P = Planner()
    es0 = ExitStack()
    with es0:
        sems = {s: es0.enter_context(nc.semaphore(s)) for s in SEMS}
        SB = lambda nm, sh, d=F32: nc.sbuf_tensor(nm, sh, d)
        PS = lambda nm, sh, d=F32: nc.psum_tensor(nm, sh, d)

        eps_sb = es0.enter_context(SB("eps_sb", [128, 2]))
        iden_sb = es0.enter_context(SB("iden_sb", [128, 128], BF16))
        mask_sb = es0.enter_context(SB("mask_sb", [128, 128]))
        qg_sb = es0.enter_context(SB("qg_sb", [1, HLOC]))
        gsc_sb = es0.enter_context(SB("gsc_sb", [128, NIC]))
        usc_sb = es0.enter_context(SB("usc_sb", [128, NIC]))
        ones_sb = es0.enter_context(SB("ones_sb", [128, 1], BF16))
        nones_sb = es0.enter_context(SB("nones_sb", [128, 1], BF16))
        onesk1 = es0.enter_context(SB("onesk1", [1, 128], BF16))
        pa_bf = es0.enter_context(SB("pa_bf", [128, HLOC * RANK], BF16))
        pb_bf = es0.enter_context(SB("pb_bf", [RANK, D], BF16))

        es_qy = ExitStack()
        q_sb = es_qy.enter_context(SB("q_sb", [128, HLOC * T], BF16))
        y_sb = q_sb
        es_kv = ExitStack()
        k_sb = es_kv.enter_context(SB("k_sb", [128, KVLOC * T], BF16))
        v_sb = es_kv.enter_context(SB("v_sb", [128, (T // 128) * KVLOC * HD], BF16))

        es_p1 = ExitStack()
        ropeA_t = es_p1.enter_context(SB("ropeA_t", [64, 512], BF16))
        ropeB_t = es_p1.enter_context(SB("ropeB_t", [64, 512], BF16))
        qa_bf = es_p1.enter_context(SB("qa_bf", [128, 16 * RANK], BF16))
        ka_bf = es_p1.enter_context(SB("ka_bf", [128, 16 * RANK], BF16))
        va_bf = es_p1.enter_context(SB("va_bf", [128, 16 * RANK], BF16))
        qb_bf = es_p1.enter_context(SB("qb_bf", [RANK, HLOC * HD], BF16))
        kb_bf = es_p1.enter_context(SB("kb_bf", [RANK, KVLOC * HD], BF16))
        vb_bf = es_p1.enter_context(SB("vb_bf", [RANK, KVLOC * HD], BF16))
        xstage = [es_p1.enter_context(SB(f"xstage{i}", [128, 1024])) for i in range(2)]
        wq_bf = es_p1.enter_context(SB("wq_bf", [128, 16 * HLOC * HD], BF16))
        wk_bf = es_p1.enter_context(SB("wk_bf", [128, 16 * KVLOC * HD], BF16))
        wv_bf = es_p1.enter_context(SB("wv_bf", [128, 16 * KVLOC * HD], BF16))

        for dpair in [(eps_sb[:], epst), (mask_sb[:], maskt), (qg_sb[:], qg),
                      (gsc_sb[:], gsc), (usc_sb[:], usc),
                      (iden_sb[:], iden)]:
            P.dma("sq", dpair[0], dpair[1])
        P.op("dv", lambda e: e.memset(ones_sb[:], 1.0))
        P.op("dv", lambda e: e.memset(nones_sb[:], -1.0))
        P.op("dv", lambda e: e.memset(onesk1[:], 1.0), inc="sDV")

        jobs = []
        for fc in range(16):
            jobs.append((wq[fc], 128, HLOC * HD, wq_bf, fc * HLOC * HD))
        for i in range(4):
            jobs.append((wk.rearrange("c p n -> p c n")[:, i * 4:(i + 1) * 4, :],
                         128, 4 * KVLOC * HD, wk_bf, i * 4 * KVLOC * HD))
            jobs.append((wv.rearrange("c p n -> p c n")[:, i * 4:(i + 1) * 4, :],
                         128, 4 * KVLOC * HD, wv_bf, i * 4 * KVLOC * HD))
        jobs += [(qa.rearrange("c p n -> p c n"), 128, 16 * RANK, qa_bf, 0),
                 (ka.rearrange("c p n -> p c n"), 128, 16 * RANK, ka_bf, 0),
                 (va.rearrange("c p n -> p c n"), 128, 16 * RANK, va_bf, 0),
                 (pa.rearrange("c p n -> p c n"), 128, HLOC * RANK, pa_bf, 0),
                 (qb, RANK, HLOC * HD, qb_bf, 0),
                 (kb, RANK, KVLOC * HD, kb_bf, 0),
                 (vb, RANK, KVLOC * HD, vb_bf, 0),
                 (pb[:, 0:1024], RANK, 1024, pb_bf, 0),
                 (pb[:, 1024:2048], RANK, 1024, pb_bf, 1024)]
        xst_ev = [None, None]
        for j, (src, rows, cols, dst, off) in enumerate(jobs):
            buf = j % 2
            st = xstage[buf]
            if len(src.shape) == 3:
                d_ap = st[:, 0:cols].rearrange("p (c n) -> p c n", c=src.shape[1])
            else:
                d_ap = st[0:rows, 0:cols]
            v = P.dma("sq", d_ap, src, w=later(xst_ev[buf]), ch=f"c{buf}")
            xst_ev[buf] = P.op(
                "dv", lambda e, a=st[0:rows, 0:cols], b=dst[0:rows, off:off + cols]:
                e.tensor_copy(b, a), w=[v], inc="sDV")
        P.gate()

        x_fm = es_p1.enter_context(SB("x_fm", [128, 16 * 512], BF16))
        qsq = [es_p1.enter_context(SB(f"qsq{i}", [128, 512], BF16)) for i in range(2)]
        stat1 = es_p1.enter_context(SB("stat1", [1, 512]))
        stat2 = es_p1.enter_context(SB("stat2", [1, 512]))
        mqrow = es_p1.enter_context(SB("mqrow", [1, 512], BF16))
        mkbr = es_p1.enter_context(SB("mkbr", [1, 512], BF16))
        bcev = es_p1.enter_context(SB("bcev", [128, 512], BF16))
        qn_sb = [es_p1.enter_context(SB(f"qn{i}", [128, 512], BF16)) for i in range(2)]
        rs1 = es_p1.enter_context(SB("rs1", [64, 512], BF16))
        rs2 = es_p1.enter_context(SB("rs2", [64, 512], BF16))
        rsa = es_p1.enter_context(SB("rsa", [32, 512], BF16))
        rsb = es_p1.enter_context(SB("rsb", [32, 512], BF16))
        t1q = es_p1.enter_context(SB("t1q", [RANK, 512], BF16))
        t1k = es_p1.enter_context(SB("t1k", [RANK, 512], BF16))
        t1v = es_p1.enter_context(SB("t1v", [RANK, 512], BF16))
        es_pa = ExitStack()
        qk_ps = [es_pa.enter_context(PS(f"qk_ps{i}", [128, 512])) for i in range(3)]
        t1_ps = es_pa.enter_context(PS("t1_ps", [RANK, 512]))
        sq_ps = [es_pa.enter_context(PS("sq_ps0", [1, 512]))]
        bc_ps = es_pa.enter_context(PS("bc_ps", [128, 512]))
        v_ps = [es_pa.enter_context(PS(f"v_ps{i}", [128, 512])) for i in range(2)]

        xfm_rd = None
        qkfree = [None, None, None]
        t1free = None
        sqfree = [None]
        bcfree = None
        vpsfree = [None, None]
        qsqfree = [None, None]
        qnfree = [None, None]
        stat2_free = None
        mqrow_free = [None]

        def rms_stats(pi, seg, wpe):
            nonlocal stat2_free
            qb_ = pi % 2
            vsc = P.op("sc", lambda e, p=qk_ps[pi], q=qsq[qb_]: e.square(q[:], p[:]),
                       w=later(wpe, qsqfree[qb_]), inc="sSC")
            vpe = P.op("pe", lambda e, sp=sq_ps[seg], q=qsq[qb_]: e.matmul(
                sp[:], ones_sb[:], q[:], start=True, stop=True),
                w=later(vsc, sqfree[seg]), inc="sPE")
            qsqfree[qb_] = vpe
            v1 = P.op("sc", lambda e, sp=sq_ps[seg]: e.activation(
                stat1[:], sp[:], AF.Ln, bias=eps_sb[0:1, 0:1], scale=1.0 / HD),
                w=later(vpe), inc="sSC")
            sqfree[seg] = v1
            P.op("sc", lambda e: e.drain())
            v2 = P.op("sc", lambda e: e.activation(
                stat2[:], stat1[:], AF.Exp, scale=-0.5),
                w=later(stat2_free), inc="sSC")
            return v2

        rope_ev = [None, None]

        def rope_emit(qb_, dst, doff, pcol, wsrc):
            src = qn_sb[qb_]
            csn = ropeA_t[:, :]
            snc = ropeB_t[:, :]
            P.op("dv", lambda e: e.drain())
            P.op("dv", lambda e: e.tensor_tensor(rs1[:, :], src[0:64, :], csn, OP.mult),
                 w=later(wsrc, rope_ev[0]))
            P.op("dv", lambda e: e.tensor_tensor(rs2[:, :], src[0:64, :], snc, OP.mult))
            P.op("dv", lambda e: e.drain())
            P.op("dv", lambda e: e.tensor_copy(rsa[:, :], rs1[32:64, :]))
            P.op("dv", lambda e: e.tensor_copy(rsb[:, :], rs2[32:64, :]))
            P.op("dv", lambda e: e.drain())
            P.op("dv", lambda e: e.tensor_tensor(
                dst[0:32, doff:doff + 512], rs1[0:32, :], rsa[:, :], OP.add))
            vd = P.op("dv", lambda e: e.tensor_tensor(
                dst[32:64, doff:doff + 512], rsb[:, :], rs2[0:32, :], OP.subtract),
                inc="sDV")
            rope_ev[1] = vd
            vg = P.op("gp", lambda e: e.tensor_copy(
                dst[64:128, doff:doff + 512], src[64:128, :]), w=later(wsrc), inc="sGP")
            return [vd, vg]

        for p in range(NPASS):
            t0 = p * 512
            P.dma("sq", ropeA_t[:, :], rope[:, t0:t0 + 512],
                  w=later(rope_ev[1]), ch="c3")
            rope_ev[0] = P.dma("sq", ropeB_t[:, :], ropeb[:, t0:t0 + 512], ch="c3")
            for fc in range(16):
                buf = fc % 2
                st = xstage[buf]
                v = P.dma("sq", st[:, 0:512], xT[fc * 128:(fc + 1) * 128, t0:t0 + 512],
                          w=later(xst_ev[buf], xfm_rd if fc < 2 else None),
                          ch=f"c{buf}")
                xst_ev[buf] = P.op(
                    "dv", lambda e, a=st[:, 0:512], b=x_fm[:, fc * 512:(fc + 1) * 512]:
                    e.tensor_copy(b, a), w=[v], inc="sDV")
            v_xf = ("sDV", P.cnt["sDV"])
            for asb, tsb in [(qa_bf, t1q), (ka_bf, t1k), (va_bf, t1v)]:
                for fc in range(16):
                    vpe = P.op("pe", lambda e, fc=fc, a=asb: e.matmul(
                        t1_ps[:], a[:, fc * RANK:(fc + 1) * RANK],
                        x_fm[:, fc * 512:(fc + 1) * 512],
                        start=(fc == 0), stop=(fc == 15)),
                        w=later(v_xf, t1free) if fc == 0 else (), inc="sPE")
                t1free = P.op("sc", lambda e, t=tsb: e.copy(t[:], t1_ps[:]),
                              w=[vpe], inc="sSC")
            v_t1 = t1free
            for h in range(HLOC):
                pi = h % 3
                for fc in range(16):
                    P.op("pe", lambda e, fc=fc, h=h, pp=qk_ps[pi]: e.matmul(
                        pp[:], wq_bf[:, fc * HLOC * HD + h * HD: fc * HLOC * HD + (h + 1) * HD],
                        x_fm[:, fc * 512:(fc + 1) * 512],
                        start=(fc == 0), stop=False),
                        w=later(qkfree[pi], v_t1) if fc == 0 else (), inc="sPE")
                vpe = P.op("pe", lambda e, h=h, pp=qk_ps[pi]: e.matmul(
                    pp[:], qb_bf[:, h * HD:(h + 1) * HD], t1q[:],
                    start=False, stop=True), inc="sPE")
                v2 = rms_stats(pi, 0, vpe)
                vmq = P.op("dv", lambda e, h=h: e.tensor_scalar(
                    mqrow[:], stat2[:], qg_sb[0:1, h:h + 1], INV_SQRT_HD,
                    OP.mult, OP.mult), w=later(v2, mqrow_free[0]), inc="sDV")
                stat2_free = vmq
                mqrow_free[0] = P.dma("sc",
                      mq_dram[h * NQC + p * 4: h * NQC + (p + 1) * 4, :]
                      .rearrange("(o c) n -> o c n", o=1),
                      mqrow[0:1, :].rearrange("o (c n) -> o c n", c=4), w=[vmq],
                      ch="c2")
                qb_ = h % 2
                vq = P.op("sc", lambda e, pp=qk_ps[pi], q=qn_sb[qb_]: e.copy(q[:], pp[:]),
                          w=later(vpe, qnfree[qb_]), inc="sSC")
                qkfree[pi] = vq
                qnfree[qb_] = rope_emit(qb_, q_sb, h * T + t0, t0, vq)
            for kv in range(KVLOC):
                pi = (HLOC + kv) % 3
                for fc in range(16):
                    P.op("pe", lambda e, fc=fc, kv=kv, pp=qk_ps[pi]: e.matmul(
                        pp[:], wk_bf[:, fc * KVLOC * HD + kv * HD: fc * KVLOC * HD + (kv + 1) * HD],
                        x_fm[:, fc * 512:(fc + 1) * 512],
                        start=(fc == 0), stop=False),
                        w=later(qkfree[pi]) if fc == 0 else (), inc="sPE")
                vpe = P.op("pe", lambda e, kv=kv, pp=qk_ps[pi]: e.matmul(
                    pp[:], kb_bf[:, kv * HD:(kv + 1) * HD], t1k[:],
                    start=False, stop=True), inc="sPE")
                v2 = rms_stats(pi, 0, vpe)
                vmk = P.op("dv", lambda e: e.tensor_copy(mkbr[:], stat2[:]),
                           w=[v2], inc="sDV")
                stat2_free = vmk
                vbc = P.op("pe", lambda e: e.matmul(
                    bc_ps[:], onesk1[:], mkbr[:], start=True, stop=True),
                    w=later(vmk, bcfree), inc="sPE")
                vbs = P.op("sc", lambda e: e.copy(bcev[:], bc_ps[:]), w=[vbc], inc="sSC")
                bcfree = vbs
                qb_ = kv % 2
                vkn = P.op("dv", lambda e, pp=qk_ps[pi], q=qn_sb[qb_]: e.tensor_tensor(
                    q[:], pp[:], bcev[:], OP.mult),
                    w=later(vbs, qnfree[qb_]), inc="sDV")
                qkfree[pi] = vkn
                qnfree[qb_] = rope_emit(qb_, k_sb, kv * T + t0, t0, vkn)
            for tc in range(4):
                buf = tc % 2
                gchunk = p * 4 + tc
                for fc in range(16):
                    P.op("pe", lambda e, fc=fc, tc=tc, pp=v_ps[buf]: e.matmul(
                        pp[:, 0:KVLOC * HD],
                        x_fm[:, fc * 512 + tc * 128: fc * 512 + (tc + 1) * 128],
                        wv_bf[:, fc * KVLOC * HD:(fc + 1) * KVLOC * HD],
                        start=(fc == 0), stop=False),
                        w=later(vpsfree[buf]) if fc == 0 else (), inc="sPE")
                vpe = P.op("pe", lambda e, tc=tc, pp=v_ps[buf]: e.matmul(
                    pp[:, 0:KVLOC * HD], t1v[:, tc * 128:(tc + 1) * 128], vb_bf[:],
                    start=False, stop=True), inc="sPE")
                vpsfree[buf] = P.op(
                    "sc", lambda e, g=gchunk, pp=v_ps[buf]: e.copy(
                        v_sb[:, g * KVLOC * HD:(g + 1) * KVLOC * HD],
                        pp[:, 0:KVLOC * HD]), w=[vpe], inc="sSC")
            xfm_rd = ("sPE", P.cnt["sPE"])
        es_pa.close()
        es_p1.close()
        P.gate()

        es_p2 = ExitStack()
        w8st = [es_p2.enter_context(SB(f"w8st{i}", [128, D], U8)) for i in range(2)]
        wbfst = [es_p2.enter_context(SB(f"wbfst{i}", [128, D], BF16)) for i in range(2)]
        mqt_bf = es_p2.enter_context(SB("mqt_bf", [128, HLOC * NQC], BF16))
        mqt = es_p2.enter_context(SB("mqt", [128, HLOC * NQC]))
        s_sb = [es_p2.enter_context(SB(f"s_sb{i}", [128, T], BF16)) for i in range(3)]
        p_sb = [es_p2.enter_context(SB(f"p_sb{i}", [128, T], BF16)) for i in range(3)]
        pt_sb = [es_p2.enter_context(SB(f"pt_sb{i}", [128, 1024], BF16)) for i in range(2)]
        mxt = [es_p2.enter_context(SB(f"mxt{i}", [128, 8])) for i in range(3)]
        ngm = [es_p2.enter_context(SB(f"ngm{i}", [128, 1])) for i in range(3)]
        rsum = [es_p2.enter_context(SB(f"rsum{i}", [128, 1])) for i in range(3)]
        rinv = [es_p2.enter_context(SB(f"rinv{i}", [128, 1])) for i in range(3)]
        vsq_sb = es_p2.enter_context(SB("vsq_sb", [128, KVLOC * HD], BF16))
        sv_sb = es_p2.enter_context(SB("sv_sb", [128, 2 * KVLOC]))
        rv_sb = es_p2.enter_context(SB("rv_sb", [128, KVLOC]))
        vh_tok = es_p2.enter_context(SB("vh_tok", [128, KVLOC * HD], BF16))
        vh_fm = es_p2.enter_context(SB("vh_fm", [128, KVLOC * HD], BF16))
        ysb = [es_p2.enter_context(SB(f"ysb{i}", [128, 128], BF16)) for i in range(4)]
        zsb = es_p2.enter_context(SB("zsb", [128, 128], BF16))
        ndsb = es_p2.enter_context(SB("ndsb", [1, 128], BF16))
        prsb = es_p2.enter_context(SB("prsb", [128, 128], BF16))
        es_pb = ExitStack()
        s_ps = [es_pb.enter_context(PS(f"s_ps{i}", [128, 512])) for i in range(2)]
        pt_ps = [es_pb.enter_context(PS(f"pt_ps{i}", [128, 1024], BF16)) for i in range(2)]
        y_ps = [es_pb.enter_context(PS(f"y_ps{i}", [128, 512])) for i in range(2)]
        miscA = es_pb.enter_context(PS("miscA", [128, 512]))
        vtr_ps = es_pb.enter_context(PS("vtr_ps", [128, KVLOC * HD], BF16))

        vt = P.op("sq", lambda e: e.dma_start_transpose(mqt_bf[:], mq_dram[:, :]),
                  w=[("c2", P.cnt["c2"])], inc="c3", n=16)
        P.op("dv", lambda e: e.tensor_copy(mqt[:], mqt_bf[:]), w=[vt], inc="sDV")

        w8cv = [None, None]
        wbf_out = [None, None]
        sps_free = [None, None]
        ptps_free = [None, None]
        ptsb_free = [None, None]
        yps_free = [None, None]
        ssb_free = [None] * 3
        psb_free = [None] * 3
        mxt_free = [None] * 3
        ngm_free = [None] * 3
        rs_free = [None] * 3
        ri_free = [None] * 3
        st2 = {"vhat": None, "vhtok": None, "vtr": None, "vhfm": None,
               "z": None, "nd": None, "pr": None, "mA": None, "svf": None}
        ysb_free = [None] * 4
        ptrot = [0]

        def attn_head(qc, kv, g):
            h = kv * 4 + g
            ext = (qc + 1) * 128
            ng = (ext + 511) // 512
            sb = h % 3
            mb = h % 3
            col = h * NQC + qc
            scans = None
            for j in range(ng):
                w = min(512, ext - j * 512)
                pj = j % 2
                vpe = P.op("pe", lambda e, j=j, w=w, pj=pj, h=h, kv=kv, qc=qc: e.matmul(
                    s_ps[pj][:, 0:w],
                    q_sb[:, h * T + qc * 128: h * T + (qc + 1) * 128],
                    k_sb[:, kv * T + j * 512: kv * T + j * 512 + w],
                    start=True, stop=True),
                    w=later(sps_free[pj], ssb_free[sb] if j == 0 else None,
                            mxt_free[mb] if j == 0 else None), inc="sPE")
                scans = P.op("dv", lambda e, j=j, w=w, pj=pj, sb=sb, mb=mb, col=col:
                             e.tensor_scalar(
                                 s_sb[sb][:, j * 512: j * 512 + w], s_ps[pj][:, 0:w],
                                 mqt[:, col:col + 1], None, OP.mult, OP.max,
                                 accum_out=mxt[mb][:, j:j + 1]),
                             w=[vpe], inc="sDV")
                sps_free[pj] = scans
            vmask = P.op("dv", lambda e, sb=sb, qc=qc: e.tensor_tensor(
                s_sb[sb][:, qc * 128:(qc + 1) * 128],
                s_sb[sb][:, qc * 128:(qc + 1) * 128], mask_sb[:], OP.add),
                w=later(scans), inc="sDV")
            vngm = P.op("dv", lambda e, mb=mb, ng=ng: e.reduce_max(
                ngm[mb][:, 0:1], mxt[mb][:, 0:ng], axis=AX, negate=True),
                w=later(scans, ngm_free[mb]), inc="sDV")
            mxt_free[mb] = vngm
            vexp = P.op("sc", lambda e, sb=sb, mb=mb, ext=ext: e.activation(
                p_sb[sb][:, 0:ext], s_sb[sb][:, 0:ext], AF.Exp,
                bias=ngm[mb][:, 0:1], accum_out=rsum[mb][:, 0:1]),
                w=later(vmask, vngm, psb_free[sb], rs_free[mb]), inc="sSC")
            ssb_free[sb] = vexp
            ngm_free[mb] = vexp
            vri = P.op("dv", lambda e, mb=mb: e.reciprocal(
                rinv[mb][:, 0:1], rsum[mb][:, 0:1]),
                w=later(vexp, ri_free[mb]), inc="sDV")
            rs_free[mb] = vri
            vpn = P.op("dv", lambda e, sb=sb, mb=mb, ext=ext: e.tensor_scalar(
                p_sb[sb][:, 0:ext], p_sb[sb][:, 0:ext], rinv[mb][:, 0:1],
                None, OP.mult), w=[vri], inc="sDV")
            ri_free[mb] = vpn
            return dict(qc=qc, kv=kv, g=g, h=h, ext=ext, sb=sb, vpn=vpn)

        def attn_ptpv(ctx):
            qc, kv, g, sb = ctx["qc"], ctx["kv"], ctx["g"], ctx["sb"]
            yb = kv % 2
            nch = qc + 1
            groups = []
            for c0 in range(0, nch, 8):
                groups.append((c0, min(8, nch - c0)))

            def emit_T(c0, cw):
                tb = ptrot[0] % 2
                ptrot[0] += 1
                vtp = None
                for ci in range(cw):
                    c = c0 + ci
                    vtp = P.op("pe", lambda e, c=c, ci=ci, tb=tb, sb=sb: e.transpose(
                        pt_ps[tb][:, ci * 128:(ci + 1) * 128],
                        p_sb[sb][:, c * 128:(c + 1) * 128], iden_sb[:]),
                        w=later(ctx["vpn"] if ci == 0 else None,
                                ptps_free[tb] if ci == 0 else None), inc="sPE")
                vev = P.op("sc", lambda e, tb=tb, cw=cw: e.copy(
                    pt_sb[tb][:, 0:cw * 128], pt_ps[tb][:, 0:cw * 128]),
                    w=later(vtp, ptsb_free[tb]), inc="sSC")
                ptps_free[tb] = vev
                return tb, vev

            def emit_PV(c0, cw, tb, vev):
                vpv = None
                for ci in range(cw):
                    c = c0 + ci
                    vpv = P.op("pe", lambda e, c=c, ci=ci, tb=tb, kv=kv, g=g, yb=yb, nch=nch:
                               e.matmul(
                                   y_ps[yb][:, g * 128:(g + 1) * 128],
                                   v_sb[:, c * KVLOC * HD + kv * HD: c * KVLOC * HD + (kv + 1) * HD],
                                   pt_sb[tb][:, ci * 128:(ci + 1) * 128],
                                   start=(c == 0), stop=(c == nch - 1)),
                               w=later(vev if ci == 0 else None,
                                       yps_free[yb] if (c == 0 and g == 0) else None),
                               inc="sPE")
                ptsb_free[tb] = vpv
                return vpv

            prev = None
            vpv = None
            for c0, cw in groups:
                tb, vev = emit_T(c0, cw)
                if prev is not None:
                    vpv = emit_PV(*prev)
                prev = (c0, cw, tb, vev)
            vpv = emit_PV(*prev)
            psb_free[sb] = vpv
            ctx["vpv"] = vpv
            ctx["yb"] = yb
            return ctx

        def attn_tail(ctx):
            qc, kv, g, h, yb = ctx["qc"], ctx["kv"], ctx["g"], ctx["h"], ctx["yb"]
            vye = P.op("sc", lambda e, yb=yb, g=g: e.copy(
                ysb[g][:], y_ps[yb][:, g * 128:(g + 1) * 128]),
                w=later(ctx["vpv"], ysb_free[g]), inc="sSC")
            vz = P.op("dv", lambda e, kv=kv, g=g: e.tensor_tensor(
                zsb[:], ysb[g][:], vh_fm[:, kv * HD:(kv + 1) * HD], OP.mult),
                w=later(vye, st2["z"], ctx["vhfm"]), inc="sDV")
            vnd = P.op("pe", lambda e: e.matmul(
                miscA[0:1, 128:256], nones_sb[:], zsb[:], start=True, stop=True),
                w=later(vz, st2["mA"]), inc="sPE")
            st2["z"] = vnd
            vnde = P.op("dv", lambda e: e.tensor_copy(ndsb[:], miscA[0:1, 128:256]),
                        w=later(vnd, st2["nd"]), inc="sDV")
            vbc2 = P.op("pe", lambda e: e.matmul(
                miscA[:, 0:128], onesk1[:], ndsb[:], start=True, stop=True),
                w=[vnde], inc="sPE")
            st2["nd"] = vbc2
            vpr = P.op("dv", lambda e, kv=kv: e.tensor_tensor(
                prsb[:], vh_fm[:, kv * HD:(kv + 1) * HD], miscA[:, 0:128], OP.mult),
                w=later(vbc2, st2["pr"]), inc="sDV")
            st2["mA"] = vpr
            vyo = P.op("dv", lambda e, h=h, qc=qc, g=g: e.tensor_tensor(
                y_sb[:, h * T + qc * 128: h * T + (qc + 1) * 128],
                ysb[g][:], prsb[:], OP.add), w=[vpr], inc="sDV")
            st2["pr"] = vyo
            ysb_free[g] = vyo
            if g == 3:
                yps_free[yb] = vye
            return vyo

        icpq_num = NIC * 3
        for qc in range(NQC):
            j0 = (qc * icpq_num) // NQC
            j1 = ((qc + 1) * icpq_num) // NQC
            for j in range(j0, j1):
                ic, mat = divmod(j, 3)
                srcd, dstd = [(gt8, gbf_d), (ut8, ubf_d), (dt8, dbf_d)][mat]
                bi = j % 2
                vin = P.dma("sq", w8st[bi][:], srcd[ic], w=later(w8cv[bi]),
                            ch=f"c{10 + bi}")
                vcv = P.op("dv", lambda e, bi=bi: e.tensor_scalar(
                    wbfst[bi][:], w8st[bi][:], -128.0, None, OP.add),
                    w=later(vin, wbf_out[bi]), inc="sDV")
                w8cv[bi] = vcv
                wbf_out[bi] = P.dma("sc", dstd[ic], wbfst[bi][:], w=[vcv],
                                    ch=f"c{6 + bi}")
            vv = P.op("sc", lambda e, qc=qc: e.square(
                vsq_sb[:], v_sb[:, qc * KVLOC * HD:(qc + 1) * KVLOC * HD]),
                w=later(st2["vhat"]), inc="sSC")
            vred = None
            for kv in range(KVLOC):
                vred = P.op("dv", lambda e, kv=kv: e.reduce_sum(
                    sv_sb[:, kv:kv + 1], vsq_sb[:, kv * HD:(kv + 1) * HD], axis=AX),
                    w=later(vv, st2["svf"]) if kv == 0 else (), inc="sDV")
            st2["vhat"] = vred
            P.op("dv", lambda e: e.drain())
            vfl = P.op("dv", lambda e: e.tensor_scalar(
                sv_sb[:, KVLOC:2 * KVLOC], sv_sb[:, 0:KVLOC], 1e-24, None, OP.max),
                inc="sDV")
            vln = P.op("sc", lambda e: e.activation(
                sv_sb[:, 0:KVLOC], sv_sb[:, KVLOC:2 * KVLOC], AF.Ln),
                w=[vfl], inc="sSC")
            P.op("sc", lambda e: e.drain())
            vrv = P.op("sc", lambda e: e.activation(
                rv_sb[:], sv_sb[:, 0:KVLOC], AF.Exp, scale=-0.5), inc="sSC")
            st2["svf"] = vrv
            vhs = []
            for kv in range(KVLOC):
                vhs.append(P.op("dv", lambda e, kv=kv, qc=qc: e.tensor_scalar(
                    vh_tok[:, kv * HD:(kv + 1) * HD],
                    v_sb[:, qc * KVLOC * HD + kv * HD: qc * KVLOC * HD + (kv + 1) * HD],
                    rv_sb[:, kv:kv + 1], None, OP.mult),
                    w=later(vrv, st2["vhtok"]) if kv == 0 else (), inc="sDV"))
            vts = None
            for kv in range(KVLOC):
                vts = P.op("pe", lambda e, kv=kv: e.transpose(
                    vtr_ps[:, kv * HD:(kv + 1) * HD], vh_tok[:, kv * HD:(kv + 1) * HD],
                    iden_sb[:]),
                    w=later(vhs[kv], st2["vtr"] if kv == 0 else None), inc="sPE")
            st2["vhtok"] = vts
            vhf = P.op("dv", lambda e: e.tensor_copy(vh_fm[:], vtr_ps[:]),
                       w=later(vts, st2["vhfm_free"] if "vhfm_free" in st2 else None),
                       inc="sDV")
            st2["vtr"] = vhf
            ctxs = []
            for kv in range(KVLOC):
                for g in range(4):
                    c = attn_head(qc, kv, g)
                    c["vhfm"] = vhf
                    ctxs.append(c)
                    if len(ctxs) >= 3:
                        attn_ptpv(ctxs[-3])
            attn_ptpv(ctxs[-2])
            attn_ptpv(ctxs[-1])
            for c in ctxs:
                attn_tail(c)
            st2["vhfm_free"] = st2["pr"]
        es_pb.close()
        es_p2.close()
        es_kv.close()
        P.gate()

        es_p3 = ExitStack()
        wp_bf = es_p3.enter_context(SB("wp_bf", [128, HLOC * D], BF16))
        wpst = [es_p3.enter_context(SB(f"wpst{i}", [128, D])) for i in range(2)]
        lr_sb = es_p3.enter_context(SB("lr_sb", [RANK, T], BF16))
        cst = [es_p3.enter_context(SB(f"cst{i}", [128, 512], BF16)) for i in range(2)]
        es_pc = ExitStack()
        c_ps = [es_pc.enter_context(PS(f"c_ps{i}", [128, 512])) for i in range(2)]
        lr_ps = es_pc.enter_context(PS("lr_ps", [RANK, 512]))

        wpst_free = [None, None]
        for fc in range(HLOC):
            buf = fc % 2
            v = P.dma("sq", wpst[buf][:], wp[fc], w=later(wpst_free[buf]),
                      ch=f"c{buf}")
            wpst_free[buf] = P.op(
                "dv", lambda e, fc=fc, buf=buf: e.tensor_copy(
                    wp_bf[:, fc * D:(fc + 1) * D], wpst[buf][:]), w=[v], inc="sDV")
        v_wp = ("sDV", P.cnt["sDV"])
        lr_free = None
        for tp in range(T // 512):
            for fc in range(HLOC):
                vpe = P.op("pe", lambda e, tp=tp, fc=fc: e.matmul(
                    lr_ps[:], pa_bf[:, fc * RANK:(fc + 1) * RANK],
                    y_sb[:, fc * T + tp * 512: fc * T + (tp + 1) * 512],
                    start=(fc == 0), stop=(fc == HLOC - 1)),
                    w=later(lr_free) if fc == 0 else (), inc="sPE")
            lr_free = P.op("sc", lambda e, tp=tp: e.copy(
                lr_sb[:, tp * 512:(tp + 1) * 512], lr_ps[:]), w=[vpe], inc="sSC")
        v_lr = lr_free
        cps_free = [None, None]
        cst_free = [None, None]
        vcc = None
        ntcf = T // 128
        tpq = ntcf // 8
        for q in range(4):
            for tc in [q * tpq + i for i in range(tpq)] + \
                      [ntcf // 2 + q * tpq + i for i in range(tpq)]:
                if tc < ntcf // 2:
                    r0 = q * tpq * 256 + (tc - q * tpq) * 128
                else:
                    r0 = q * tpq * 256 + tpq * 128 + (tc - ntcf // 2 - q * tpq) * 128
                for fog in range(4):
                    buf = fog % 2
                    for fc in range(HLOC):
                        P.op("pe", lambda e, tc=tc, fog=fog, fc=fc, buf=buf: e.matmul(
                            c_ps[buf][:], y_sb[:, fc * T + tc * 128: fc * T + (tc + 1) * 128],
                            wp_bf[:, fc * D + fog * 512: fc * D + (fog + 1) * 512],
                            start=(fc == 0), stop=False),
                            w=later(cps_free[buf], v_wp if tc == 0 else None,
                                    v_lr if tc == 0 else None) if fc == 0 else (),
                            inc="sPE")
                    vpe = P.op("pe", lambda e, tc=tc, fog=fog, buf=buf: e.matmul(
                        c_ps[buf][:], lr_sb[:, tc * 128:(tc + 1) * 128],
                        pb_bf[:, fog * 512:(fog + 1) * 512],
                        start=False, stop=True), inc="sPE")
                    vev = P.op("sc", lambda e, buf=buf: e.copy(cst[buf][:], c_ps[buf][:]),
                               w=later(vpe, cst_free[buf]), inc="sSC")
                    cps_free[buf] = vev
                    cst_free[buf] = P.dma(
                        "sc", cc_in[r0:r0 + 128, fog * 512:(fog + 1) * 512],
                        cst[buf][:], w=[vev], ch=f"c{4 + buf}")
            v_allst = [("c4", P.cnt["c4"]), ("c5", P.cnt["c5"])]
            vcc = P.op("gp", lambda e, q=q: e.collective_compute(
                "ReduceScatter", mybir.AluOpType.add,
                replica_groups=[[0, 1], [2, 3], [4, 5], [6, 7]],
                ins=[cc_in[q * (T // 4):(q + 1) * (T // 4), :].opt()],
                outs=[cc_out[q * (T // 8):(q + 1) * (T // 8), :].opt()]),
                w=v_allst, inc="sCC")
        es_pc.close()
        es_p3.close()
        es_qy.close()
        P.gate()

        es_p4 = ExitStack()
        h_fm = es0.enter_context(SB("h_fm", [128, 16 * To], BF16))
        att_t = [es_p4.enter_context(SB(f"att{i}", [128, D], BF16)) for i in range(2)]
        xo_t = [es_p4.enter_context(SB(f"xo{i}", [128, D])) for i in range(2)]
        r1_t = es_p4.enter_context(SB("r1", [128, D]))
        htm = es_p4.enter_context(SB("htm", [128, D], BF16))
        ln1 = LN(P, nc, es_p4, "ln1", eps_sb[:, 1:2])
        es_pd = ExitStack()
        btr_ps = [es_pd.enter_context(PS(f"btr{i}", [128, 1024], BF16)) for i in range(2)]

        att_free = [None, None]
        xo_free = [None, None]
        r1_free = None
        htm_free = None
        btr_free = [None, None]
        for tc in range(NTC):
            buf = tc % 2
            va = P.dma("sq", att_t[buf][:], cc_out[tc * 128:(tc + 1) * 128, :],
                       w=later(vcc if tc < 2 else None, att_free[buf]),
                       ch=f"c{buf}")
            vx = P.dma("sq", xo_t[buf][:], x_own[tc * 128:(tc + 1) * 128, :],
                       w=later(xo_free[buf]), ch=f"c{2 + buf}")
            vr = P.op("dv", lambda e, buf=buf: e.tensor_tensor(
                r1_t[:], att_t[buf][:], xo_t[buf][:], OP.add),
                w=later(va, vx, r1_free), inc="sDV")
            att_free[buf] = vr
            xo_free[buf] = vr
            vh, r1_readers = ln1.norm(r1_t[:], htm[:], [vr], [htm_free], D)
            r1_free = r1_readers
            vtp_last = None
            for half in range(2):
                tb = half
                vtp = None
                for fc8 in range(8):
                    fc = half * 8 + fc8
                    vtp = P.op("pe", lambda e, fc=fc, fc8=fc8, tb=tb: e.transpose(
                        btr_ps[tb][:, fc8 * 128:(fc8 + 1) * 128],
                        htm[:, fc * 128:(fc + 1) * 128], iden_sb[:]),
                        w=later(vh if fc8 == 0 else None,
                                btr_free[tb] if fc8 == 0 else None), inc="sPE")
                vtp_last = vtp
                evs = []
                for fc8 in range(8):
                    fc = half * 8 + fc8
                    evs.append(P.op("dv", lambda e, fc=fc, fc8=fc8, tb=tb, tc=tc:
                                    e.tensor_copy(
                                        h_fm[:, fc * To + tc * 128: fc * To + (tc + 1) * 128],
                                        btr_ps[tb][:, fc8 * 128:(fc8 + 1) * 128]),
                                    w=[vtp] if fc8 == 0 else (), inc="sDV"))
                btr_free[tb] = evs[-1]
            htm_free = ("sPE", vtp_last[1])
        es_pd.close()
        es_p4.close()
        P.gate()

        es_p5 = ExitStack()
        acc = es_p5.enter_context(SB("acc", [128, FTC * D]))
        dw_bf = es_p5.enter_context(SB("dw_bf", [128, 8 * D], BF16))
        hmid = es_p5.enter_context(SB("hmid", [128, 8 * FP], BF16))
        gw_bf = [es_p5.enter_context(SB(f"gw_bf{i}", [128, D], BF16)) for i in range(3)]
        uw_bf = [es_p5.enter_context(SB(f"uw_bf{i}", [128, D], BF16)) for i in range(3)]
        silu_sb = es_p5.enter_context(SB("silu_sb", [128, FP], BF16))
        dsc_sb = es_p5.enter_context(SB("dsc_sb", [128, D]))
        htm2 = es_p5.enter_context(SB("htm2", [128, D], BF16))
        r2_t = es_p5.enter_context(SB("r2", [128, D]))
        outst = [es_p5.enter_context(SB(f"outst{i}", [128, D])) for i in range(2)]
        ln2 = LN(P, nc, es_p5, "ln2", eps_sb[:, 1:2], alloc_sq=False)
        es_pe_ = ExitStack()
        g_ps = es_pe_.enter_context(PS("g_ps", [128, FP]))
        u_ps = es_pe_.enter_context(PS("u_ps", [128, FP]))
        d_ps = [es_pe_.enter_context(PS(f"d_ps{i}", [128, 512])) for i in range(2)]
        bt2_ps = [es_pe_.enter_context(PS(f"bt2{i}", [128, 1024], BF16)) for i in range(2)]

        v_dsc = P.dma("sq", dsc_sb[:], dscT, ch="c6")
        gwf = [None, None, None]
        uwf = [None, None, None]
        dw_cvt = [None] * 8
        dw_used = None
        gpsf = None
        upsf = None
        dpsf = [None, None]
        hm_cvt = [None] * 8
        hm_used = None
        siluf = None
        accf = [None] * (FTC * 4)
        bt2f = [None, None]
        htm2f = None
        r2f = None
        outf = [None, None]

        for p in range(NFP):
            tp0 = p * FP
            for o in range(NOCT):
                for i8 in range(8):
                    ic = o * 8 + i8
                    bi = ic % 3
                    vgc = P.dma("sq", gw_bf[bi][:], gbf_d[ic], w=later(gwf[bi]),
                                ch=f"c{bi}")
                    vuc = P.dma("sc", uw_bf[bi][:], ubf_d[ic], w=later(uwf[bi]),
                                ch=f"c{8 + bi}")
                    vdc = P.dma("sc", dw_bf[:, i8 * D:(i8 + 1) * D], dbf_d[ic],
                                w=later(dw_used if i8 == 0 else None),
                                ch=f"c{4 + (i8 % 2)}")
                    dw_cvt[i8] = vdc
                    vgp = None
                    for fc in range(16):
                        vgp = P.op("pe", lambda e, fc=fc, bi=bi, tp0=tp0: e.matmul(
                            g_ps[:], gw_bf[bi][:, fc * 128:(fc + 1) * 128],
                            h_fm[:, fc * To + tp0: fc * To + tp0 + FP],
                            start=(fc == 0), stop=(fc == 15)),
                            w=later(vgc, gpsf) if fc == 0 else (), inc="sPE")
                    vup = None
                    for fc in range(16):
                        vup = P.op("pe", lambda e, fc=fc, bi=bi, tp0=tp0: e.matmul(
                            u_ps[:], uw_bf[bi][:, fc * 128:(fc + 1) * 128],
                            h_fm[:, fc * To + tp0: fc * To + tp0 + FP],
                            start=(fc == 0), stop=(fc == 15)),
                            w=later(vuc, upsf) if fc == 0 else (), inc="sPE")
                    gwf[bi] = vgp
                    uwf[bi] = vup
                    vsl = P.op("sc", lambda e, ic=ic: e.activation(
                        silu_sb[:], g_ps[:], AF.Silu, scale=gsc_sb[:, ic:ic + 1]),
                        w=later(vgp, siluf), inc="sSC")
                    gpsf = vsl
                    vhm = P.op("dv", lambda e, ic=ic, i8=i8: e.scalar_tensor_tensor(
                        hmid[:, i8 * FP:(i8 + 1) * FP], u_ps[:], usc_sb[:, ic:ic + 1],
                        silu_sb[:], OP.mult, OP.mult),
                        w=later(vup, vsl, hm_used if i8 == 0 else None), inc="sDV")
                    upsf = vhm
                    siluf = vhm
                    hm_cvt[i8] = vhm
                hm_max = ("sDV", max(v[1] for v in hm_cvt))
                dw_max4 = max(v[1] for v in dw_cvt if v[0] == "c4")
                dw_max5 = max((v[1] for v in dw_cvt if v[0] == "c5"), default=0)
                dw_max = ("c4", dw_max4)
                dw_max_b = ("c5", dw_max5) if dw_max5 else None
                vdw_pe = None
                for tc in range(FTC):
                    for fg in range(4):
                        db = (tc * 4 + fg) % 2
                        vdp = None
                        for i8 in range(8):
                            vdp = P.op("pe", lambda e, i8=i8, tc=tc, fg=fg, db=db:
                                       e.matmul(
                                           d_ps[db][:],
                                           hmid[:, i8 * FP + tc * 128: i8 * FP + (tc + 1) * 128],
                                           dw_bf[:, i8 * D + fg * 512: i8 * D + (fg + 1) * 512],
                                           start=(i8 == 0), stop=(i8 == 7)),
                                       w=later(dpsf[db], hm_max, dw_max, dw_max_b)
                                       if i8 == 0 else (), inc="sPE")
                        vdw_pe = vdp
                        ai = tc * 4 + fg
                        if o == 0:
                            vac = P.op("dv", lambda e, tc=tc, fg=fg, db=db: e.tensor_scalar(
                                acc[:, tc * D + fg * 512: tc * D + (fg + 1) * 512],
                                d_ps[db][:], 1.0, None, OP.mult),
                                w=later(vdp, accf[ai]), inc="sDV")
                        else:
                            vac = P.op("dv", lambda e, tc=tc, fg=fg, db=db:
                                       e.scalar_tensor_tensor(
                                           acc[:, tc * D + fg * 512: tc * D + (fg + 1) * 512],
                                           d_ps[db][:], 1.0,
                                           acc[:, tc * D + fg * 512: tc * D + (fg + 1) * 512],
                                           OP.mult, OP.add),
                                       w=later(vdp, accf[ai]), inc="sDV")
                        dpsf[db] = vac
                        accf[ai] = vac
                dw_used = ("sPE", vdw_pe[1])
                hm_used = dw_used
            for tc in range(FTC):
                tb = tc % 2
                vht = None
                vtp_last = None
                for half in range(2):
                    vtp = None
                    for fc8 in range(8):
                        fc = half * 8 + fc8
                        vtp = P.op("pe", lambda e, fc=fc, fc8=fc8, tb=tb, tc=tc, tp0=tp0:
                                   e.transpose(
                                       bt2_ps[tb][:, fc8 * 128:(fc8 + 1) * 128],
                                       h_fm[:, fc * To + tp0 + tc * 128: fc * To + tp0 + (tc + 1) * 128],
                                       iden_sb[:]),
                                   w=later(bt2f[tb] if fc8 == 0 else None,
                                           htm2f if (half == 0 and fc8 == 0) else None),
                                   inc="sPE")
                    vtp_last = vtp
                    evs = []
                    for fc8 in range(8):
                        fc = half * 8 + fc8
                        evs.append(P.op("dv", lambda e, fc=fc, fc8=fc8, tb=tb:
                                        e.tensor_copy(
                                            htm2[:, fc * 128:(fc + 1) * 128],
                                            bt2_ps[tb][:, fc8 * 128:(fc8 + 1) * 128]),
                                        w=[vtp] if fc8 == 0 else (), inc="sDV"))
                    bt2f[tb] = evs[-1]
                    vht = evs[-1]
                ai0 = tc * 4
                vr2a = P.op("dv", lambda e, tc=tc: e.tensor_tensor(
                    r2_t[:], acc[:, tc * D:(tc + 1) * D], dsc_sb[:], OP.mult),
                    w=later(accf[ai0], accf[ai0 + 1], accf[ai0 + 2], accf[ai0 + 3],
                            r2f, v_dsc), inc="sDV")
                vr2 = P.op("gp", lambda e: e.tensor_tensor(
                    r2_t[:], r2_t[:], htm2[:], OP.add), w=later(vr2a, vht), inc="sGP")
                htm2f = vr2
                ob = tc % 2
                vout, r2_readers = ln2.norm(r2_t[:], outst[ob][:], [vr2],
                                            [outf[ob]], D, sq_ap=outst[ob][:])
                r2f = r2_readers + [vr2]
                outf[ob] = P.dma("sc", out_d[tp0 + tc * 128: tp0 + (tc + 1) * 128, :],
                                 outst[ob][:], w=[vout], ch=f"c{6 + ob}")
                for i in range(4):
                    accf[ai0 + i] = vr2a
        es_pe_.close()
        es_p5.close()
        P.op("sc", lambda e: e.nop(),
             w=[("sDs", P.cnt["sDs"]), ("c6", P.cnt["c6"]), ("c7", P.cnt["c7"])])

        block = es0.enter_context(nc.Block())
        P.emit(block, sems)
    return nc


def _u8_low_byte(w_int32):
    a = np.ascontiguousarray(w_int32)
    assert a.dtype == np.int32
    return a.view(np.uint8).reshape(a.shape + (4,))[..., 0]


def prep_core(c, T, I, x, Wq, Wk, Wv, Wproj, q_gain, q_A, q_B, k_A, k_B,
              v_A, v_B, p_A, p_B, gate_w, gate_scale, up_w, up_scale,
              down_w, down_scale):
    b, s = c // 2, c % 2
    To = T // 2
    NIC = I // 128
    f0 = 1024 * s
    h0 = 8 * s
    kv0 = 256 * s
    bf = ml_dtypes.bfloat16
    tile16 = lambda a: np.ascontiguousarray(a).reshape(16, 128, a.shape[1])
    m = {}
    m["xT"] = np.ascontiguousarray(x[b].T)
    m["x_own"] = np.ascontiguousarray(x[b, s * To:(s + 1) * To, :])
    m["wq"] = tile16(Wq[:, f0:f0 + 1024])
    m["wk"] = tile16(Wk[:, kv0:kv0 + 256])
    m["wv"] = tile16(Wv[:, kv0:kv0 + 256])
    m["wp"] = np.ascontiguousarray(Wproj[f0:f0 + 1024, :]).reshape(8, 128, D)
    m["qa"] = tile16(q_A)
    m["ka"] = tile16(k_A)
    m["va"] = tile16(v_A)
    m["pa"] = np.ascontiguousarray(p_A[f0:f0 + 1024, :]).reshape(8, 128, RANK)
    m["qb"] = np.ascontiguousarray(q_B[:, f0:f0 + 1024])
    m["kb"] = np.ascontiguousarray(k_B[:, kv0:kv0 + 256])
    m["vb"] = np.ascontiguousarray(v_B[:, kv0:kv0 + 256])
    m["pb"] = np.ascontiguousarray(p_B)
    m["qg"] = np.ascontiguousarray(q_gain[h0:h0 + 8]).reshape(1, 8)
    inv = 10000.0 ** (-(np.arange(32, dtype=np.float64)) / 32.0)
    ang = np.arange(T, dtype=np.float64)[:, None] * inv[None, :]
    cosT, sinT = np.cos(ang).T, np.sin(ang).T
    m["rope"] = np.concatenate([cosT, sinT], 0).astype(np.float32).astype(bf)
    m["ropeb"] = np.concatenate([sinT, cosT], 0).astype(np.float32).astype(bf)
    mk = np.zeros((128, 128), np.float32)
    mk[np.triu_indices(128, 1)] = -30000.0
    m["maskt"] = mk
    m["iden"] = np.eye(128, dtype=np.float32).astype(bf)
    gT = np.ascontiguousarray(_u8_low_byte(gate_w).T)
    uT = np.ascontiguousarray(_u8_low_byte(up_w).T)
    dT = np.ascontiguousarray(_u8_low_byte(down_w).T)
    m["gt8"] = np.ascontiguousarray(
        gT.reshape(16, 128, NIC, 128).transpose(2, 1, 0, 3).reshape(NIC, 128, D))
    m["ut8"] = np.ascontiguousarray(
        uT.reshape(16, 128, NIC, 128).transpose(2, 1, 0, 3).reshape(NIC, 128, D))
    m["dt8"] = np.ascontiguousarray(dT.reshape(NIC, 128, D))
    m["gsc"] = np.ascontiguousarray(gate_scale.reshape(NIC, 128).T)
    m["usc"] = np.ascontiguousarray(up_scale.reshape(NIC, 128).T)
    m["dscT"] = np.ascontiguousarray(np.broadcast_to(down_scale[None, :], (128, D)))
    ep = np.empty((128, 2), np.float32)
    ep[:, 0] = RMS_EPS
    ep[:, 1] = LN_EPS
    m["epst"] = ep
    return m


def run(x, ws, T, I, core_ids=None):
    in_maps = [prep_core(c, T, I, x, *ws) for c in range(8)]
    nc = build(T, I)
    res = run_bass_kernel_spmd(nc, in_maps, core_ids=list(range(8)))
    out = np.empty((4, T, D), np.float32)
    To = T // 2
    for c in range(8):
        b, s = c // 2, c % 2
        out[b, s * To:(s + 1) * To, :] = res.results[c]["out"]
    return out


def kernel(L0_attn_x, L0_attn_Wq, L0_attn_Wk, L0_attn_Wv, L0_attn_Wproj,
           L0_attn_q_gain, L0_attn_q_A, L0_attn_q_B, L0_attn_k_A, L0_attn_k_B,
           L0_attn_v_A, L0_attn_v_B, L0_attn_p_A, L0_attn_p_B,
           L0_ffn_gate_w, L0_ffn_gate_scale, L0_ffn_up_w, L0_ffn_up_scale,
           L0_ffn_down_w, L0_ffn_down_scale):
    x = np.asarray(L0_attn_x, dtype=np.float32)
    ws = (np.asarray(L0_attn_Wq, np.float32), np.asarray(L0_attn_Wk, np.float32),
          np.asarray(L0_attn_Wv, np.float32), np.asarray(L0_attn_Wproj, np.float32),
          np.asarray(L0_attn_q_gain, np.float32), np.asarray(L0_attn_q_A, np.float32),
          np.asarray(L0_attn_q_B, np.float32), np.asarray(L0_attn_k_A, np.float32),
          np.asarray(L0_attn_k_B, np.float32), np.asarray(L0_attn_v_A, np.float32),
          np.asarray(L0_attn_v_B, np.float32), np.asarray(L0_attn_p_A, np.float32),
          np.asarray(L0_attn_p_B, np.float32),
          np.asarray(L0_ffn_gate_w, np.int32), np.asarray(L0_ffn_gate_scale, np.float32),
          np.asarray(L0_ffn_up_w, np.int32), np.asarray(L0_ffn_up_scale, np.float32),
          np.asarray(L0_ffn_down_w, np.int32), np.asarray(L0_ffn_down_scale, np.float32))
    return run(x, ws, 4096, 8192)


# version 30
# speedup vs baseline: 1.0766x; 1.0766x over previous
import numpy as np
import ml_dtypes
import concourse.bass as bass
import concourse.mybir as mybir
from concourse.bass_utils import run_bass_kernel_spmd
from concourse.alu_op_type import AluOpType
from contextlib import ExitStack

F32 = mybir.dt.float32
BF16 = mybir.dt.bfloat16
U8 = mybir.dt.uint8
AF = mybir.ActivationFunctionType
AX = mybir.AxisListType.X
OP = AluOpType

D = 2048
HD = 128
HLOC = 8
KVLOC = 2
RANK = 16
RMS_EPS = 1.1920929e-07
LN_EPS = 1e-5
INV_SQRT_HD = float(HD) ** -0.5

SEMS = (["sDq", "sDs", "sDg", "sPE", "sSC", "sDV", "sGP", "sCC", "sPH"]
        + [f"c{i}" for i in range(12)])
ENGS = ["sq", "sc", "dv", "gp", "pe"]


def later(*evs):
    out = []
    for e in evs:
        if e is None:
            continue
        if isinstance(e, list):
            out.extend(x for x in e if x is not None)
        else:
            out.append(e)
    return out


class Planner:
    def __init__(self):
        self.recs = {e: [] for e in ENGS}
        self.cnt = {s: 0 for s in SEMS}
        self.gate_no = 0
        self.last_gate = {e: 0 for e in ENGS}

    def op(self, eng, fn, w=(), inc=None, n=1):
        waits = [x for x in w if x is not None and x[1] > 0]
        if self.last_gate[eng] < self.gate_no:
            waits.insert(0, ("sPH", self.gate_no))
            self.last_gate[eng] = self.gate_no
        self.recs[eng].append((fn, waits, inc, n))
        if inc is not None:
            self.cnt[inc] += n
            return (inc, self.cnt[inc])
        return None

    def dma(self, q, dst, src, w=(), ch=None):
        sem = ch or {"sq": "sDq", "sc": "sDs", "gp": "sDg", "dv": "sDg"}[q]
        return self.op(q, lambda e, d=dst, s=src: e.dma_start(out=d, in_=s),
                       w=w, inc=sem, n=16)

    def gate(self):
        waits = [(s, self.cnt[s]) for s in SEMS if s != "sPH" and self.cnt[s] > 0]
        self.recs["gp"].append((lambda e: e.nop(), waits, "sPH", 1))
        self.cnt["sPH"] += 1
        self.gate_no = self.cnt["sPH"]
        self.last_gate["gp"] = self.gate_no

    def emit(self, block, sems):
        eng_of = {"sq": block.sync, "sc": block.scalar, "dv": block.vector,
                  "gp": block.gpsimd, "pe": block.tensor}
        for name in ENGS:
            recs = self.recs[name]

            def make(recs=recs):
                def body(e):
                    for fn, waits, inc, n in recs:
                        for s, v in waits:
                            e.wait_ge(sems[s], v)
                        inst = fn(e)
                        if inc is not None:
                            inst.then_inc(sems[inc], n)
                return body
            eng_of[name](make())


class LN:

    def __init__(self, P, nc, es, tag, eps_ap, alloc_sq=True):
        self.P = P
        self.eps = eps_ap
        self.stm = es.enter_context(nc.sbuf_tensor(f"stm_{tag}", [128, 4], F32))
        self.sq = (es.enter_context(nc.sbuf_tensor(f"sq_{tag}", [128, D], F32))
                   if alloc_sq else None)
        self.free = None

    def norm(self, src_ap, out_ap, w_src, w_out, ND, sq_ap=None):
        P, stm = self.P, self.stm
        sq = sq_ap if sq_ap is not None else self.sq[:]
        v1 = P.op("dv", lambda e: e.reduce_sum(stm[:, 0:1], src_ap, axis=AX),
                  w=later(*w_src, self.free), inc="sDV")
        v2 = P.op("sc", lambda e: e.square(sq, src_ap),
                  w=later(*w_src, self.free, *w_out), inc="sSC")
        v3 = P.op("dv", lambda e: e.reduce_sum(stm[:, 1:2], sq, axis=AX),
                  w=[v2], inc="sDV")
        P.op("dv", lambda e: e.drain())
        P.op("dv", lambda e: e.tensor_scalar(stm[:, 2:3], stm[:, 0:1],
                                             1.0 / ND, None, OP.mult))
        P.op("dv", lambda e: e.tensor_scalar(stm[:, 1:2], stm[:, 1:2],
                                             1.0 / ND, None, OP.mult))
        P.op("dv", lambda e: e.drain())
        P.op("dv", lambda e: e.tensor_tensor(stm[:, 0:1], stm[:, 2:3],
                                             stm[:, 2:3], OP.mult))
        P.op("dv", lambda e: e.drain())
        v5 = P.op("dv", lambda e: e.tensor_tensor(stm[:, 1:2], stm[:, 1:2],
                                                  stm[:, 0:1], OP.subtract), inc="sDV")
        v6 = P.op("sc", lambda e: e.activation(stm[:, 0:1], stm[:, 1:2],
                                               AF.Ln, bias=self.eps), w=[v5], inc="sSC")
        P.op("sc", lambda e: e.drain())
        v7 = P.op("sc", lambda e: e.activation(stm[:, 3:4], stm[:, 0:1],
                                               AF.Exp, scale=-0.5), inc="sSC")
        v8 = P.op("dv", lambda e, o=out_ap: e.tensor_scalar(
            o, src_ap, stm[:, 2:3], stm[:, 3:4], OP.subtract, OP.mult),
            w=later(v7, *w_out), inc="sDV")
        self.free = [v8, v7]
        return v8, [v8, v2]


def build(T, I):
    To = T // 2
    NQC = T // 128
    NPASS = T // 512
    NIC = I // 128
    NOCT = NIC // 8
    FP = 512
    NFP = To // FP
    FTC = FP // 128
    NTC = To // 128

    nc = bass.Bass("TRN2", target_bir_lowering=False, debug=False)
    din = lambda nm, sh, d=F32: nc.dram_tensor(nm, sh, d, kind="ExternalInput").ap()
    xT = din("xT", [D, T])
    x_own = din("x_own", [To, D])
    wq = din("wq", [16, 128, HLOC * HD])
    wk = din("wk", [16, 128, KVLOC * HD])
    wv = din("wv", [16, 128, KVLOC * HD])
    wp = din("wp", [HLOC, 128, D])
    qa = din("qa", [16, 128, RANK])
    ka = din("ka", [16, 128, RANK])
    va = din("va", [16, 128, RANK])
    pa = din("pa", [HLOC, 128, RANK])
    qb = din("qb", [RANK, HLOC * HD])
    kb = din("kb", [RANK, KVLOC * HD])
    vb = din("vb", [RANK, KVLOC * HD])
    pb = din("pb", [RANK, D])
    qg = din("qg", [1, HLOC])
    rope = din("rope", [64, T], BF16)
    ropeb = din("ropeb", [64, T], BF16)
    maskt = din("maskt", [128, 128])
    iden = din("iden", [128, 128], BF16)
    gt8 = din("gt8", [NIC, 128, D], U8)
    ut8 = din("ut8", [NIC, 128, D], U8)
    dt8 = din("dt8", [NIC, 128, D], U8)
    gsc = din("gsc", [128, NIC])
    usc = din("usc", [128, NIC])
    dscT = din("dscT", [128, D])
    epst = din("epst", [128, 2])
    out_d = nc.dram_tensor("out", [To, D], F32, kind="ExternalOutput").ap()
    mq_dram = nc.dram_tensor("mq_dram", [HLOC * NQC, 128], BF16).ap()
    cc_in = nc.dram_tensor("cc_in", [T, D], BF16)
    gbf_d = nc.dram_tensor("gbf_d", [NIC, 128, D], BF16).ap()
    ubf_d = nc.dram_tensor("ubf_d", [NIC, 128, D], BF16).ap()
    dbf_d = nc.dram_tensor("dbf_d", [NIC, 128, D], BF16).ap()
    cc_out = nc.dram_tensor("cc_out", [To, D], BF16)

    P = Planner()
    es0 = ExitStack()
    with es0:
        sems = {s: es0.enter_context(nc.semaphore(s)) for s in SEMS}
        SB = lambda nm, sh, d=F32: nc.sbuf_tensor(nm, sh, d)
        PS = lambda nm, sh, d=F32: nc.psum_tensor(nm, sh, d)

        eps_sb = es0.enter_context(SB("eps_sb", [128, 2]))
        iden_sb = es0.enter_context(SB("iden_sb", [128, 128], BF16))
        mask_sb = es0.enter_context(SB("mask_sb", [128, 128]))
        qg_sb = es0.enter_context(SB("qg_sb", [1, HLOC]))
        gsc_sb = es0.enter_context(SB("gsc_sb", [128, NIC]))
        usc_sb = es0.enter_context(SB("usc_sb", [128, NIC]))
        ones_sb = es0.enter_context(SB("ones_sb", [128, 1], BF16))
        nones_sb = es0.enter_context(SB("nones_sb", [128, 1], BF16))
        onesk1 = es0.enter_context(SB("onesk1", [1, 128], BF16))
        pa_bf = es0.enter_context(SB("pa_bf", [128, HLOC * RANK], BF16))
        pb_bf = es0.enter_context(SB("pb_bf", [RANK, D], BF16))

        es_qy = ExitStack()
        q_sb = es_qy.enter_context(SB("q_sb", [128, HLOC * T], BF16))
        y_sb = q_sb
        es_kv = ExitStack()
        k_sb = es_kv.enter_context(SB("k_sb", [128, KVLOC * T], BF16))
        v_sb = es_kv.enter_context(SB("v_sb", [128, (T // 128) * KVLOC * HD], BF16))

        es_p1 = ExitStack()
        ropeA_t = es_p1.enter_context(SB("ropeA_t", [64, 512], BF16))
        ropeB_t = es_p1.enter_context(SB("ropeB_t", [64, 512], BF16))
        qa_bf = es_p1.enter_context(SB("qa_bf", [128, 16 * RANK], BF16))
        ka_bf = es_p1.enter_context(SB("ka_bf", [128, 16 * RANK], BF16))
        va_bf = es_p1.enter_context(SB("va_bf", [128, 16 * RANK], BF16))
        qb_bf = es_p1.enter_context(SB("qb_bf", [RANK, HLOC * HD], BF16))
        kb_bf = es_p1.enter_context(SB("kb_bf", [RANK, KVLOC * HD], BF16))
        vb_bf = es_p1.enter_context(SB("vb_bf", [RANK, KVLOC * HD], BF16))
        xstage = [es_p1.enter_context(SB(f"xstage{i}", [128, 1024])) for i in range(2)]
        wq_bf = es_p1.enter_context(SB("wq_bf", [128, 16 * HLOC * HD], BF16))
        wk_bf = es_p1.enter_context(SB("wk_bf", [128, 16 * KVLOC * HD], BF16))
        wv_bf = es_p1.enter_context(SB("wv_bf", [128, 16 * KVLOC * HD], BF16))

        for dpair in [(eps_sb[:], epst), (mask_sb[:], maskt), (qg_sb[:], qg),
                      (gsc_sb[:], gsc), (usc_sb[:], usc),
                      (iden_sb[:], iden)]:
            P.dma("sq", dpair[0], dpair[1])
        P.op("dv", lambda e: e.memset(ones_sb[:], 1.0))
        P.op("dv", lambda e: e.memset(nones_sb[:], -1.0))
        P.op("dv", lambda e: e.memset(onesk1[:], 1.0), inc="sDV")

        jobs = []
        for fc in range(16):
            jobs.append((wq[fc], 128, HLOC * HD, wq_bf, fc * HLOC * HD))
        for i in range(4):
            jobs.append((wk.rearrange("c p n -> p c n")[:, i * 4:(i + 1) * 4, :],
                         128, 4 * KVLOC * HD, wk_bf, i * 4 * KVLOC * HD))
            jobs.append((wv.rearrange("c p n -> p c n")[:, i * 4:(i + 1) * 4, :],
                         128, 4 * KVLOC * HD, wv_bf, i * 4 * KVLOC * HD))
        jobs += [(qa.rearrange("c p n -> p c n"), 128, 16 * RANK, qa_bf, 0),
                 (ka.rearrange("c p n -> p c n"), 128, 16 * RANK, ka_bf, 0),
                 (va.rearrange("c p n -> p c n"), 128, 16 * RANK, va_bf, 0),
                 (pa.rearrange("c p n -> p c n"), 128, HLOC * RANK, pa_bf, 0),
                 (qb, RANK, HLOC * HD, qb_bf, 0),
                 (kb, RANK, KVLOC * HD, kb_bf, 0),
                 (vb, RANK, KVLOC * HD, vb_bf, 0),
                 (pb[:, 0:1024], RANK, 1024, pb_bf, 0),
                 (pb[:, 1024:2048], RANK, 1024, pb_bf, 1024)]
        xst_ev = [None, None]
        for j, (src, rows, cols, dst, off) in enumerate(jobs):
            buf = j % 2
            st = xstage[buf]
            if len(src.shape) == 3:
                d_ap = st[:, 0:cols].rearrange("p (c n) -> p c n", c=src.shape[1])
            else:
                d_ap = st[0:rows, 0:cols]
            v = P.dma("sq", d_ap, src, w=later(xst_ev[buf]), ch=f"c{buf}")
            xst_ev[buf] = P.op(
                "dv", lambda e, a=st[0:rows, 0:cols], b=dst[0:rows, off:off + cols]:
                e.tensor_copy(b, a), w=[v], inc="sDV")
        P.gate()

        x_fm = es_p1.enter_context(SB("x_fm", [128, 16 * 512], BF16))
        qsq = [es_p1.enter_context(SB(f"qsq{i}", [128, 512], BF16)) for i in range(2)]
        stat1 = es_p1.enter_context(SB("stat1", [1, 512]))
        stat2 = es_p1.enter_context(SB("stat2", [1, 512]))
        mqrow = es_p1.enter_context(SB("mqrow", [1, 512], BF16))
        mkbr = es_p1.enter_context(SB("mkbr", [1, 512], BF16))
        bcev = es_p1.enter_context(SB("bcev", [128, 512], BF16))
        qn_sb = [es_p1.enter_context(SB(f"qn{i}", [128, 512], BF16)) for i in range(2)]
        rs1 = es_p1.enter_context(SB("rs1", [64, 512], BF16))
        rs2 = es_p1.enter_context(SB("rs2", [64, 512], BF16))
        rsa = es_p1.enter_context(SB("rsa", [32, 512], BF16))
        rsb = es_p1.enter_context(SB("rsb", [32, 512], BF16))
        t1q = es_p1.enter_context(SB("t1q", [RANK, 512], BF16))
        t1k = es_p1.enter_context(SB("t1k", [RANK, 512], BF16))
        t1v = es_p1.enter_context(SB("t1v", [RANK, 512], BF16))
        es_pa = ExitStack()
        qk_ps = [es_pa.enter_context(PS(f"qk_ps{i}", [128, 512])) for i in range(3)]
        t1_ps = es_pa.enter_context(PS("t1_ps", [RANK, 512]))
        sq_ps = [es_pa.enter_context(PS("sq_ps0", [1, 512]))]
        bc_ps = es_pa.enter_context(PS("bc_ps", [128, 512]))
        v_ps = [es_pa.enter_context(PS(f"v_ps{i}", [128, 512])) for i in range(2)]

        xfm_rd = None
        qkfree = [None, None, None]
        t1free = None
        sqfree = [None]
        bcfree = None
        vpsfree = [None, None]
        qsqfree = [None, None]
        qnfree = [None, None]
        stat2_free = None
        mqrow_free = [None]

        def rms_stats(pi, seg, wpe):
            nonlocal stat2_free
            qb_ = pi % 2
            vsc = P.op("sc", lambda e, p=qk_ps[pi], q=qsq[qb_]: e.square(q[:], p[:]),
                       w=later(wpe, qsqfree[qb_]), inc="sSC")
            vpe = P.op("pe", lambda e, sp=sq_ps[seg], q=qsq[qb_]: e.matmul(
                sp[:], ones_sb[:], q[:], start=True, stop=True),
                w=later(vsc, sqfree[seg]), inc="sPE")
            qsqfree[qb_] = vpe
            v1 = P.op("sc", lambda e, sp=sq_ps[seg]: e.activation(
                stat1[:], sp[:], AF.Ln, bias=eps_sb[0:1, 0:1], scale=1.0 / HD),
                w=later(vpe), inc="sSC")
            sqfree[seg] = v1
            P.op("sc", lambda e: e.drain())
            v2 = P.op("sc", lambda e: e.activation(
                stat2[:], stat1[:], AF.Exp, scale=-0.5),
                w=later(stat2_free), inc="sSC")
            return v2

        rope_ev = [None, None]

        def rope_emit(qb_, dst, doff, pcol, wsrc):
            src = qn_sb[qb_]
            csn = ropeA_t[:, :]
            snc = ropeB_t[:, :]
            P.op("dv", lambda e: e.drain())
            P.op("dv", lambda e: e.tensor_tensor(rs1[:, :], src[0:64, :], csn, OP.mult),
                 w=later(wsrc, rope_ev[0]))
            P.op("dv", lambda e: e.tensor_tensor(rs2[:, :], src[0:64, :], snc, OP.mult))
            P.op("dv", lambda e: e.drain())
            P.op("dv", lambda e: e.tensor_copy(rsa[:, :], rs1[32:64, :]))
            P.op("dv", lambda e: e.tensor_copy(rsb[:, :], rs2[32:64, :]))
            P.op("dv", lambda e: e.drain())
            P.op("dv", lambda e: e.tensor_tensor(
                dst[0:32, doff:doff + 512], rs1[0:32, :], rsa[:, :], OP.add))
            vd = P.op("dv", lambda e: e.tensor_tensor(
                dst[32:64, doff:doff + 512], rsb[:, :], rs2[0:32, :], OP.subtract),
                inc="sDV")
            rope_ev[1] = vd
            vg = P.op("gp", lambda e: e.tensor_copy(
                dst[64:128, doff:doff + 512], src[64:128, :]), w=later(wsrc), inc="sGP")
            return [vd, vg]

        for p in range(NPASS):
            t0 = p * 512
            P.dma("sq", ropeA_t[:, :], rope[:, t0:t0 + 512],
                  w=later(rope_ev[1]), ch="c3")
            rope_ev[0] = P.dma("sq", ropeB_t[:, :], ropeb[:, t0:t0 + 512], ch="c3")
            for fc in range(16):
                buf = fc % 2
                st = xstage[buf]
                v = P.dma("sq", st[:, 0:512], xT[fc * 128:(fc + 1) * 128, t0:t0 + 512],
                          w=later(xst_ev[buf], xfm_rd if fc < 2 else None),
                          ch=f"c{buf}")
                xst_ev[buf] = P.op(
                    "dv", lambda e, a=st[:, 0:512], b=x_fm[:, fc * 512:(fc + 1) * 512]:
                    e.tensor_copy(b, a), w=[v], inc="sDV")
            v_xf = ("sDV", P.cnt["sDV"])
            for asb, tsb in [(qa_bf, t1q), (ka_bf, t1k), (va_bf, t1v)]:
                for fc in range(16):
                    vpe = P.op("pe", lambda e, fc=fc, a=asb: e.matmul(
                        t1_ps[:], a[:, fc * RANK:(fc + 1) * RANK],
                        x_fm[:, fc * 512:(fc + 1) * 512],
                        start=(fc == 0), stop=(fc == 15)),
                        w=later(v_xf, t1free) if fc == 0 else (), inc="sPE")
                t1free = P.op("sc", lambda e, t=tsb: e.copy(t[:], t1_ps[:]),
                              w=[vpe], inc="sSC")
            v_t1 = t1free
            for h in range(HLOC):
                pi = h % 3
                for fc in range(16):
                    P.op("pe", lambda e, fc=fc, h=h, pp=qk_ps[pi]: e.matmul(
                        pp[:], wq_bf[:, fc * HLOC * HD + h * HD: fc * HLOC * HD + (h + 1) * HD],
                        x_fm[:, fc * 512:(fc + 1) * 512],
                        start=(fc == 0), stop=False),
                        w=later(qkfree[pi], v_t1) if fc == 0 else (), inc="sPE")
                vpe = P.op("pe", lambda e, h=h, pp=qk_ps[pi]: e.matmul(
                    pp[:], qb_bf[:, h * HD:(h + 1) * HD], t1q[:],
                    start=False, stop=True), inc="sPE")
                v2 = rms_stats(pi, 0, vpe)
                vmq = P.op("dv", lambda e, h=h: e.tensor_scalar(
                    mqrow[:], stat2[:], qg_sb[0:1, h:h + 1], INV_SQRT_HD,
                    OP.mult, OP.mult), w=later(v2, mqrow_free[0]), inc="sDV")
                stat2_free = vmq
                mqrow_free[0] = P.dma("sc",
                      mq_dram[h * NQC + p * 4: h * NQC + (p + 1) * 4, :]
                      .rearrange("(o c) n -> o c n", o=1),
                      mqrow[0:1, :].rearrange("o (c n) -> o c n", c=4), w=[vmq],
                      ch="c2")
                qb_ = h % 2
                vq = P.op("sc", lambda e, pp=qk_ps[pi], q=qn_sb[qb_]: e.copy(q[:], pp[:]),
                          w=later(vpe, qnfree[qb_]), inc="sSC")
                qkfree[pi] = vq
                qnfree[qb_] = rope_emit(qb_, q_sb, h * T + t0, t0, vq)
            for kv in range(KVLOC):
                pi = (HLOC + kv) % 3
                for fc in range(16):
                    P.op("pe", lambda e, fc=fc, kv=kv, pp=qk_ps[pi]: e.matmul(
                        pp[:], wk_bf[:, fc * KVLOC * HD + kv * HD: fc * KVLOC * HD + (kv + 1) * HD],
                        x_fm[:, fc * 512:(fc + 1) * 512],
                        start=(fc == 0), stop=False),
                        w=later(qkfree[pi]) if fc == 0 else (), inc="sPE")
                vpe = P.op("pe", lambda e, kv=kv, pp=qk_ps[pi]: e.matmul(
                    pp[:], kb_bf[:, kv * HD:(kv + 1) * HD], t1k[:],
                    start=False, stop=True), inc="sPE")
                v2 = rms_stats(pi, 0, vpe)
                vmk = P.op("dv", lambda e: e.tensor_copy(mkbr[:], stat2[:]),
                           w=[v2], inc="sDV")
                stat2_free = vmk
                vbc = P.op("pe", lambda e: e.matmul(
                    bc_ps[:], onesk1[:], mkbr[:], start=True, stop=True),
                    w=later(vmk, bcfree), inc="sPE")
                vbs = P.op("sc", lambda e: e.copy(bcev[:], bc_ps[:]), w=[vbc], inc="sSC")
                bcfree = vbs
                qb_ = kv % 2
                vkn = P.op("dv", lambda e, pp=qk_ps[pi], q=qn_sb[qb_]: e.tensor_tensor(
                    q[:], pp[:], bcev[:], OP.mult),
                    w=later(vbs, qnfree[qb_]), inc="sDV")
                qkfree[pi] = vkn
                qnfree[qb_] = rope_emit(qb_, k_sb, kv * T + t0, t0, vkn)
            for tc in range(4):
                buf = tc % 2
                gchunk = p * 4 + tc
                for fc in range(16):
                    P.op("pe", lambda e, fc=fc, tc=tc, pp=v_ps[buf]: e.matmul(
                        pp[:, 0:KVLOC * HD],
                        x_fm[:, fc * 512 + tc * 128: fc * 512 + (tc + 1) * 128],
                        wv_bf[:, fc * KVLOC * HD:(fc + 1) * KVLOC * HD],
                        start=(fc == 0), stop=False),
                        w=later(vpsfree[buf]) if fc == 0 else (), inc="sPE")
                vpe = P.op("pe", lambda e, tc=tc, pp=v_ps[buf]: e.matmul(
                    pp[:, 0:KVLOC * HD], t1v[:, tc * 128:(tc + 1) * 128], vb_bf[:],
                    start=False, stop=True), inc="sPE")
                vpsfree[buf] = P.op(
                    "sc", lambda e, g=gchunk, pp=v_ps[buf]: e.copy(
                        v_sb[:, g * KVLOC * HD:(g + 1) * KVLOC * HD],
                        pp[:, 0:KVLOC * HD]), w=[vpe], inc="sSC")
            xfm_rd = ("sPE", P.cnt["sPE"])
        es_pa.close()
        es_p1.close()
        P.gate()

        es_p2 = ExitStack()
        w8st = [es_p2.enter_context(SB(f"w8st{i}", [128, D], U8)) for i in range(2)]
        wbfst = [es_p2.enter_context(SB(f"wbfst{i}", [128, D], BF16)) for i in range(2)]
        mqt_bf = es_p2.enter_context(SB("mqt_bf", [128, HLOC * NQC], BF16))
        mqt = es_p2.enter_context(SB("mqt", [128, HLOC * NQC]))
        s_sb = [es_p2.enter_context(SB(f"s_sb{i}", [128, T], BF16)) for i in range(3)]
        p_sb = [es_p2.enter_context(SB(f"p_sb{i}", [128, T], BF16)) for i in range(3)]
        pt_sb = [es_p2.enter_context(SB(f"pt_sb{i}", [128, 1024], BF16)) for i in range(2)]
        mxt = [es_p2.enter_context(SB(f"mxt{i}", [128, 8])) for i in range(3)]
        ngm = [es_p2.enter_context(SB(f"ngm{i}", [128, 1])) for i in range(3)]
        rsum = [es_p2.enter_context(SB(f"rsum{i}", [128, 1])) for i in range(3)]
        rinv = [es_p2.enter_context(SB(f"rinv{i}", [128, 1])) for i in range(3)]
        vsq_sb = es_p2.enter_context(SB("vsq_sb", [128, KVLOC * HD], BF16))
        sv_sb = es_p2.enter_context(SB("sv_sb", [128, 2 * KVLOC]))
        rv_sb = es_p2.enter_context(SB("rv_sb", [128, KVLOC]))
        vh_tok = es_p2.enter_context(SB("vh_tok", [128, KVLOC * HD], BF16))
        vh_fm = es_p2.enter_context(SB("vh_fm", [128, KVLOC * HD], BF16))
        ysb = [es_p2.enter_context(SB(f"ysb{i}", [128, 128], BF16)) for i in range(4)]
        zsb = es_p2.enter_context(SB("zsb", [128, 128], BF16))
        ndsb = es_p2.enter_context(SB("ndsb", [1, 128], BF16))
        prsb = es_p2.enter_context(SB("prsb", [128, 128], BF16))
        es_pb = ExitStack()
        s_ps = [es_pb.enter_context(PS(f"s_ps{i}", [128, 512])) for i in range(2)]
        pt_ps = [es_pb.enter_context(PS(f"pt_ps{i}", [128, 1024], BF16)) for i in range(2)]
        y_ps = [es_pb.enter_context(PS(f"y_ps{i}", [128, 512])) for i in range(2)]
        miscA = es_pb.enter_context(PS("miscA", [128, 512]))
        vtr_ps = es_pb.enter_context(PS("vtr_ps", [128, KVLOC * HD], BF16))

        vt = P.op("sq", lambda e: e.dma_start_transpose(mqt_bf[:], mq_dram[:, :]),
                  w=[("c2", P.cnt["c2"])], inc="c3", n=16)
        P.op("dv", lambda e: e.tensor_copy(mqt[:], mqt_bf[:]), w=[vt], inc="sDV")

        w8cv = [None, None]
        wbf_out = [None, None]
        sps_free = [None, None]
        ptps_free = [None, None]
        ptsb_free = [None, None]
        yps_free = [None, None]
        ssb_free = [None] * 3
        psb_free = [None] * 3
        mxt_free = [None] * 3
        ngm_free = [None] * 3
        rs_free = [None] * 3
        ri_free = [None] * 3
        st2 = {"vhat": None, "vhtok": None, "vtr": None, "vhfm": None,
               "z": None, "nd": None, "pr": None, "mA": None, "svf": None}
        ysb_free = [None] * 4
        ptrot = [0]

        def attn_head(qc, kv, g):
            h = kv * 4 + g
            ext = (qc + 1) * 128
            ng = (ext + 511) // 512
            sb = h % 3
            mb = h % 3
            col = h * NQC + qc
            scans = None
            for j in range(ng):
                w = min(512, ext - j * 512)
                pj = j % 2
                vpe = P.op("pe", lambda e, j=j, w=w, pj=pj, h=h, kv=kv, qc=qc: e.matmul(
                    s_ps[pj][:, 0:w],
                    q_sb[:, h * T + qc * 128: h * T + (qc + 1) * 128],
                    k_sb[:, kv * T + j * 512: kv * T + j * 512 + w],
                    start=True, stop=True),
                    w=later(sps_free[pj], ssb_free[sb] if j == 0 else None,
                            mxt_free[mb] if j == 0 else None), inc="sPE")
                scans = P.op("dv", lambda e, j=j, w=w, pj=pj, sb=sb, mb=mb, col=col:
                             e.tensor_scalar(
                                 s_sb[sb][:, j * 512: j * 512 + w], s_ps[pj][:, 0:w],
                                 mqt[:, col:col + 1], None, OP.mult, OP.max,
                                 accum_out=mxt[mb][:, j:j + 1]),
                             w=[vpe], inc="sDV")
                sps_free[pj] = scans
            vmask = P.op("dv", lambda e, sb=sb, qc=qc: e.tensor_tensor(
                s_sb[sb][:, qc * 128:(qc + 1) * 128],
                s_sb[sb][:, qc * 128:(qc + 1) * 128], mask_sb[:], OP.add),
                w=later(scans), inc="sDV")
            vngm = P.op("dv", lambda e, mb=mb, ng=ng: e.reduce_max(
                ngm[mb][:, 0:1], mxt[mb][:, 0:ng], axis=AX, negate=True),
                w=later(scans, ngm_free[mb]), inc="sDV")
            mxt_free[mb] = vngm
            vexp = P.op("sc", lambda e, sb=sb, mb=mb, ext=ext: e.activation(
                p_sb[sb][:, 0:ext], s_sb[sb][:, 0:ext], AF.Exp,
                bias=ngm[mb][:, 0:1], accum_out=rsum[mb][:, 0:1]),
                w=later(vmask, vngm, psb_free[sb], rs_free[mb]), inc="sSC")
            ssb_free[sb] = vexp
            ngm_free[mb] = vexp
            return dict(qc=qc, kv=kv, g=g, h=h, ext=ext, sb=sb, mb=mb, vexp=vexp)

        def attn_ptpv(ctx):
            qc, kv, g, sb, mb = (ctx["qc"], ctx["kv"], ctx["g"], ctx["sb"],
                                 ctx["mb"])
            ext = ctx["ext"]
            vri = P.op("dv", lambda e, mb=mb: e.reciprocal(
                rinv[mb][:, 0:1], rsum[mb][:, 0:1]),
                w=later(ctx["vexp"], ri_free[mb]), inc="sDV")
            rs_free[mb] = vri
            vpn = P.op("dv", lambda e, sb=sb, mb=mb, ext=ext: e.tensor_scalar(
                p_sb[sb][:, 0:ext], p_sb[sb][:, 0:ext], rinv[mb][:, 0:1],
                None, OP.mult), w=[vri], inc="sDV")
            ri_free[mb] = vpn
            ctx["vpn"] = vpn
            yb = kv % 2
            nch = qc + 1
            groups = []
            for c0 in range(0, nch, 8):
                groups.append((c0, min(8, nch - c0)))

            def emit_T(c0, cw):
                tb = ptrot[0] % 2
                ptrot[0] += 1
                vtp = None
                for ci in range(cw):
                    c = c0 + ci
                    vtp = P.op("pe", lambda e, c=c, ci=ci, tb=tb, sb=sb: e.transpose(
                        pt_ps[tb][:, ci * 128:(ci + 1) * 128],
                        p_sb[sb][:, c * 128:(c + 1) * 128], iden_sb[:]),
                        w=later(ctx["vpn"] if ci == 0 else None,
                                ptps_free[tb] if ci == 0 else None), inc="sPE")
                vev = P.op("sc", lambda e, tb=tb, cw=cw: e.copy(
                    pt_sb[tb][:, 0:cw * 128], pt_ps[tb][:, 0:cw * 128]),
                    w=later(vtp, ptsb_free[tb]), inc="sSC")
                ptps_free[tb] = vev
                return tb, vev

            def emit_PV(c0, cw, tb, vev):
                vpv = None
                for ci in range(cw):
                    c = c0 + ci
                    vpv = P.op("pe", lambda e, c=c, ci=ci, tb=tb, kv=kv, g=g, yb=yb, nch=nch:
                               e.matmul(
                                   y_ps[yb][:, g * 128:(g + 1) * 128],
                                   v_sb[:, c * KVLOC * HD + kv * HD: c * KVLOC * HD + (kv + 1) * HD],
                                   pt_sb[tb][:, ci * 128:(ci + 1) * 128],
                                   start=(c == 0), stop=(c == nch - 1)),
                               w=later(vev if ci == 0 else None,
                                       yps_free[yb] if (c == 0 and g == 0) else None),
                               inc="sPE")
                ptsb_free[tb] = vpv
                return vpv

            prev = None
            vpv = None
            for c0, cw in groups:
                tb, vev = emit_T(c0, cw)
                if prev is not None:
                    vpv = emit_PV(*prev)
                prev = (c0, cw, tb, vev)
            vpv = emit_PV(*prev)
            psb_free[sb] = vpv
            ctx["vpv"] = vpv
            ctx["yb"] = yb
            return ctx

        def attn_tail(ctx):
            qc, kv, g, h, yb = ctx["qc"], ctx["kv"], ctx["g"], ctx["h"], ctx["yb"]
            vye = P.op("sc", lambda e, yb=yb, g=g: e.copy(
                ysb[g][:], y_ps[yb][:, g * 128:(g + 1) * 128]),
                w=later(ctx["vpv"], ysb_free[g]), inc="sSC")
            vz = P.op("dv", lambda e, kv=kv, g=g: e.tensor_tensor(
                zsb[:], ysb[g][:], vh_fm[:, kv * HD:(kv + 1) * HD], OP.mult),
                w=later(vye, st2["z"], ctx["vhfm"]), inc="sDV")
            vnd = P.op("pe", lambda e: e.matmul(
                miscA[0:1, 128:256], nones_sb[:], zsb[:], start=True, stop=True),
                w=later(vz, st2["mA"]), inc="sPE")
            st2["z"] = vnd
            vnde = P.op("dv", lambda e: e.tensor_copy(ndsb[:], miscA[0:1, 128:256]),
                        w=later(vnd, st2["nd"]), inc="sDV")
            vbc2 = P.op("pe", lambda e: e.matmul(
                miscA[:, 0:128], onesk1[:], ndsb[:], start=True, stop=True),
                w=[vnde], inc="sPE")
            st2["nd"] = vbc2
            vpr = P.op("dv", lambda e, kv=kv: e.tensor_tensor(
                prsb[:], vh_fm[:, kv * HD:(kv + 1) * HD], miscA[:, 0:128], OP.mult),
                w=later(vbc2, st2["pr"]), inc="sDV")
            st2["mA"] = vpr
            vyo = P.op("dv", lambda e, h=h, qc=qc, g=g: e.tensor_tensor(
                y_sb[:, h * T + qc * 128: h * T + (qc + 1) * 128],
                ysb[g][:], prsb[:], OP.add), w=[vpr], inc="sDV")
            st2["pr"] = vyo
            ysb_free[g] = vyo
            if g == 3:
                yps_free[yb] = vye
            return vyo

        icpq_num = NIC * 3
        for qc in range(NQC):
            j0 = (qc * icpq_num) // NQC
            j1 = ((qc + 1) * icpq_num) // NQC
            for j in range(j0, j1):
                ic, mat = divmod(j, 3)
                srcd, dstd = [(gt8, gbf_d), (ut8, ubf_d), (dt8, dbf_d)][mat]
                bi = j % 2
                vin = P.dma("sq", w8st[bi][:], srcd[ic], w=later(w8cv[bi]),
                            ch=f"c{10 + bi}")
                vcv = P.op("dv", lambda e, bi=bi: e.tensor_scalar(
                    wbfst[bi][:], w8st[bi][:], -128.0, None, OP.add),
                    w=later(vin, wbf_out[bi]), inc="sDV")
                w8cv[bi] = vcv
                wbf_out[bi] = P.dma("sc", dstd[ic], wbfst[bi][:], w=[vcv],
                                    ch=f"c{6 + bi}")
            vv = P.op("sc", lambda e, qc=qc: e.square(
                vsq_sb[:], v_sb[:, qc * KVLOC * HD:(qc + 1) * KVLOC * HD]),
                w=later(st2["vhat"]), inc="sSC")
            vred = None
            for kv in range(KVLOC):
                vred = P.op("dv", lambda e, kv=kv: e.reduce_sum(
                    sv_sb[:, kv:kv + 1], vsq_sb[:, kv * HD:(kv + 1) * HD], axis=AX),
                    w=later(vv, st2["svf"]) if kv == 0 else (), inc="sDV")
            st2["vhat"] = vred
            P.op("dv", lambda e: e.drain())
            vfl = P.op("dv", lambda e: e.tensor_scalar(
                sv_sb[:, KVLOC:2 * KVLOC], sv_sb[:, 0:KVLOC], 1e-24, None, OP.max),
                inc="sDV")
            vln = P.op("sc", lambda e: e.activation(
                sv_sb[:, 0:KVLOC], sv_sb[:, KVLOC:2 * KVLOC], AF.Ln),
                w=[vfl], inc="sSC")
            P.op("sc", lambda e: e.drain())
            vrv = P.op("sc", lambda e: e.activation(
                rv_sb[:], sv_sb[:, 0:KVLOC], AF.Exp, scale=-0.5), inc="sSC")
            st2["svf"] = vrv
            vhs = []
            for kv in range(KVLOC):
                vhs.append(P.op("dv", lambda e, kv=kv, qc=qc: e.tensor_scalar(
                    vh_tok[:, kv * HD:(kv + 1) * HD],
                    v_sb[:, qc * KVLOC * HD + kv * HD: qc * KVLOC * HD + (kv + 1) * HD],
                    rv_sb[:, kv:kv + 1], None, OP.mult),
                    w=later(vrv, st2["vhtok"]) if kv == 0 else (), inc="sDV"))
            vts = None
            for kv in range(KVLOC):
                vts = P.op("pe", lambda e, kv=kv: e.transpose(
                    vtr_ps[:, kv * HD:(kv + 1) * HD], vh_tok[:, kv * HD:(kv + 1) * HD],
                    iden_sb[:]),
                    w=later(vhs[kv], st2["vtr"] if kv == 0 else None), inc="sPE")
            st2["vhtok"] = vts
            vhf = P.op("dv", lambda e: e.tensor_copy(vh_fm[:], vtr_ps[:]),
                       w=later(vts, st2["vhfm_free"] if "vhfm_free" in st2 else None),
                       inc="sDV")
            st2["vtr"] = vhf
            ctxs = []
            for kv in range(KVLOC):
                for g in range(4):
                    c = attn_head(qc, kv, g)
                    c["vhfm"] = vhf
                    ctxs.append(c)
                    if len(ctxs) >= 3:
                        attn_ptpv(ctxs[-3])
            attn_ptpv(ctxs[-2])
            attn_ptpv(ctxs[-1])
            for c in ctxs:
                attn_tail(c)
            st2["vhfm_free"] = st2["pr"]
        es_pb.close()
        es_p2.close()
        es_kv.close()
        P.gate()

        es_p3 = ExitStack()
        wp_bf = es_p3.enter_context(SB("wp_bf", [128, HLOC * D], BF16))
        wpst = [es_p3.enter_context(SB(f"wpst{i}", [128, D])) for i in range(2)]
        lr_sb = es_p3.enter_context(SB("lr_sb", [RANK, T], BF16))
        cst = [es_p3.enter_context(SB(f"cst{i}", [128, 512], BF16)) for i in range(2)]
        es_pc = ExitStack()
        c_ps = [es_pc.enter_context(PS(f"c_ps{i}", [128, 512])) for i in range(2)]
        lr_ps = es_pc.enter_context(PS("lr_ps", [RANK, 512]))

        wpst_free = [None, None]
        for fc in range(HLOC):
            buf = fc % 2
            v = P.dma("sq", wpst[buf][:], wp[fc], w=later(wpst_free[buf]),
                      ch=f"c{buf}")
            wpst_free[buf] = P.op(
                "dv", lambda e, fc=fc, buf=buf: e.tensor_copy(
                    wp_bf[:, fc * D:(fc + 1) * D], wpst[buf][:]), w=[v], inc="sDV")
        v_wp = ("sDV", P.cnt["sDV"])
        lr_free = None
        for tp in range(T // 512):
            for fc in range(HLOC):
                vpe = P.op("pe", lambda e, tp=tp, fc=fc: e.matmul(
                    lr_ps[:], pa_bf[:, fc * RANK:(fc + 1) * RANK],
                    y_sb[:, fc * T + tp * 512: fc * T + (tp + 1) * 512],
                    start=(fc == 0), stop=(fc == HLOC - 1)),
                    w=later(lr_free) if fc == 0 else (), inc="sPE")
            lr_free = P.op("sc", lambda e, tp=tp: e.copy(
                lr_sb[:, tp * 512:(tp + 1) * 512], lr_ps[:]), w=[vpe], inc="sSC")
        v_lr = lr_free
        cps_free = [None, None]
        cst_free = [None, None]
        vcc = None
        ntcf = T // 128
        tpq = ntcf // 8
        for q in range(4):
            for tc in [q * tpq + i for i in range(tpq)] + \
                      [ntcf // 2 + q * tpq + i for i in range(tpq)]:
                if tc < ntcf // 2:
                    r0 = q * tpq * 256 + (tc - q * tpq) * 128
                else:
                    r0 = q * tpq * 256 + tpq * 128 + (tc - ntcf // 2 - q * tpq) * 128
                for fog in range(4):
                    buf = fog % 2
                    for fc in range(HLOC):
                        P.op("pe", lambda e, tc=tc, fog=fog, fc=fc, buf=buf: e.matmul(
                            c_ps[buf][:], y_sb[:, fc * T + tc * 128: fc * T + (tc + 1) * 128],
                            wp_bf[:, fc * D + fog * 512: fc * D + (fog + 1) * 512],
                            start=(fc == 0), stop=False),
                            w=later(cps_free[buf], v_wp if tc == 0 else None,
                                    v_lr if tc == 0 else None) if fc == 0 else (),
                            inc="sPE")
                    vpe = P.op("pe", lambda e, tc=tc, fog=fog, buf=buf: e.matmul(
                        c_ps[buf][:], lr_sb[:, tc * 128:(tc + 1) * 128],
                        pb_bf[:, fog * 512:(fog + 1) * 512],
                        start=False, stop=True), inc="sPE")
                    vev = P.op("sc", lambda e, buf=buf: e.copy(cst[buf][:], c_ps[buf][:]),
                               w=later(vpe, cst_free[buf]), inc="sSC")
                    cps_free[buf] = vev
                    cst_free[buf] = P.dma(
                        "sc", cc_in[r0:r0 + 128, fog * 512:(fog + 1) * 512],
                        cst[buf][:], w=[vev], ch=f"c{4 + buf}")
            v_allst = [("c4", P.cnt["c4"]), ("c5", P.cnt["c5"])]
            vcc = P.op("gp", lambda e, q=q: e.collective_compute(
                "ReduceScatter", mybir.AluOpType.add,
                replica_groups=[[0, 1], [2, 3], [4, 5], [6, 7]],
                ins=[cc_in[q * (T // 4):(q + 1) * (T // 4), :].opt()],
                outs=[cc_out[q * (T // 8):(q + 1) * (T // 8), :].opt()]),
                w=v_allst, inc="sCC")
        es_pc.close()
        es_p3.close()
        es_qy.close()
        P.gate()

        es_p4 = ExitStack()
        h_fm = es0.enter_context(SB("h_fm", [128, 16 * To], BF16))
        att_t = [es_p4.enter_context(SB(f"att{i}", [128, D], BF16)) for i in range(2)]
        xo_t = [es_p4.enter_context(SB(f"xo{i}", [128, D])) for i in range(2)]
        r1_t = es_p4.enter_context(SB("r1", [128, D]))
        htm = es_p4.enter_context(SB("htm", [128, D], BF16))
        ln1 = LN(P, nc, es_p4, "ln1", eps_sb[:, 1:2])
        es_pd = ExitStack()
        btr_ps = [es_pd.enter_context(PS(f"btr{i}", [128, 1024], BF16)) for i in range(2)]

        att_free = [None, None]
        xo_free = [None, None]
        r1_free = None
        htm_free = None
        btr_free = [None, None]
        for tc in range(NTC):
            buf = tc % 2
            va = P.dma("sq", att_t[buf][:], cc_out[tc * 128:(tc + 1) * 128, :],
                       w=later(vcc if tc < 2 else None, att_free[buf]),
                       ch=f"c{buf}")
            vx = P.dma("sq", xo_t[buf][:], x_own[tc * 128:(tc + 1) * 128, :],
                       w=later(xo_free[buf]), ch=f"c{2 + buf}")
            vr = P.op("dv", lambda e, buf=buf: e.tensor_tensor(
                r1_t[:], att_t[buf][:], xo_t[buf][:], OP.add),
                w=later(va, vx, r1_free), inc="sDV")
            att_free[buf] = vr
            xo_free[buf] = vr
            vh, r1_readers = ln1.norm(r1_t[:], htm[:], [vr], [htm_free], D)
            r1_free = r1_readers
            vtp_last = None
            for half in range(2):
                tb = half
                vtp = None
                for fc8 in range(8):
                    fc = half * 8 + fc8
                    vtp = P.op("pe", lambda e, fc=fc, fc8=fc8, tb=tb: e.transpose(
                        btr_ps[tb][:, fc8 * 128:(fc8 + 1) * 128],
                        htm[:, fc * 128:(fc + 1) * 128], iden_sb[:]),
                        w=later(vh if fc8 == 0 else None,
                                btr_free[tb] if fc8 == 0 else None), inc="sPE")
                vtp_last = vtp
                evs = []
                for fc8 in range(8):
                    fc = half * 8 + fc8
                    evs.append(P.op("dv", lambda e, fc=fc, fc8=fc8, tb=tb, tc=tc:
                                    e.tensor_copy(
                                        h_fm[:, fc * To + tc * 128: fc * To + (tc + 1) * 128],
                                        btr_ps[tb][:, fc8 * 128:(fc8 + 1) * 128]),
                                    w=[vtp] if fc8 == 0 else (), inc="sDV"))
                btr_free[tb] = evs[-1]
            htm_free = ("sPE", vtp_last[1])
        es_pd.close()
        es_p4.close()
        P.gate()

        es_p5 = ExitStack()
        acc = es_p5.enter_context(SB("acc", [128, FTC * D]))
        dw_bf = es_p5.enter_context(SB("dw_bf", [128, 8 * D], BF16))
        hmid = es_p5.enter_context(SB("hmid", [128, 8 * FP], BF16))
        gw_bf = [es_p5.enter_context(SB(f"gw_bf{i}", [128, D], BF16)) for i in range(3)]
        uw_bf = [es_p5.enter_context(SB(f"uw_bf{i}", [128, D], BF16)) for i in range(3)]
        silu_sb = es_p5.enter_context(SB("silu_sb", [128, FP], BF16))
        dsc_sb = es_p5.enter_context(SB("dsc_sb", [128, D]))
        htm2 = es_p5.enter_context(SB("htm2", [128, D], BF16))
        r2_t = es_p5.enter_context(SB("r2", [128, D]))
        outst = [es_p5.enter_context(SB(f"outst{i}", [128, D])) for i in range(2)]
        ln2 = LN(P, nc, es_p5, "ln2", eps_sb[:, 1:2], alloc_sq=False)
        es_pe_ = ExitStack()
        g_ps = es_pe_.enter_context(PS("g_ps", [128, FP]))
        u_ps = es_pe_.enter_context(PS("u_ps", [128, FP]))
        d_ps = [es_pe_.enter_context(PS(f"d_ps{i}", [128, 512])) for i in range(2)]
        bt2_ps = [es_pe_.enter_context(PS(f"bt2{i}", [128, 1024], BF16)) for i in range(2)]

        v_dsc = P.dma("sq", dsc_sb[:], dscT, ch="c6")
        gwf = [None, None, None]
        uwf = [None, None, None]
        dw_cvt = [None] * 8
        dw_used = None
        gpsf = None
        upsf = None
        dpsf = [None, None]
        hm_cvt = [None] * 8
        hm_used = None
        siluf = None
        accf = [None] * (FTC * 4)
        bt2f = [None, None]
        htm2f = None
        r2f = None
        outf = [None, None]

        for p in range(NFP):
            tp0 = p * FP
            for o in range(NOCT):
                for i8 in range(8):
                    ic = o * 8 + i8
                    bi = ic % 3
                    vgc = P.dma("sq", gw_bf[bi][:], gbf_d[ic], w=later(gwf[bi]),
                                ch=f"c{bi}")
                    vuc = P.dma("sc", uw_bf[bi][:], ubf_d[ic], w=later(uwf[bi]),
                                ch=f"c{8 + bi}")
                    vdc = P.dma("sc", dw_bf[:, i8 * D:(i8 + 1) * D], dbf_d[ic],
                                w=later(dw_used if i8 == 0 else None),
                                ch=f"c{4 + (i8 % 2)}")
                    dw_cvt[i8] = vdc
                    vgp = None
                    for fc in range(16):
                        vgp = P.op("pe", lambda e, fc=fc, bi=bi, tp0=tp0: e.matmul(
                            g_ps[:], gw_bf[bi][:, fc * 128:(fc + 1) * 128],
                            h_fm[:, fc * To + tp0: fc * To + tp0 + FP],
                            start=(fc == 0), stop=(fc == 15)),
                            w=later(vgc, gpsf) if fc == 0 else (), inc="sPE")
                    vup = None
                    for fc in range(16):
                        vup = P.op("pe", lambda e, fc=fc, bi=bi, tp0=tp0: e.matmul(
                            u_ps[:], uw_bf[bi][:, fc * 128:(fc + 1) * 128],
                            h_fm[:, fc * To + tp0: fc * To + tp0 + FP],
                            start=(fc == 0), stop=(fc == 15)),
                            w=later(vuc, upsf) if fc == 0 else (), inc="sPE")
                    gwf[bi] = vgp
                    uwf[bi] = vup
                    vsl = P.op("sc", lambda e, ic=ic: e.activation(
                        silu_sb[:], g_ps[:], AF.Silu, scale=gsc_sb[:, ic:ic + 1]),
                        w=later(vgp, siluf), inc="sSC")
                    gpsf = vsl
                    vhm = P.op("dv", lambda e, ic=ic, i8=i8: e.scalar_tensor_tensor(
                        hmid[:, i8 * FP:(i8 + 1) * FP], u_ps[:], usc_sb[:, ic:ic + 1],
                        silu_sb[:], OP.mult, OP.mult),
                        w=later(vup, vsl, hm_used if i8 == 0 else None), inc="sDV")
                    upsf = vhm
                    siluf = vhm
                    hm_cvt[i8] = vhm
                hm_max = ("sDV", max(v[1] for v in hm_cvt))
                dw_max4 = max(v[1] for v in dw_cvt if v[0] == "c4")
                dw_max5 = max((v[1] for v in dw_cvt if v[0] == "c5"), default=0)
                dw_max = ("c4", dw_max4)
                dw_max_b = ("c5", dw_max5) if dw_max5 else None
                vdw_pe = None
                for tc in range(FTC):
                    for fg in range(4):
                        db = (tc * 4 + fg) % 2
                        vdp = None
                        for i8 in range(8):
                            vdp = P.op("pe", lambda e, i8=i8, tc=tc, fg=fg, db=db:
                                       e.matmul(
                                           d_ps[db][:],
                                           hmid[:, i8 * FP + tc * 128: i8 * FP + (tc + 1) * 128],
                                           dw_bf[:, i8 * D + fg * 512: i8 * D + (fg + 1) * 512],
                                           start=(i8 == 0), stop=(i8 == 7)),
                                       w=later(dpsf[db], hm_max, dw_max, dw_max_b)
                                       if i8 == 0 else (), inc="sPE")
                        vdw_pe = vdp
                        ai = tc * 4 + fg
                        if o == 0:
                            vac = P.op("dv", lambda e, tc=tc, fg=fg, db=db: e.tensor_scalar(
                                acc[:, tc * D + fg * 512: tc * D + (fg + 1) * 512],
                                d_ps[db][:], 1.0, None, OP.mult),
                                w=later(vdp, accf[ai]), inc="sDV")
                        else:
                            vac = P.op("dv", lambda e, tc=tc, fg=fg, db=db:
                                       e.scalar_tensor_tensor(
                                           acc[:, tc * D + fg * 512: tc * D + (fg + 1) * 512],
                                           d_ps[db][:], 1.0,
                                           acc[:, tc * D + fg * 512: tc * D + (fg + 1) * 512],
                                           OP.mult, OP.add),
                                       w=later(vdp, accf[ai]), inc="sDV")
                        dpsf[db] = vac
                        accf[ai] = vac
                dw_used = ("sPE", vdw_pe[1])
                hm_used = dw_used
            for tc in range(FTC):
                tb = tc % 2
                vht = None
                vtp_last = None
                for half in range(2):
                    vtp = None
                    for fc8 in range(8):
                        fc = half * 8 + fc8
                        vtp = P.op("pe", lambda e, fc=fc, fc8=fc8, tb=tb, tc=tc, tp0=tp0:
                                   e.transpose(
                                       bt2_ps[tb][:, fc8 * 128:(fc8 + 1) * 128],
                                       h_fm[:, fc * To + tp0 + tc * 128: fc * To + tp0 + (tc + 1) * 128],
                                       iden_sb[:]),
                                   w=later(bt2f[tb] if fc8 == 0 else None,
                                           htm2f if (half == 0 and fc8 == 0) else None),
                                   inc="sPE")
                    vtp_last = vtp
                    evs = []
                    for fc8 in range(8):
                        fc = half * 8 + fc8
                        evs.append(P.op("dv", lambda e, fc=fc, fc8=fc8, tb=tb:
                                        e.tensor_copy(
                                            htm2[:, fc * 128:(fc + 1) * 128],
                                            bt2_ps[tb][:, fc8 * 128:(fc8 + 1) * 128]),
                                        w=[vtp] if fc8 == 0 else (), inc="sDV"))
                    bt2f[tb] = evs[-1]
                    vht = evs[-1]
                ai0 = tc * 4
                vr2a = P.op("dv", lambda e, tc=tc: e.tensor_tensor(
                    r2_t[:], acc[:, tc * D:(tc + 1) * D], dsc_sb[:], OP.mult),
                    w=later(accf[ai0], accf[ai0 + 1], accf[ai0 + 2], accf[ai0 + 3],
                            r2f, v_dsc), inc="sDV")
                vr2 = P.op("gp", lambda e: e.tensor_tensor(
                    r2_t[:], r2_t[:], htm2[:], OP.add), w=later(vr2a, vht), inc="sGP")
                htm2f = vr2
                ob = tc % 2
                vout, r2_readers = ln2.norm(r2_t[:], outst[ob][:], [vr2],
                                            [outf[ob]], D, sq_ap=outst[ob][:])
                r2f = r2_readers + [vr2]
                outf[ob] = P.dma("sc", out_d[tp0 + tc * 128: tp0 + (tc + 1) * 128, :],
                                 outst[ob][:], w=[vout], ch=f"c{6 + ob}")
                for i in range(4):
                    accf[ai0 + i] = vr2a
        es_pe_.close()
        es_p5.close()
        P.op("sc", lambda e: e.nop(),
             w=[("sDs", P.cnt["sDs"]), ("c6", P.cnt["c6"]), ("c7", P.cnt["c7"])])

        block = es0.enter_context(nc.Block())
        P.emit(block, sems)
    return nc


def _u8_low_byte(w_int32):
    a = np.ascontiguousarray(w_int32)
    assert a.dtype == np.int32
    return a.view(np.uint8).reshape(a.shape + (4,))[..., 0]


def prep_core(c, T, I, x, Wq, Wk, Wv, Wproj, q_gain, q_A, q_B, k_A, k_B,
              v_A, v_B, p_A, p_B, gate_w, gate_scale, up_w, up_scale,
              down_w, down_scale):
    b, s = c // 2, c % 2
    To = T // 2
    NIC = I // 128
    f0 = 1024 * s
    h0 = 8 * s
    kv0 = 256 * s
    bf = ml_dtypes.bfloat16
    tile16 = lambda a: np.ascontiguousarray(a).reshape(16, 128, a.shape[1])
    m = {}
    m["xT"] = np.ascontiguousarray(x[b].T)
    m["x_own"] = np.ascontiguousarray(x[b, s * To:(s + 1) * To, :])
    m["wq"] = tile16(Wq[:, f0:f0 + 1024])
    m["wk"] = tile16(Wk[:, kv0:kv0 + 256])
    m["wv"] = tile16(Wv[:, kv0:kv0 + 256])
    m["wp"] = np.ascontiguousarray(Wproj[f0:f0 + 1024, :]).reshape(8, 128, D)
    m["qa"] = tile16(q_A)
    m["ka"] = tile16(k_A)
    m["va"] = tile16(v_A)
    m["pa"] = np.ascontiguousarray(p_A[f0:f0 + 1024, :]).reshape(8, 128, RANK)
    m["qb"] = np.ascontiguousarray(q_B[:, f0:f0 + 1024])
    m["kb"] = np.ascontiguousarray(k_B[:, kv0:kv0 + 256])
    m["vb"] = np.ascontiguousarray(v_B[:, kv0:kv0 + 256])
    m["pb"] = np.ascontiguousarray(p_B)
    m["qg"] = np.ascontiguousarray(q_gain[h0:h0 + 8]).reshape(1, 8)
    inv = 10000.0 ** (-(np.arange(32, dtype=np.float64)) / 32.0)
    ang = np.arange(T, dtype=np.float64)[:, None] * inv[None, :]
    cosT, sinT = np.cos(ang).T, np.sin(ang).T
    m["rope"] = np.concatenate([cosT, sinT], 0).astype(np.float32).astype(bf)
    m["ropeb"] = np.concatenate([sinT, cosT], 0).astype(np.float32).astype(bf)
    mk = np.zeros((128, 128), np.float32)
    mk[np.triu_indices(128, 1)] = -30000.0
    m["maskt"] = mk
    m["iden"] = np.eye(128, dtype=np.float32).astype(bf)
    gT = np.ascontiguousarray(_u8_low_byte(gate_w).T)
    uT = np.ascontiguousarray(_u8_low_byte(up_w).T)
    dT = np.ascontiguousarray(_u8_low_byte(down_w).T)
    m["gt8"] = np.ascontiguousarray(
        gT.reshape(16, 128, NIC, 128).transpose(2, 1, 0, 3).reshape(NIC, 128, D))
    m["ut8"] = np.ascontiguousarray(
        uT.reshape(16, 128, NIC, 128).transpose(2, 1, 0, 3).reshape(NIC, 128, D))
    m["dt8"] = np.ascontiguousarray(dT.reshape(NIC, 128, D))
    m["gsc"] = np.ascontiguousarray(gate_scale.reshape(NIC, 128).T)
    m["usc"] = np.ascontiguousarray(up_scale.reshape(NIC, 128).T)
    m["dscT"] = np.ascontiguousarray(np.broadcast_to(down_scale[None, :], (128, D)))
    ep = np.empty((128, 2), np.float32)
    ep[:, 0] = RMS_EPS
    ep[:, 1] = LN_EPS
    m["epst"] = ep
    return m


def run(x, ws, T, I, core_ids=None):
    in_maps = [prep_core(c, T, I, x, *ws) for c in range(8)]
    nc = build(T, I)
    res = run_bass_kernel_spmd(nc, in_maps, core_ids=list(range(8)))
    out = np.empty((4, T, D), np.float32)
    To = T // 2
    for c in range(8):
        b, s = c // 2, c % 2
        out[b, s * To:(s + 1) * To, :] = res.results[c]["out"]
    return out


def kernel(L0_attn_x, L0_attn_Wq, L0_attn_Wk, L0_attn_Wv, L0_attn_Wproj,
           L0_attn_q_gain, L0_attn_q_A, L0_attn_q_B, L0_attn_k_A, L0_attn_k_B,
           L0_attn_v_A, L0_attn_v_B, L0_attn_p_A, L0_attn_p_B,
           L0_ffn_gate_w, L0_ffn_gate_scale, L0_ffn_up_w, L0_ffn_up_scale,
           L0_ffn_down_w, L0_ffn_down_scale):
    x = np.asarray(L0_attn_x, dtype=np.float32)
    ws = (np.asarray(L0_attn_Wq, np.float32), np.asarray(L0_attn_Wk, np.float32),
          np.asarray(L0_attn_Wv, np.float32), np.asarray(L0_attn_Wproj, np.float32),
          np.asarray(L0_attn_q_gain, np.float32), np.asarray(L0_attn_q_A, np.float32),
          np.asarray(L0_attn_q_B, np.float32), np.asarray(L0_attn_k_A, np.float32),
          np.asarray(L0_attn_k_B, np.float32), np.asarray(L0_attn_v_A, np.float32),
          np.asarray(L0_attn_v_B, np.float32), np.asarray(L0_attn_p_A, np.float32),
          np.asarray(L0_attn_p_B, np.float32),
          np.asarray(L0_ffn_gate_w, np.int32), np.asarray(L0_ffn_gate_scale, np.float32),
          np.asarray(L0_ffn_up_w, np.int32), np.asarray(L0_ffn_up_scale, np.float32),
          np.asarray(L0_ffn_down_w, np.int32), np.asarray(L0_ffn_down_scale, np.float32))
    return run(x, ws, 4096, 8192)


# version 31
# speedup vs baseline: 1.0845x; 1.0073x over previous
import numpy as np
import ml_dtypes
import concourse.bass as bass
import concourse.mybir as mybir
from concourse.bass_utils import run_bass_kernel_spmd
from concourse.alu_op_type import AluOpType
from contextlib import ExitStack

F32 = mybir.dt.float32
BF16 = mybir.dt.bfloat16
U8 = mybir.dt.uint8
AF = mybir.ActivationFunctionType
AX = mybir.AxisListType.X
OP = AluOpType

D = 2048
HD = 128
HLOC = 8
KVLOC = 2
RANK = 16
RMS_EPS = 1.1920929e-07
LN_EPS = 1e-5
INV_SQRT_HD = float(HD) ** -0.5

SEMS = (["sDq", "sDs", "sDg", "sPE", "sSC", "sDV", "sGP", "sCC", "sPH"]
        + [f"c{i}" for i in range(12)])
ENGS = ["sq", "sc", "dv", "gp", "pe"]


def later(*evs):
    out = []
    for e in evs:
        if e is None:
            continue
        if isinstance(e, list):
            out.extend(x for x in e if x is not None)
        else:
            out.append(e)
    return out


class Planner:
    def __init__(self):
        self.recs = {e: [] for e in ENGS}
        self.cnt = {s: 0 for s in SEMS}
        self.gate_no = 0
        self.last_gate = {e: 0 for e in ENGS}

    def op(self, eng, fn, w=(), inc=None, n=1):
        waits = [x for x in w if x is not None and x[1] > 0]
        if self.last_gate[eng] < self.gate_no:
            waits.insert(0, ("sPH", self.gate_no))
            self.last_gate[eng] = self.gate_no
        self.recs[eng].append((fn, waits, inc, n))
        if inc is not None:
            self.cnt[inc] += n
            return (inc, self.cnt[inc])
        return None

    def dma(self, q, dst, src, w=(), ch=None):
        sem = ch or {"sq": "sDq", "sc": "sDs", "gp": "sDg", "dv": "sDg"}[q]
        return self.op(q, lambda e, d=dst, s=src: e.dma_start(out=d, in_=s),
                       w=w, inc=sem, n=16)

    def gate(self):
        waits = [(s, self.cnt[s]) for s in SEMS if s != "sPH" and self.cnt[s] > 0]
        self.recs["gp"].append((lambda e: e.nop(), waits, "sPH", 1))
        self.cnt["sPH"] += 1
        self.gate_no = self.cnt["sPH"]
        self.last_gate["gp"] = self.gate_no

    def emit(self, block, sems):
        eng_of = {"sq": block.sync, "sc": block.scalar, "dv": block.vector,
                  "gp": block.gpsimd, "pe": block.tensor}
        for name in ENGS:
            recs = self.recs[name]

            def make(recs=recs):
                def body(e):
                    for fn, waits, inc, n in recs:
                        for s, v in waits:
                            e.wait_ge(sems[s], v)
                        inst = fn(e)
                        if inc is not None:
                            inst.then_inc(sems[inc], n)
                return body
            eng_of[name](make())


class LN:

    def __init__(self, P, nc, es, tag, eps_ap, alloc_sq=True):
        self.P = P
        self.eps = eps_ap
        self.stm = es.enter_context(nc.sbuf_tensor(f"stm_{tag}", [128, 4], F32))
        self.sq = (es.enter_context(nc.sbuf_tensor(f"sq_{tag}", [128, D], F32))
                   if alloc_sq else None)
        self.free = None

    def norm(self, src_ap, out_ap, w_src, w_out, ND, sq_ap=None):
        P, stm = self.P, self.stm
        sq = sq_ap if sq_ap is not None else self.sq[:]
        v1 = P.op("dv", lambda e: e.reduce_sum(stm[:, 0:1], src_ap, axis=AX),
                  w=later(*w_src, self.free), inc="sDV")
        v2 = P.op("sc", lambda e: e.square(sq, src_ap),
                  w=later(*w_src, self.free, *w_out), inc="sSC")
        v3 = P.op("dv", lambda e: e.reduce_sum(stm[:, 1:2], sq, axis=AX),
                  w=[v2], inc="sDV")
        P.op("dv", lambda e: e.drain())
        P.op("dv", lambda e: e.tensor_scalar(stm[:, 2:3], stm[:, 0:1],
                                             1.0 / ND, None, OP.mult))
        P.op("dv", lambda e: e.tensor_scalar(stm[:, 1:2], stm[:, 1:2],
                                             1.0 / ND, None, OP.mult))
        P.op("dv", lambda e: e.drain())
        P.op("dv", lambda e: e.tensor_tensor(stm[:, 0:1], stm[:, 2:3],
                                             stm[:, 2:3], OP.mult))
        P.op("dv", lambda e: e.drain())
        v5 = P.op("dv", lambda e: e.tensor_tensor(stm[:, 1:2], stm[:, 1:2],
                                                  stm[:, 0:1], OP.subtract), inc="sDV")
        v6 = P.op("sc", lambda e: e.activation(stm[:, 0:1], stm[:, 1:2],
                                               AF.Ln, bias=self.eps), w=[v5], inc="sSC")
        P.op("sc", lambda e: e.drain())
        v7 = P.op("sc", lambda e: e.activation(stm[:, 3:4], stm[:, 0:1],
                                               AF.Exp, scale=-0.5), inc="sSC")
        v8 = P.op("dv", lambda e, o=out_ap: e.tensor_scalar(
            o, src_ap, stm[:, 2:3], stm[:, 3:4], OP.subtract, OP.mult),
            w=later(v7, *w_out), inc="sDV")
        self.free = [v8, v7]
        return v8, [v8, v2]


def build(T, I):
    To = T // 2
    NQC = T // 128
    NPASS = T // 512
    NIC = I // 128
    NOCT = NIC // 8
    FP = 512
    NFP = To // FP
    FTC = FP // 128
    NTC = To // 128

    nc = bass.Bass("TRN2", target_bir_lowering=False, debug=False)
    din = lambda nm, sh, d=F32: nc.dram_tensor(nm, sh, d, kind="ExternalInput").ap()
    xT = din("xT", [D, T])
    x_own = din("x_own", [To, D])
    wq = din("wq", [16, 128, HLOC * HD])
    wk = din("wk", [16, 128, KVLOC * HD])
    wv = din("wv", [16, 128, KVLOC * HD])
    wp = din("wp", [HLOC, 128, D])
    qa = din("qa", [16, 128, RANK])
    ka = din("ka", [16, 128, RANK])
    va = din("va", [16, 128, RANK])
    pa = din("pa", [HLOC, 128, RANK])
    qb = din("qb", [RANK, HLOC * HD])
    kb = din("kb", [RANK, KVLOC * HD])
    vb = din("vb", [RANK, KVLOC * HD])
    pb = din("pb", [RANK, D])
    qg = din("qg", [1, HLOC])
    rope = din("rope", [64, T], BF16)
    ropeb = din("ropeb", [64, T], BF16)
    maskt = din("maskt", [128, 128])
    iden = din("iden", [128, 128], BF16)
    gt8 = din("gt8", [NIC, 128, D], U8)
    ut8 = din("ut8", [NIC, 128, D], U8)
    dt8 = din("dt8", [NIC, 128, D], U8)
    gsc = din("gsc", [128, NIC])
    usc = din("usc", [128, NIC])
    dscT = din("dscT", [128, D])
    epst = din("epst", [128, 2])
    out_d = nc.dram_tensor("out", [To, D], F32, kind="ExternalOutput").ap()
    mq_dram = nc.dram_tensor("mq_dram", [HLOC * NQC, 128], BF16).ap()
    cc_in = nc.dram_tensor("cc_in", [T, D], BF16)
    gbf_d = nc.dram_tensor("gbf_d", [NIC, 128, D], BF16).ap()
    ubf_d = nc.dram_tensor("ubf_d", [NIC, 128, D], BF16).ap()
    dbf_d = nc.dram_tensor("dbf_d", [NIC, 128, D], BF16).ap()
    cc_out = nc.dram_tensor("cc_out", [To, D], BF16)

    P = Planner()
    es0 = ExitStack()
    with es0:
        sems = {s: es0.enter_context(nc.semaphore(s)) for s in SEMS}
        SB = lambda nm, sh, d=F32: nc.sbuf_tensor(nm, sh, d)
        PS = lambda nm, sh, d=F32: nc.psum_tensor(nm, sh, d)

        eps_sb = es0.enter_context(SB("eps_sb", [128, 2]))
        iden_sb = es0.enter_context(SB("iden_sb", [128, 128], BF16))
        mask_sb = es0.enter_context(SB("mask_sb", [128, 128]))
        qg_sb = es0.enter_context(SB("qg_sb", [1, HLOC]))
        gsc_sb = es0.enter_context(SB("gsc_sb", [128, NIC]))
        usc_sb = es0.enter_context(SB("usc_sb", [128, NIC]))
        ones_sb = es0.enter_context(SB("ones_sb", [128, 1], BF16))
        nones_sb = es0.enter_context(SB("nones_sb", [128, 1], BF16))
        onesk1 = es0.enter_context(SB("onesk1", [1, 128], BF16))
        pa_bf = es0.enter_context(SB("pa_bf", [128, HLOC * RANK], BF16))
        pb_bf = es0.enter_context(SB("pb_bf", [RANK, D], BF16))

        es_qy = ExitStack()
        q_sb = es_qy.enter_context(SB("q_sb", [128, HLOC * T], BF16))
        y_sb = q_sb
        es_kv = ExitStack()
        k_sb = es_kv.enter_context(SB("k_sb", [128, KVLOC * T], BF16))
        v_sb = es_kv.enter_context(SB("v_sb", [128, (T // 128) * KVLOC * HD], BF16))

        es_p1 = ExitStack()
        ropeA_t = es_p1.enter_context(SB("ropeA_t", [64, 512], BF16))
        ropeB_t = es_p1.enter_context(SB("ropeB_t", [64, 512], BF16))
        qa_bf = es_p1.enter_context(SB("qa_bf", [128, 16 * RANK], BF16))
        ka_bf = es_p1.enter_context(SB("ka_bf", [128, 16 * RANK], BF16))
        va_bf = es_p1.enter_context(SB("va_bf", [128, 16 * RANK], BF16))
        qb_bf = es_p1.enter_context(SB("qb_bf", [RANK, HLOC * HD], BF16))
        kb_bf = es_p1.enter_context(SB("kb_bf", [RANK, KVLOC * HD], BF16))
        vb_bf = es_p1.enter_context(SB("vb_bf", [RANK, KVLOC * HD], BF16))
        xstage = [es_p1.enter_context(SB(f"xstage{i}", [128, 1024])) for i in range(2)]
        wq_bf = es_p1.enter_context(SB("wq_bf", [128, 16 * HLOC * HD], BF16))
        wk_bf = es_p1.enter_context(SB("wk_bf", [128, 16 * KVLOC * HD], BF16))
        wv_bf = es_p1.enter_context(SB("wv_bf", [128, 16 * KVLOC * HD], BF16))

        for dpair in [(eps_sb[:], epst), (mask_sb[:], maskt), (qg_sb[:], qg),
                      (gsc_sb[:], gsc), (usc_sb[:], usc),
                      (iden_sb[:], iden)]:
            P.dma("sq", dpair[0], dpair[1])
        P.op("dv", lambda e: e.memset(ones_sb[:], 1.0))
        P.op("dv", lambda e: e.memset(nones_sb[:], -1.0))
        P.op("dv", lambda e: e.memset(onesk1[:], 1.0), inc="sDV")

        jobs = []
        for fc in range(16):
            jobs.append((wq[fc], 128, HLOC * HD, wq_bf, fc * HLOC * HD))
        for i in range(4):
            jobs.append((wk.rearrange("c p n -> p c n")[:, i * 4:(i + 1) * 4, :],
                         128, 4 * KVLOC * HD, wk_bf, i * 4 * KVLOC * HD))
            jobs.append((wv.rearrange("c p n -> p c n")[:, i * 4:(i + 1) * 4, :],
                         128, 4 * KVLOC * HD, wv_bf, i * 4 * KVLOC * HD))
        jobs += [(qa.rearrange("c p n -> p c n"), 128, 16 * RANK, qa_bf, 0),
                 (ka.rearrange("c p n -> p c n"), 128, 16 * RANK, ka_bf, 0),
                 (va.rearrange("c p n -> p c n"), 128, 16 * RANK, va_bf, 0),
                 (pa.rearrange("c p n -> p c n"), 128, HLOC * RANK, pa_bf, 0),
                 (qb, RANK, HLOC * HD, qb_bf, 0),
                 (kb, RANK, KVLOC * HD, kb_bf, 0),
                 (vb, RANK, KVLOC * HD, vb_bf, 0),
                 (pb[:, 0:1024], RANK, 1024, pb_bf, 0),
                 (pb[:, 1024:2048], RANK, 1024, pb_bf, 1024)]
        xst_ev = [None, None]
        for j, (src, rows, cols, dst, off) in enumerate(jobs):
            buf = j % 2
            st = xstage[buf]
            if len(src.shape) == 3:
                d_ap = st[:, 0:cols].rearrange("p (c n) -> p c n", c=src.shape[1])
            else:
                d_ap = st[0:rows, 0:cols]
            v = P.dma("sq", d_ap, src, w=later(xst_ev[buf]), ch=f"c{buf}")
            xst_ev[buf] = P.op(
                "dv", lambda e, a=st[0:rows, 0:cols], b=dst[0:rows, off:off + cols]:
                e.tensor_copy(b, a), w=[v], inc="sDV")
        P.gate()

        x_fm = es_p1.enter_context(SB("x_fm", [128, 16 * 512], BF16))
        qsq = [es_p1.enter_context(SB(f"qsq{i}", [128, 512], BF16)) for i in range(2)]
        stat1 = es_p1.enter_context(SB("stat1", [1, 512]))
        stat2 = es_p1.enter_context(SB("stat2", [1, 512]))
        mqrow = es_p1.enter_context(SB("mqrow", [1, 512], BF16))
        mkbr = es_p1.enter_context(SB("mkbr", [1, 512], BF16))
        bcev = es_p1.enter_context(SB("bcev", [128, 512], BF16))
        qn_sb = [es_p1.enter_context(SB(f"qn{i}", [128, 512], BF16)) for i in range(2)]
        rs1 = es_p1.enter_context(SB("rs1", [64, 512], BF16))
        rs2 = es_p1.enter_context(SB("rs2", [64, 512], BF16))
        rsa = es_p1.enter_context(SB("rsa", [32, 512], BF16))
        rsb = es_p1.enter_context(SB("rsb", [32, 512], BF16))
        t1q = es_p1.enter_context(SB("t1q", [RANK, 512], BF16))
        t1k = es_p1.enter_context(SB("t1k", [RANK, 512], BF16))
        t1v = es_p1.enter_context(SB("t1v", [RANK, 512], BF16))
        es_pa = ExitStack()
        qk_ps = [es_pa.enter_context(PS(f"qk_ps{i}", [128, 512])) for i in range(3)]
        t1_ps = es_pa.enter_context(PS("t1_ps", [RANK, 512]))
        sq_ps = [es_pa.enter_context(PS("sq_ps0", [1, 512]))]
        bc_ps = es_pa.enter_context(PS("bc_ps", [128, 512]))
        v_ps = [es_pa.enter_context(PS(f"v_ps{i}", [128, 512])) for i in range(2)]

        xfm_rd = None
        qkfree = [None, None, None]
        t1free = None
        sqfree = [None]
        bcfree = None
        vpsfree = [None, None]
        qsqfree = [None, None]
        qnfree = [None, None]
        stat2_free = None
        mqrow_free = [None]

        def rms_stats(pi, seg, wpe):
            nonlocal stat2_free
            qb_ = pi % 2
            vsc = P.op("sc", lambda e, p=qk_ps[pi], q=qsq[qb_]: e.square(q[:], p[:]),
                       w=later(wpe, qsqfree[qb_]), inc="sSC")
            vpe = P.op("pe", lambda e, sp=sq_ps[seg], q=qsq[qb_]: e.matmul(
                sp[:], ones_sb[:], q[:], start=True, stop=True),
                w=later(vsc, sqfree[seg]), inc="sPE")
            qsqfree[qb_] = vpe
            v1 = P.op("sc", lambda e, sp=sq_ps[seg]: e.activation(
                stat1[:], sp[:], AF.Ln, bias=eps_sb[0:1, 0:1], scale=1.0 / HD),
                w=later(vpe), inc="sSC")
            sqfree[seg] = v1
            P.op("sc", lambda e: e.drain())
            v2 = P.op("sc", lambda e: e.activation(
                stat2[:], stat1[:], AF.Exp, scale=-0.5),
                w=later(stat2_free), inc="sSC")
            return v2

        rope_ev = [None, None]

        def rope_emit(qb_, dst, doff, pcol, wsrc):
            src = qn_sb[qb_]
            csn = ropeA_t[:, :]
            snc = ropeB_t[:, :]
            P.op("dv", lambda e: e.drain())
            P.op("dv", lambda e: e.tensor_tensor(rs1[:, :], src[0:64, :], csn, OP.mult),
                 w=later(wsrc, rope_ev[0]))
            P.op("dv", lambda e: e.tensor_tensor(rs2[:, :], src[0:64, :], snc, OP.mult))
            P.op("dv", lambda e: e.drain())
            P.op("dv", lambda e: e.tensor_copy(rsa[:, :], rs1[32:64, :]))
            P.op("dv", lambda e: e.tensor_copy(rsb[:, :], rs2[32:64, :]))
            P.op("dv", lambda e: e.drain())
            P.op("dv", lambda e: e.tensor_tensor(
                dst[0:32, doff:doff + 512], rs1[0:32, :], rsa[:, :], OP.add))
            vd = P.op("dv", lambda e: e.tensor_tensor(
                dst[32:64, doff:doff + 512], rsb[:, :], rs2[0:32, :], OP.subtract),
                inc="sDV")
            rope_ev[1] = vd
            vg = P.op("gp", lambda e: e.tensor_copy(
                dst[64:128, doff:doff + 512], src[64:128, :]), w=later(wsrc), inc="sGP")
            return [vd, vg]

        for p in range(NPASS):
            t0 = p * 512
            P.dma("sq", ropeA_t[:, :], rope[:, t0:t0 + 512],
                  w=later(rope_ev[1]), ch="c3")
            rope_ev[0] = P.dma("sq", ropeB_t[:, :], ropeb[:, t0:t0 + 512], ch="c3")
            for fc in range(16):
                buf = fc % 2
                st = xstage[buf]
                v = P.dma("sq", st[:, 0:512], xT[fc * 128:(fc + 1) * 128, t0:t0 + 512],
                          w=later(xst_ev[buf], xfm_rd if fc < 2 else None),
                          ch=f"c{buf}")
                xst_ev[buf] = P.op(
                    "dv", lambda e, a=st[:, 0:512], b=x_fm[:, fc * 512:(fc + 1) * 512]:
                    e.tensor_copy(b, a), w=[v], inc="sDV")
            v_xf = ("sDV", P.cnt["sDV"])
            for asb, tsb in [(qa_bf, t1q), (ka_bf, t1k), (va_bf, t1v)]:
                for fc in range(16):
                    vpe = P.op("pe", lambda e, fc=fc, a=asb: e.matmul(
                        t1_ps[:], a[:, fc * RANK:(fc + 1) * RANK],
                        x_fm[:, fc * 512:(fc + 1) * 512],
                        start=(fc == 0), stop=(fc == 15)),
                        w=later(v_xf, t1free) if fc == 0 else (), inc="sPE")
                t1free = P.op("sc", lambda e, t=tsb: e.copy(t[:], t1_ps[:]),
                              w=[vpe], inc="sSC")
            v_t1 = t1free
            for h in range(HLOC):
                pi = h % 3
                for fc in range(16):
                    P.op("pe", lambda e, fc=fc, h=h, pp=qk_ps[pi]: e.matmul(
                        pp[:], wq_bf[:, fc * HLOC * HD + h * HD: fc * HLOC * HD + (h + 1) * HD],
                        x_fm[:, fc * 512:(fc + 1) * 512],
                        start=(fc == 0), stop=False),
                        w=later(qkfree[pi], v_t1) if fc == 0 else (), inc="sPE")
                vpe = P.op("pe", lambda e, h=h, pp=qk_ps[pi]: e.matmul(
                    pp[:], qb_bf[:, h * HD:(h + 1) * HD], t1q[:],
                    start=False, stop=True), inc="sPE")
                v2 = rms_stats(pi, 0, vpe)
                vmq = P.op("dv", lambda e, h=h: e.tensor_scalar(
                    mqrow[:], stat2[:], qg_sb[0:1, h:h + 1], INV_SQRT_HD,
                    OP.mult, OP.mult), w=later(v2, mqrow_free[0]), inc="sDV")
                stat2_free = vmq
                mqrow_free[0] = P.dma("sc",
                      mq_dram[h * NQC + p * 4: h * NQC + (p + 1) * 4, :]
                      .rearrange("(o c) n -> o c n", o=1),
                      mqrow[0:1, :].rearrange("o (c n) -> o c n", c=4), w=[vmq],
                      ch="c2")
                qb_ = h % 2
                vq = P.op("sc", lambda e, pp=qk_ps[pi], q=qn_sb[qb_]: e.copy(q[:], pp[:]),
                          w=later(vpe, qnfree[qb_]), inc="sSC")
                qkfree[pi] = vq
                qnfree[qb_] = rope_emit(qb_, q_sb, h * T + t0, t0, vq)
            for kv in range(KVLOC):
                pi = (HLOC + kv) % 3
                for fc in range(16):
                    P.op("pe", lambda e, fc=fc, kv=kv, pp=qk_ps[pi]: e.matmul(
                        pp[:], wk_bf[:, fc * KVLOC * HD + kv * HD: fc * KVLOC * HD + (kv + 1) * HD],
                        x_fm[:, fc * 512:(fc + 1) * 512],
                        start=(fc == 0), stop=False),
                        w=later(qkfree[pi]) if fc == 0 else (), inc="sPE")
                vpe = P.op("pe", lambda e, kv=kv, pp=qk_ps[pi]: e.matmul(
                    pp[:], kb_bf[:, kv * HD:(kv + 1) * HD], t1k[:],
                    start=False, stop=True), inc="sPE")
                v2 = rms_stats(pi, 0, vpe)
                vmk = P.op("dv", lambda e: e.tensor_copy(mkbr[:], stat2[:]),
                           w=[v2], inc="sDV")
                stat2_free = vmk
                vbc = P.op("pe", lambda e: e.matmul(
                    bc_ps[:], onesk1[:], mkbr[:], start=True, stop=True),
                    w=later(vmk, bcfree), inc="sPE")
                vbs = P.op("sc", lambda e: e.copy(bcev[:], bc_ps[:]), w=[vbc], inc="sSC")
                bcfree = vbs
                qb_ = kv % 2
                vkn = P.op("dv", lambda e, pp=qk_ps[pi], q=qn_sb[qb_]: e.tensor_tensor(
                    q[:], pp[:], bcev[:], OP.mult),
                    w=later(vbs, qnfree[qb_]), inc="sDV")
                qkfree[pi] = vkn
                qnfree[qb_] = rope_emit(qb_, k_sb, kv * T + t0, t0, vkn)
            for tc in range(4):
                buf = tc % 2
                gchunk = p * 4 + tc
                for fc in range(16):
                    P.op("pe", lambda e, fc=fc, tc=tc, pp=v_ps[buf]: e.matmul(
                        pp[:, 0:KVLOC * HD],
                        x_fm[:, fc * 512 + tc * 128: fc * 512 + (tc + 1) * 128],
                        wv_bf[:, fc * KVLOC * HD:(fc + 1) * KVLOC * HD],
                        start=(fc == 0), stop=False),
                        w=later(vpsfree[buf]) if fc == 0 else (), inc="sPE")
                vpe = P.op("pe", lambda e, tc=tc, pp=v_ps[buf]: e.matmul(
                    pp[:, 0:KVLOC * HD], t1v[:, tc * 128:(tc + 1) * 128], vb_bf[:],
                    start=False, stop=True), inc="sPE")
                vpsfree[buf] = P.op(
                    "sc", lambda e, g=gchunk, pp=v_ps[buf]: e.copy(
                        v_sb[:, g * KVLOC * HD:(g + 1) * KVLOC * HD],
                        pp[:, 0:KVLOC * HD]), w=[vpe], inc="sSC")
            xfm_rd = ("sPE", P.cnt["sPE"])
        es_pa.close()
        es_p1.close()
        P.gate()

        es_p2 = ExitStack()
        w8st = [es_p2.enter_context(SB(f"w8st{i}", [128, D], U8)) for i in range(2)]
        wbfst = [es_p2.enter_context(SB(f"wbfst{i}", [128, D], BF16)) for i in range(2)]
        mqt_bf = es_p2.enter_context(SB("mqt_bf", [128, HLOC * NQC], BF16))
        mqt = es_p2.enter_context(SB("mqt", [128, HLOC * NQC]))
        s_sb = [es_p2.enter_context(SB(f"s_sb{i}", [128, T], BF16)) for i in range(3)]
        p_sb = [es_p2.enter_context(SB(f"p_sb{i}", [128, T], BF16)) for i in range(3)]
        pt_sb = [es_p2.enter_context(SB(f"pt_sb{i}", [128, 1024], BF16)) for i in range(2)]
        mxt = [es_p2.enter_context(SB(f"mxt{i}", [128, 8])) for i in range(3)]
        ngm = [es_p2.enter_context(SB(f"ngm{i}", [128, 1])) for i in range(3)]
        rsum = [es_p2.enter_context(SB(f"rsum{i}", [128, 1])) for i in range(3)]
        rinv = [es_p2.enter_context(SB(f"rinv{i}", [128, 1])) for i in range(3)]
        vsq_sb = es_p2.enter_context(SB("vsq_sb", [128, KVLOC * HD], BF16))
        sv_sb = es_p2.enter_context(SB("sv_sb", [128, 2 * KVLOC]))
        rv_sb = es_p2.enter_context(SB("rv_sb", [128, KVLOC]))
        vh_tok = es_p2.enter_context(SB("vh_tok", [128, KVLOC * HD], BF16))
        vh_fm = es_p2.enter_context(SB("vh_fm", [128, KVLOC * HD], BF16))
        ysb = [es_p2.enter_context(SB(f"ysb{i}", [128, 128], BF16)) for i in range(4)]
        zsb = es_p2.enter_context(SB("zsb", [128, 128], BF16))
        ndsb = es_p2.enter_context(SB("ndsb", [1, 128], BF16))
        prsb = es_p2.enter_context(SB("prsb", [128, 128], BF16))
        es_pb = ExitStack()
        s_ps = [es_pb.enter_context(PS(f"s_ps{i}", [128, 512])) for i in range(3)]
        pt_ps = [es_pb.enter_context(PS(f"pt_ps{i}", [128, 1024], BF16)) for i in range(2)]
        y_ps = [es_pb.enter_context(PS(f"y_ps{i}", [128, 512])) for i in range(2)]
        miscA = es_pb.enter_context(PS("miscA", [128, 512]))

        vt = P.op("sq", lambda e: e.dma_start_transpose(mqt_bf[:], mq_dram[:, :]),
                  w=[("c2", P.cnt["c2"])], inc="c3", n=16)
        P.op("dv", lambda e: e.tensor_copy(mqt[:], mqt_bf[:]), w=[vt], inc="sDV")

        w8cv = [None, None]
        wbf_out = [None, None]
        sps_free = [None, None, None]
        ptps_free = [None, None]
        ptsb_free = [None, None]
        yps_free = [None, None]
        ssb_free = [None] * 3
        psb_free = [None] * 3
        mxt_free = [None] * 3
        ngm_free = [None] * 3
        rs_free = [None] * 3
        ri_free = [None] * 3
        st2 = {"vhat": None, "vhtok": None, "vtr": None, "vhfm": None,
               "z": None, "nd": None, "pr": None, "mA": None, "svf": None}
        ysb_free = [None] * 4
        ptrot = [0]

        def attn_head(qc, kv, g):
            h = kv * 4 + g
            ext = (qc + 1) * 128
            ng = (ext + 511) // 512
            sb = h % 3
            mb = h % 3
            col = h * NQC + qc
            scans = None
            for j in range(ng):
                w = min(512, ext - j * 512)
                pj = j % 3
                vpe = P.op("pe", lambda e, j=j, w=w, pj=pj, h=h, kv=kv, qc=qc: e.matmul(
                    s_ps[pj][:, 0:w],
                    q_sb[:, h * T + qc * 128: h * T + (qc + 1) * 128],
                    k_sb[:, kv * T + j * 512: kv * T + j * 512 + w],
                    start=True, stop=True),
                    w=later(sps_free[pj], ssb_free[sb] if j == 0 else None,
                            mxt_free[mb] if j == 0 else None), inc="sPE")
                scans = P.op("dv", lambda e, j=j, w=w, pj=pj, sb=sb, mb=mb, col=col:
                             e.tensor_scalar(
                                 s_sb[sb][:, j * 512: j * 512 + w], s_ps[pj][:, 0:w],
                                 mqt[:, col:col + 1], None, OP.mult, OP.max,
                                 accum_out=mxt[mb][:, j:j + 1]),
                             w=[vpe], inc="sDV")
                sps_free[pj] = scans
            vmask = P.op("dv", lambda e, sb=sb, qc=qc: e.tensor_tensor(
                s_sb[sb][:, qc * 128:(qc + 1) * 128],
                s_sb[sb][:, qc * 128:(qc + 1) * 128], mask_sb[:], OP.add),
                w=later(scans), inc="sDV")
            vngm = P.op("dv", lambda e, mb=mb, ng=ng: e.reduce_max(
                ngm[mb][:, 0:1], mxt[mb][:, 0:ng], axis=AX, negate=True),
                w=later(scans, ngm_free[mb]), inc="sDV")
            mxt_free[mb] = vngm
            vexp = P.op("sc", lambda e, sb=sb, mb=mb, ext=ext: e.activation(
                p_sb[sb][:, 0:ext], s_sb[sb][:, 0:ext], AF.Exp,
                bias=ngm[mb][:, 0:1], accum_out=rsum[mb][:, 0:1]),
                w=later(vmask, vngm, psb_free[sb], rs_free[mb]), inc="sSC")
            ssb_free[sb] = vexp
            ngm_free[mb] = vexp
            return dict(qc=qc, kv=kv, g=g, h=h, ext=ext, sb=sb, mb=mb, vexp=vexp)

        def attn_ptpv(ctx):
            qc, kv, g, sb, mb = (ctx["qc"], ctx["kv"], ctx["g"], ctx["sb"],
                                 ctx["mb"])
            ext = ctx["ext"]
            vri = P.op("dv", lambda e, mb=mb: e.reciprocal(
                rinv[mb][:, 0:1], rsum[mb][:, 0:1]),
                w=later(ctx["vexp"], ri_free[mb]), inc="sDV")
            rs_free[mb] = vri
            vpn = P.op("dv", lambda e, sb=sb, mb=mb, ext=ext: e.tensor_scalar(
                p_sb[sb][:, 0:ext], p_sb[sb][:, 0:ext], rinv[mb][:, 0:1],
                None, OP.mult), w=[vri], inc="sDV")
            ri_free[mb] = vpn
            ctx["vpn"] = vpn
            yb = kv % 2
            nch = qc + 1
            groups = []
            for c0 in range(0, nch, 8):
                groups.append((c0, min(8, nch - c0)))

            def emit_T(c0, cw):
                tb = ptrot[0] % 2
                ptrot[0] += 1
                vtp = None
                for ci in range(cw):
                    c = c0 + ci
                    vtp = P.op("pe", lambda e, c=c, ci=ci, tb=tb, sb=sb: e.transpose(
                        pt_ps[tb][:, ci * 128:(ci + 1) * 128],
                        p_sb[sb][:, c * 128:(c + 1) * 128], iden_sb[:]),
                        w=later(ctx["vpn"] if ci == 0 else None,
                                ptps_free[tb] if ci == 0 else None), inc="sPE")
                vev = P.op("sc", lambda e, tb=tb, cw=cw: e.copy(
                    pt_sb[tb][:, 0:cw * 128], pt_ps[tb][:, 0:cw * 128]),
                    w=later(vtp, ptsb_free[tb]), inc="sSC")
                ptps_free[tb] = vev
                return tb, vev

            def emit_PV(c0, cw, tb, vev):
                vpv = None
                for ci in range(cw):
                    c = c0 + ci
                    vpv = P.op("pe", lambda e, c=c, ci=ci, tb=tb, kv=kv, g=g, yb=yb, nch=nch:
                               e.matmul(
                                   y_ps[yb][:, g * 128:(g + 1) * 128],
                                   v_sb[:, c * KVLOC * HD + kv * HD: c * KVLOC * HD + (kv + 1) * HD],
                                   pt_sb[tb][:, ci * 128:(ci + 1) * 128],
                                   start=(c == 0), stop=(c == nch - 1)),
                               w=later(vev if ci == 0 else None,
                                       yps_free[yb] if (c == 0 and g == 0) else None),
                               inc="sPE")
                ptsb_free[tb] = vpv
                return vpv

            prev = None
            vpv = None
            for c0, cw in groups:
                tb, vev = emit_T(c0, cw)
                if prev is not None:
                    vpv = emit_PV(*prev)
                prev = (c0, cw, tb, vev)
            vpv = emit_PV(*prev)
            psb_free[sb] = vpv
            ctx["vpv"] = vpv
            ctx["yb"] = yb
            return ctx

        def attn_tail(ctx):
            qc, kv, g, h, yb = ctx["qc"], ctx["kv"], ctx["g"], ctx["h"], ctx["yb"]
            vye = P.op("sc", lambda e, yb=yb, g=g: e.copy(
                ysb[g][:], y_ps[yb][:, g * 128:(g + 1) * 128]),
                w=later(ctx["vpv"], ysb_free[g]), inc="sSC")
            vz = P.op("dv", lambda e, kv=kv, g=g: e.tensor_tensor(
                zsb[:], ysb[g][:], vh_fm[:, kv * HD:(kv + 1) * HD], OP.mult),
                w=later(vye, st2["z"], ctx["vhfm"]), inc="sDV")
            vnd = P.op("pe", lambda e: e.matmul(
                miscA[0:1, 128:256], nones_sb[:], zsb[:], start=True, stop=True),
                w=later(vz, st2["mA"]), inc="sPE")
            st2["z"] = vnd
            vnde = P.op("dv", lambda e: e.tensor_copy(ndsb[:], miscA[0:1, 128:256]),
                        w=later(vnd, st2["nd"]), inc="sDV")
            vbc2 = P.op("pe", lambda e: e.matmul(
                miscA[:, 0:128], onesk1[:], ndsb[:], start=True, stop=True),
                w=[vnde], inc="sPE")
            st2["nd"] = vbc2
            vpr = P.op("dv", lambda e, kv=kv: e.tensor_tensor(
                prsb[:], vh_fm[:, kv * HD:(kv + 1) * HD], miscA[:, 0:128], OP.mult),
                w=later(vbc2, st2["pr"]), inc="sDV")
            st2["mA"] = vpr
            vyo = P.op("dv", lambda e, h=h, qc=qc, g=g: e.tensor_tensor(
                y_sb[:, h * T + qc * 128: h * T + (qc + 1) * 128],
                ysb[g][:], prsb[:], OP.add), w=[vpr], inc="sDV")
            st2["pr"] = vyo
            ysb_free[g] = vyo
            if g == 3:
                yps_free[yb] = vye
            return vyo

        icpq_num = NIC * 3
        for qc in range(NQC):
            j0 = (qc * icpq_num) // NQC
            j1 = ((qc + 1) * icpq_num) // NQC
            for j in range(j0, j1):
                ic, mat = divmod(j, 3)
                srcd, dstd = [(gt8, gbf_d), (ut8, ubf_d), (dt8, dbf_d)][mat]
                bi = j % 2
                vin = P.dma("sq", w8st[bi][:], srcd[ic], w=later(w8cv[bi]),
                            ch=f"c{10 + bi}")
                vcv = P.op("dv", lambda e, bi=bi: e.tensor_scalar(
                    wbfst[bi][:], w8st[bi][:], -128.0, None, OP.add),
                    w=later(vin, wbf_out[bi]), inc="sDV")
                w8cv[bi] = vcv
                wbf_out[bi] = P.dma("sc", dstd[ic], wbfst[bi][:], w=[vcv],
                                    ch=f"c{6 + bi}")
            vv = P.op("sc", lambda e, qc=qc: e.square(
                vsq_sb[:], v_sb[:, qc * KVLOC * HD:(qc + 1) * KVLOC * HD]),
                w=later(st2["vhat"]), inc="sSC")
            vred = None
            for kv in range(KVLOC):
                vred = P.op("dv", lambda e, kv=kv: e.reduce_sum(
                    sv_sb[:, kv:kv + 1], vsq_sb[:, kv * HD:(kv + 1) * HD], axis=AX),
                    w=later(vv, st2["svf"]) if kv == 0 else (), inc="sDV")
            st2["vhat"] = vred
            P.op("dv", lambda e: e.drain())
            vfl = P.op("dv", lambda e: e.tensor_scalar(
                sv_sb[:, KVLOC:2 * KVLOC], sv_sb[:, 0:KVLOC], 1e-24, None, OP.max),
                inc="sDV")
            vln = P.op("sc", lambda e: e.activation(
                sv_sb[:, 0:KVLOC], sv_sb[:, KVLOC:2 * KVLOC], AF.Ln),
                w=[vfl], inc="sSC")
            P.op("sc", lambda e: e.drain())
            vrv = P.op("sc", lambda e: e.activation(
                rv_sb[:], sv_sb[:, 0:KVLOC], AF.Exp, scale=-0.5), inc="sSC")
            st2["svf"] = vrv
            vhs = []
            for kv in range(KVLOC):
                vhs.append(P.op("dv", lambda e, kv=kv, qc=qc: e.tensor_scalar(
                    vh_tok[:, kv * HD:(kv + 1) * HD],
                    v_sb[:, qc * KVLOC * HD + kv * HD: qc * KVLOC * HD + (kv + 1) * HD],
                    rv_sb[:, kv:kv + 1], None, OP.mult),
                    w=later(vrv, st2["vhtok"]) if kv == 0 else (), inc="sDV"))
            vts = None
            vtb = ptrot[0] % 2
            ptrot[0] += 1
            for kv in range(KVLOC):
                vts = P.op("pe", lambda e, kv=kv, vtb=vtb: e.transpose(
                    pt_ps[vtb][:, kv * HD:(kv + 1) * HD],
                    vh_tok[:, kv * HD:(kv + 1) * HD], iden_sb[:]),
                    w=later(vhs[kv], ptps_free[vtb] if kv == 0 else None),
                    inc="sPE")
            st2["vhtok"] = vts
            vhf = P.op("dv", lambda e, vtb=vtb: e.tensor_copy(
                vh_fm[:], pt_ps[vtb][:, 0:KVLOC * HD]),
                w=later(vts, st2["vhfm_free"] if "vhfm_free" in st2 else None),
                inc="sDV")
            ptps_free[vtb] = vhf
            ctxs = []
            for kv in range(KVLOC):
                for g in range(4):
                    c = attn_head(qc, kv, g)
                    c["vhfm"] = vhf
                    ctxs.append(c)
                    if len(ctxs) >= 3:
                        attn_ptpv(ctxs[-3])
            attn_ptpv(ctxs[-2])
            attn_ptpv(ctxs[-1])
            for c in ctxs:
                attn_tail(c)
            st2["vhfm_free"] = st2["pr"]
        es_pb.close()
        es_p2.close()
        es_kv.close()
        P.gate()

        es_p3 = ExitStack()
        wp_bf = es_p3.enter_context(SB("wp_bf", [128, HLOC * D], BF16))
        wpst = [es_p3.enter_context(SB(f"wpst{i}", [128, D])) for i in range(2)]
        lr_sb = es_p3.enter_context(SB("lr_sb", [RANK, T], BF16))
        cst = [es_p3.enter_context(SB(f"cst{i}", [128, 512], BF16)) for i in range(2)]
        es_pc = ExitStack()
        c_ps = [es_pc.enter_context(PS(f"c_ps{i}", [128, 512])) for i in range(2)]
        lr_ps = es_pc.enter_context(PS("lr_ps", [RANK, 512]))

        wpst_free = [None, None]
        for fc in range(HLOC):
            buf = fc % 2
            v = P.dma("sq", wpst[buf][:], wp[fc], w=later(wpst_free[buf]),
                      ch=f"c{buf}")
            wpst_free[buf] = P.op(
                "dv", lambda e, fc=fc, buf=buf: e.tensor_copy(
                    wp_bf[:, fc * D:(fc + 1) * D], wpst[buf][:]), w=[v], inc="sDV")
        v_wp = ("sDV", P.cnt["sDV"])
        lr_free = None
        for tp in range(T // 512):
            for fc in range(HLOC):
                vpe = P.op("pe", lambda e, tp=tp, fc=fc: e.matmul(
                    lr_ps[:], pa_bf[:, fc * RANK:(fc + 1) * RANK],
                    y_sb[:, fc * T + tp * 512: fc * T + (tp + 1) * 512],
                    start=(fc == 0), stop=(fc == HLOC - 1)),
                    w=later(lr_free) if fc == 0 else (), inc="sPE")
            lr_free = P.op("sc", lambda e, tp=tp: e.copy(
                lr_sb[:, tp * 512:(tp + 1) * 512], lr_ps[:]), w=[vpe], inc="sSC")
        v_lr = lr_free
        cps_free = [None, None]
        cst_free = [None, None]
        vcc = None
        ntcf = T // 128
        tpq = ntcf // 8
        for q in range(4):
            for tc in [q * tpq + i for i in range(tpq)] + \
                      [ntcf // 2 + q * tpq + i for i in range(tpq)]:
                if tc < ntcf // 2:
                    r0 = q * tpq * 256 + (tc - q * tpq) * 128
                else:
                    r0 = q * tpq * 256 + tpq * 128 + (tc - ntcf // 2 - q * tpq) * 128
                for fog in range(4):
                    buf = fog % 2
                    for fc in range(HLOC):
                        P.op("pe", lambda e, tc=tc, fog=fog, fc=fc, buf=buf: e.matmul(
                            c_ps[buf][:], y_sb[:, fc * T + tc * 128: fc * T + (tc + 1) * 128],
                            wp_bf[:, fc * D + fog * 512: fc * D + (fog + 1) * 512],
                            start=(fc == 0), stop=False),
                            w=later(cps_free[buf], v_wp if tc == 0 else None,
                                    v_lr if tc == 0 else None) if fc == 0 else (),
                            inc="sPE")
                    vpe = P.op("pe", lambda e, tc=tc, fog=fog, buf=buf: e.matmul(
                        c_ps[buf][:], lr_sb[:, tc * 128:(tc + 1) * 128],
                        pb_bf[:, fog * 512:(fog + 1) * 512],
                        start=False, stop=True), inc="sPE")
                    vev = P.op("sc", lambda e, buf=buf: e.copy(cst[buf][:], c_ps[buf][:]),
                               w=later(vpe, cst_free[buf]), inc="sSC")
                    cps_free[buf] = vev
                    cst_free[buf] = P.dma(
                        "sc", cc_in[r0:r0 + 128, fog * 512:(fog + 1) * 512],
                        cst[buf][:], w=[vev], ch=f"c{4 + buf}")
            v_allst = [("c4", P.cnt["c4"]), ("c5", P.cnt["c5"])]
            vcc = P.op("gp", lambda e, q=q: e.collective_compute(
                "ReduceScatter", mybir.AluOpType.add,
                replica_groups=[[0, 1], [2, 3], [4, 5], [6, 7]],
                ins=[cc_in[q * (T // 4):(q + 1) * (T // 4), :].opt()],
                outs=[cc_out[q * (T // 8):(q + 1) * (T // 8), :].opt()]),
                w=v_allst, inc="sCC")
        es_pc.close()
        es_p3.close()
        es_qy.close()
        P.gate()

        es_p4 = ExitStack()
        h_fm = es0.enter_context(SB("h_fm", [128, 16 * To], BF16))
        att_t = [es_p4.enter_context(SB(f"att{i}", [128, D], BF16)) for i in range(2)]
        xo_t = [es_p4.enter_context(SB(f"xo{i}", [128, D])) for i in range(2)]
        r1_t = es_p4.enter_context(SB("r1", [128, D]))
        htm = es_p4.enter_context(SB("htm", [128, D], BF16))
        ln1 = LN(P, nc, es_p4, "ln1", eps_sb[:, 1:2])
        es_pd = ExitStack()
        btr_ps = [es_pd.enter_context(PS(f"btr{i}", [128, 1024], BF16)) for i in range(2)]

        att_free = [None, None]
        xo_free = [None, None]
        r1_free = None
        htm_free = None
        btr_free = [None, None]
        for tc in range(NTC):
            buf = tc % 2
            va = P.dma("sq", att_t[buf][:], cc_out[tc * 128:(tc + 1) * 128, :],
                       w=later(vcc if tc < 2 else None, att_free[buf]),
                       ch=f"c{buf}")
            vx = P.dma("sq", xo_t[buf][:], x_own[tc * 128:(tc + 1) * 128, :],
                       w=later(xo_free[buf]), ch=f"c{2 + buf}")
            vr = P.op("dv", lambda e, buf=buf: e.tensor_tensor(
                r1_t[:], att_t[buf][:], xo_t[buf][:], OP.add),
                w=later(va, vx, r1_free), inc="sDV")
            att_free[buf] = vr
            xo_free[buf] = vr
            vh, r1_readers = ln1.norm(r1_t[:], htm[:], [vr], [htm_free], D)
            r1_free = r1_readers
            vtp_last = None
            for half in range(2):
                tb = half
                vtp = None
                for fc8 in range(8):
                    fc = half * 8 + fc8
                    vtp = P.op("pe", lambda e, fc=fc, fc8=fc8, tb=tb: e.transpose(
                        btr_ps[tb][:, fc8 * 128:(fc8 + 1) * 128],
                        htm[:, fc * 128:(fc + 1) * 128], iden_sb[:]),
                        w=later(vh if fc8 == 0 else None,
                                btr_free[tb] if fc8 == 0 else None), inc="sPE")
                vtp_last = vtp
                evs = []
                for fc8 in range(8):
                    fc = half * 8 + fc8
                    evs.append(P.op("dv", lambda e, fc=fc, fc8=fc8, tb=tb, tc=tc:
                                    e.tensor_copy(
                                        h_fm[:, fc * To + tc * 128: fc * To + (tc + 1) * 128],
                                        btr_ps[tb][:, fc8 * 128:(fc8 + 1) * 128]),
                                    w=[vtp] if fc8 == 0 else (), inc="sDV"))
                btr_free[tb] = evs[-1]
            htm_free = ("sPE", vtp_last[1])
        es_pd.close()
        es_p4.close()
        P.gate()

        es_p5 = ExitStack()
        acc = es_p5.enter_context(SB("acc", [128, FTC * D]))
        dw_bf = es_p5.enter_context(SB("dw_bf", [128, 8 * D], BF16))
        hmid = es_p5.enter_context(SB("hmid", [128, 8 * FP], BF16))
        gw_bf = [es_p5.enter_context(SB(f"gw_bf{i}", [128, D], BF16)) for i in range(3)]
        uw_bf = [es_p5.enter_context(SB(f"uw_bf{i}", [128, D], BF16)) for i in range(3)]
        silu_sb = es_p5.enter_context(SB("silu_sb", [128, FP], BF16))
        dsc_sb = es_p5.enter_context(SB("dsc_sb", [128, D]))
        htm2 = es_p5.enter_context(SB("htm2", [128, D], BF16))
        r2_t = es_p5.enter_context(SB("r2", [128, D]))
        outst = [es_p5.enter_context(SB(f"outst{i}", [128, D])) for i in range(2)]
        ln2 = LN(P, nc, es_p5, "ln2", eps_sb[:, 1:2], alloc_sq=False)
        es_pe_ = ExitStack()
        g_ps = es_pe_.enter_context(PS("g_ps", [128, FP]))
        u_ps = es_pe_.enter_context(PS("u_ps", [128, FP]))
        d_ps = [es_pe_.enter_context(PS(f"d_ps{i}", [128, 512])) for i in range(2)]
        bt2_ps = [es_pe_.enter_context(PS(f"bt2{i}", [128, 1024], BF16)) for i in range(2)]

        v_dsc = P.dma("sq", dsc_sb[:], dscT, ch="c6")
        gwf = [None, None, None]
        uwf = [None, None, None]
        dw_cvt = [None] * 8
        dw_used = None
        gpsf = None
        upsf = None
        dpsf = [None, None]
        hm_cvt = [None] * 8
        hm_used = None
        siluf = None
        accf = [None] * (FTC * 4)
        bt2f = [None, None]
        htm2f = None
        r2f = None
        outf = [None, None]

        for p in range(NFP):
            tp0 = p * FP
            for o in range(NOCT):
                for i8 in range(8):
                    ic = o * 8 + i8
                    bi = ic % 3
                    vgc = P.dma("sq", gw_bf[bi][:], gbf_d[ic], w=later(gwf[bi]),
                                ch=f"c{bi}")
                    vuc = P.dma("sc", uw_bf[bi][:], ubf_d[ic], w=later(uwf[bi]),
                                ch=f"c{8 + bi}")
                    vdc = P.dma("sc", dw_bf[:, i8 * D:(i8 + 1) * D], dbf_d[ic],
                                w=later(dw_used if i8 == 0 else None),
                                ch=f"c{4 + (i8 % 2)}")
                    dw_cvt[i8] = vdc
                    vgp = None
                    for fc in range(16):
                        vgp = P.op("pe", lambda e, fc=fc, bi=bi, tp0=tp0: e.matmul(
                            g_ps[:], gw_bf[bi][:, fc * 128:(fc + 1) * 128],
                            h_fm[:, fc * To + tp0: fc * To + tp0 + FP],
                            start=(fc == 0), stop=(fc == 15)),
                            w=later(vgc, gpsf) if fc == 0 else (), inc="sPE")
                    vup = None
                    for fc in range(16):
                        vup = P.op("pe", lambda e, fc=fc, bi=bi, tp0=tp0: e.matmul(
                            u_ps[:], uw_bf[bi][:, fc * 128:(fc + 1) * 128],
                            h_fm[:, fc * To + tp0: fc * To + tp0 + FP],
                            start=(fc == 0), stop=(fc == 15)),
                            w=later(vuc, upsf) if fc == 0 else (), inc="sPE")
                    gwf[bi] = vgp
                    uwf[bi] = vup
                    vsl = P.op("sc", lambda e, ic=ic: e.activation(
                        silu_sb[:], g_ps[:], AF.Silu, scale=gsc_sb[:, ic:ic + 1]),
                        w=later(vgp, siluf), inc="sSC")
                    gpsf = vsl
                    vhm = P.op("dv", lambda e, ic=ic, i8=i8: e.scalar_tensor_tensor(
                        hmid[:, i8 * FP:(i8 + 1) * FP], u_ps[:], usc_sb[:, ic:ic + 1],
                        silu_sb[:], OP.mult, OP.mult),
                        w=later(vup, vsl, hm_used if i8 == 0 else None), inc="sDV")
                    upsf = vhm
                    siluf = vhm
                    hm_cvt[i8] = vhm
                hm_max = ("sDV", max(v[1] for v in hm_cvt))
                dw_max4 = max(v[1] for v in dw_cvt if v[0] == "c4")
                dw_max5 = max((v[1] for v in dw_cvt if v[0] == "c5"), default=0)
                dw_max = ("c4", dw_max4)
                dw_max_b = ("c5", dw_max5) if dw_max5 else None
                vdw_pe = None
                for tc in range(FTC):
                    for fg in range(4):
                        db = (tc * 4 + fg) % 2
                        vdp = None
                        for i8 in range(8):
                            vdp = P.op("pe", lambda e, i8=i8, tc=tc, fg=fg, db=db:
                                       e.matmul(
                                           d_ps[db][:],
                                           hmid[:, i8 * FP + tc * 128: i8 * FP + (tc + 1) * 128],
                                           dw_bf[:, i8 * D + fg * 512: i8 * D + (fg + 1) * 512],
                                           start=(i8 == 0), stop=(i8 == 7)),
                                       w=later(dpsf[db], hm_max, dw_max, dw_max_b)
                                       if i8 == 0 else (), inc="sPE")
                        vdw_pe = vdp
                        ai = tc * 4 + fg
                        if o == 0:
                            vac = P.op("dv", lambda e, tc=tc, fg=fg, db=db: e.tensor_scalar(
                                acc[:, tc * D + fg * 512: tc * D + (fg + 1) * 512],
                                d_ps[db][:], 1.0, None, OP.mult),
                                w=later(vdp, accf[ai]), inc="sDV")
                        else:
                            vac = P.op("dv", lambda e, tc=tc, fg=fg, db=db:
                                       e.scalar_tensor_tensor(
                                           acc[:, tc * D + fg * 512: tc * D + (fg + 1) * 512],
                                           d_ps[db][:], 1.0,
                                           acc[:, tc * D + fg * 512: tc * D + (fg + 1) * 512],
                                           OP.mult, OP.add),
                                       w=later(vdp, accf[ai]), inc="sDV")
                        dpsf[db] = vac
                        accf[ai] = vac
                dw_used = ("sPE", vdw_pe[1])
                hm_used = dw_used
            for tc in range(FTC):
                tb = tc % 2
                vht = None
                vtp_last = None
                for half in range(2):
                    vtp = None
                    for fc8 in range(8):
                        fc = half * 8 + fc8
                        vtp = P.op("pe", lambda e, fc=fc, fc8=fc8, tb=tb, tc=tc, tp0=tp0:
                                   e.transpose(
                                       bt2_ps[tb][:, fc8 * 128:(fc8 + 1) * 128],
                                       h_fm[:, fc * To + tp0 + tc * 128: fc * To + tp0 + (tc + 1) * 128],
                                       iden_sb[:]),
                                   w=later(bt2f[tb] if fc8 == 0 else None,
                                           htm2f if (half == 0 and fc8 == 0) else None),
                                   inc="sPE")
                    vtp_last = vtp
                    evs = []
                    for fc8 in range(8):
                        fc = half * 8 + fc8
                        evs.append(P.op("dv", lambda e, fc=fc, fc8=fc8, tb=tb:
                                        e.tensor_copy(
                                            htm2[:, fc * 128:(fc + 1) * 128],
                                            bt2_ps[tb][:, fc8 * 128:(fc8 + 1) * 128]),
                                        w=[vtp] if fc8 == 0 else (), inc="sDV"))
                    bt2f[tb] = evs[-1]
                    vht = evs[-1]
                ai0 = tc * 4
                vr2a = P.op("dv", lambda e, tc=tc: e.tensor_tensor(
                    r2_t[:], acc[:, tc * D:(tc + 1) * D], dsc_sb[:], OP.mult),
                    w=later(accf[ai0], accf[ai0 + 1], accf[ai0 + 2], accf[ai0 + 3],
                            r2f, v_dsc), inc="sDV")
                vr2 = P.op("gp", lambda e: e.tensor_tensor(
                    r2_t[:], r2_t[:], htm2[:], OP.add), w=later(vr2a, vht), inc="sGP")
                htm2f = vr2
                ob = tc % 2
                vout, r2_readers = ln2.norm(r2_t[:], outst[ob][:], [vr2],
                                            [outf[ob]], D, sq_ap=outst[ob][:])
                r2f = r2_readers + [vr2]
                outf[ob] = P.dma("sc", out_d[tp0 + tc * 128: tp0 + (tc + 1) * 128, :],
                                 outst[ob][:], w=[vout], ch=f"c{6 + ob}")
                for i in range(4):
                    accf[ai0 + i] = vr2a
        es_pe_.close()
        es_p5.close()
        P.op("sc", lambda e: e.nop(),
             w=[("sDs", P.cnt["sDs"]), ("c6", P.cnt["c6"]), ("c7", P.cnt["c7"])])

        block = es0.enter_context(nc.Block())
        P.emit(block, sems)
    return nc


def _u8_low_byte(w_int32):
    a = np.ascontiguousarray(w_int32)
    assert a.dtype == np.int32
    return a.view(np.uint8).reshape(a.shape + (4,))[..., 0]


def prep_core(c, T, I, x, Wq, Wk, Wv, Wproj, q_gain, q_A, q_B, k_A, k_B,
              v_A, v_B, p_A, p_B, gate_w, gate_scale, up_w, up_scale,
              down_w, down_scale):
    b, s = c // 2, c % 2
    To = T // 2
    NIC = I // 128
    f0 = 1024 * s
    h0 = 8 * s
    kv0 = 256 * s
    bf = ml_dtypes.bfloat16
    tile16 = lambda a: np.ascontiguousarray(a).reshape(16, 128, a.shape[1])
    m = {}
    m["xT"] = np.ascontiguousarray(x[b].T)
    m["x_own"] = np.ascontiguousarray(x[b, s * To:(s + 1) * To, :])
    m["wq"] = tile16(Wq[:, f0:f0 + 1024])
    m["wk"] = tile16(Wk[:, kv0:kv0 + 256])
    m["wv"] = tile16(Wv[:, kv0:kv0 + 256])
    m["wp"] = np.ascontiguousarray(Wproj[f0:f0 + 1024, :]).reshape(8, 128, D)
    m["qa"] = tile16(q_A)
    m["ka"] = tile16(k_A)
    m["va"] = tile16(v_A)
    m["pa"] = np.ascontiguousarray(p_A[f0:f0 + 1024, :]).reshape(8, 128, RANK)
    m["qb"] = np.ascontiguousarray(q_B[:, f0:f0 + 1024])
    m["kb"] = np.ascontiguousarray(k_B[:, kv0:kv0 + 256])
    m["vb"] = np.ascontiguousarray(v_B[:, kv0:kv0 + 256])
    m["pb"] = np.ascontiguousarray(p_B)
    m["qg"] = np.ascontiguousarray(q_gain[h0:h0 + 8]).reshape(1, 8)
    inv = 10000.0 ** (-(np.arange(32, dtype=np.float64)) / 32.0)
    ang = np.arange(T, dtype=np.float64)[:, None] * inv[None, :]
    cosT, sinT = np.cos(ang).T, np.sin(ang).T
    m["rope"] = np.concatenate([cosT, sinT], 0).astype(np.float32).astype(bf)
    m["ropeb"] = np.concatenate([sinT, cosT], 0).astype(np.float32).astype(bf)
    mk = np.zeros((128, 128), np.float32)
    mk[np.triu_indices(128, 1)] = -30000.0
    m["maskt"] = mk
    m["iden"] = np.eye(128, dtype=np.float32).astype(bf)
    gT = np.ascontiguousarray(_u8_low_byte(gate_w).T)
    uT = np.ascontiguousarray(_u8_low_byte(up_w).T)
    dT = np.ascontiguousarray(_u8_low_byte(down_w).T)
    m["gt8"] = np.ascontiguousarray(
        gT.reshape(16, 128, NIC, 128).transpose(2, 1, 0, 3).reshape(NIC, 128, D))
    m["ut8"] = np.ascontiguousarray(
        uT.reshape(16, 128, NIC, 128).transpose(2, 1, 0, 3).reshape(NIC, 128, D))
    m["dt8"] = np.ascontiguousarray(dT.reshape(NIC, 128, D))
    m["gsc"] = np.ascontiguousarray(gate_scale.reshape(NIC, 128).T)
    m["usc"] = np.ascontiguousarray(up_scale.reshape(NIC, 128).T)
    m["dscT"] = np.ascontiguousarray(np.broadcast_to(down_scale[None, :], (128, D)))
    ep = np.empty((128, 2), np.float32)
    ep[:, 0] = RMS_EPS
    ep[:, 1] = LN_EPS
    m["epst"] = ep
    return m


def run(x, ws, T, I, core_ids=None):
    in_maps = [prep_core(c, T, I, x, *ws) for c in range(8)]
    nc = build(T, I)
    res = run_bass_kernel_spmd(nc, in_maps, core_ids=list(range(8)))
    out = np.empty((4, T, D), np.float32)
    To = T // 2
    for c in range(8):
        b, s = c // 2, c % 2
        out[b, s * To:(s + 1) * To, :] = res.results[c]["out"]
    return out


def kernel(L0_attn_x, L0_attn_Wq, L0_attn_Wk, L0_attn_Wv, L0_attn_Wproj,
           L0_attn_q_gain, L0_attn_q_A, L0_attn_q_B, L0_attn_k_A, L0_attn_k_B,
           L0_attn_v_A, L0_attn_v_B, L0_attn_p_A, L0_attn_p_B,
           L0_ffn_gate_w, L0_ffn_gate_scale, L0_ffn_up_w, L0_ffn_up_scale,
           L0_ffn_down_w, L0_ffn_down_scale):
    x = np.asarray(L0_attn_x, dtype=np.float32)
    ws = (np.asarray(L0_attn_Wq, np.float32), np.asarray(L0_attn_Wk, np.float32),
          np.asarray(L0_attn_Wv, np.float32), np.asarray(L0_attn_Wproj, np.float32),
          np.asarray(L0_attn_q_gain, np.float32), np.asarray(L0_attn_q_A, np.float32),
          np.asarray(L0_attn_q_B, np.float32), np.asarray(L0_attn_k_A, np.float32),
          np.asarray(L0_attn_k_B, np.float32), np.asarray(L0_attn_v_A, np.float32),
          np.asarray(L0_attn_v_B, np.float32), np.asarray(L0_attn_p_A, np.float32),
          np.asarray(L0_attn_p_B, np.float32),
          np.asarray(L0_ffn_gate_w, np.int32), np.asarray(L0_ffn_gate_scale, np.float32),
          np.asarray(L0_ffn_up_w, np.int32), np.asarray(L0_ffn_up_scale, np.float32),
          np.asarray(L0_ffn_down_w, np.int32), np.asarray(L0_ffn_down_scale, np.float32))
    return run(x, ws, 4096, 8192)


# version 32 (final)
# speedup vs baseline: 1.0892x; 1.0044x over previous
import numpy as np
import ml_dtypes
import concourse.bass as bass
import concourse.mybir as mybir
from concourse.bass_utils import run_bass_kernel_spmd
from concourse.alu_op_type import AluOpType
from contextlib import ExitStack

F32 = mybir.dt.float32
BF16 = mybir.dt.bfloat16
U8 = mybir.dt.uint8
AF = mybir.ActivationFunctionType
AX = mybir.AxisListType.X
OP = AluOpType

D = 2048
HD = 128
HLOC = 8
KVLOC = 2
RANK = 16
RMS_EPS = 1.1920929e-07
LN_EPS = 1e-5
INV_SQRT_HD = float(HD) ** -0.5

SEMS = (["sDq", "sDs", "sDg", "sPE", "sSC", "sDV", "sGP", "sCC", "sPH"]
        + [f"c{i}" for i in range(12)])
ENGS = ["sq", "sc", "dv", "gp", "pe"]


def later(*evs):
    out = []
    for e in evs:
        if e is None:
            continue
        if isinstance(e, list):
            out.extend(x for x in e if x is not None)
        else:
            out.append(e)
    return out


class Planner:
    def __init__(self):
        self.recs = {e: [] for e in ENGS}
        self.cnt = {s: 0 for s in SEMS}
        self.gate_no = 0
        self.last_gate = {e: 0 for e in ENGS}

    def op(self, eng, fn, w=(), inc=None, n=1):
        waits = [x for x in w if x is not None and x[1] > 0]
        if self.last_gate[eng] < self.gate_no:
            waits.insert(0, ("sPH", self.gate_no))
            self.last_gate[eng] = self.gate_no
        self.recs[eng].append((fn, waits, inc, n))
        if inc is not None:
            self.cnt[inc] += n
            return (inc, self.cnt[inc])
        return None

    def dma(self, q, dst, src, w=(), ch=None):
        sem = ch or {"sq": "sDq", "sc": "sDs", "gp": "sDg", "dv": "sDg"}[q]
        return self.op(q, lambda e, d=dst, s=src: e.dma_start(out=d, in_=s),
                       w=w, inc=sem, n=16)

    def gate(self):
        waits = [(s, self.cnt[s]) for s in SEMS if s != "sPH" and self.cnt[s] > 0]
        self.recs["gp"].append((lambda e: e.nop(), waits, "sPH", 1))
        self.cnt["sPH"] += 1
        self.gate_no = self.cnt["sPH"]
        self.last_gate["gp"] = self.gate_no

    def emit(self, block, sems):
        eng_of = {"sq": block.sync, "sc": block.scalar, "dv": block.vector,
                  "gp": block.gpsimd, "pe": block.tensor}
        for name in ENGS:
            recs = self.recs[name]

            def make(recs=recs):
                def body(e):
                    for fn, waits, inc, n in recs:
                        for s, v in waits:
                            e.wait_ge(sems[s], v)
                        inst = fn(e)
                        if inc is not None:
                            inst.then_inc(sems[inc], n)
                return body
            eng_of[name](make())


class LN:

    def __init__(self, P, nc, es, tag, eps_ap, alloc_sq=True):
        self.P = P
        self.eps = eps_ap
        self.stm = es.enter_context(nc.sbuf_tensor(f"stm_{tag}", [128, 4], F32))
        self.sq = (es.enter_context(nc.sbuf_tensor(f"sq_{tag}", [128, D], F32))
                   if alloc_sq else None)
        self.free = None

    def norm(self, src_ap, out_ap, w_src, w_out, ND, sq_ap=None):
        P, stm = self.P, self.stm
        sq = sq_ap if sq_ap is not None else self.sq[:]
        v1 = P.op("dv", lambda e: e.reduce_sum(stm[:, 0:1], src_ap, axis=AX),
                  w=later(*w_src, self.free), inc="sDV")
        v2 = P.op("sc", lambda e: e.square(sq, src_ap),
                  w=later(*w_src, self.free, *w_out), inc="sSC")
        v3 = P.op("dv", lambda e: e.reduce_sum(stm[:, 1:2], sq, axis=AX),
                  w=[v2], inc="sDV")
        P.op("dv", lambda e: e.drain())
        P.op("dv", lambda e: e.tensor_scalar(stm[:, 2:3], stm[:, 0:1],
                                             1.0 / ND, None, OP.mult))
        P.op("dv", lambda e: e.tensor_scalar(stm[:, 1:2], stm[:, 1:2],
                                             1.0 / ND, None, OP.mult))
        P.op("dv", lambda e: e.drain())
        P.op("dv", lambda e: e.tensor_tensor(stm[:, 0:1], stm[:, 2:3],
                                             stm[:, 2:3], OP.mult))
        P.op("dv", lambda e: e.drain())
        v5 = P.op("dv", lambda e: e.tensor_tensor(stm[:, 1:2], stm[:, 1:2],
                                                  stm[:, 0:1], OP.subtract), inc="sDV")
        v6 = P.op("sc", lambda e: e.activation(stm[:, 0:1], stm[:, 1:2],
                                               AF.Ln, bias=self.eps), w=[v5], inc="sSC")
        P.op("sc", lambda e: e.drain())
        v7 = P.op("sc", lambda e: e.activation(stm[:, 3:4], stm[:, 0:1],
                                               AF.Exp, scale=-0.5), inc="sSC")
        v8 = P.op("dv", lambda e, o=out_ap: e.tensor_scalar(
            o, src_ap, stm[:, 2:3], stm[:, 3:4], OP.subtract, OP.mult),
            w=later(v7, *w_out), inc="sDV")
        self.free = [v8, v7]
        return v8, [v8, v2]


def build(T, I):
    To = T // 2
    NQC = T // 128
    NPASS = T // 512
    NIC = I // 128
    NOCT = NIC // 8
    FP = 512
    NFP = To // FP
    FTC = FP // 128
    NTC = To // 128

    nc = bass.Bass("TRN2", target_bir_lowering=False, debug=False)
    din = lambda nm, sh, d=F32: nc.dram_tensor(nm, sh, d, kind="ExternalInput").ap()
    xT = din("xT", [D, T])
    x_own = din("x_own", [To, D])
    wq = din("wq", [16, 128, HLOC * HD])
    wk = din("wk", [16, 128, KVLOC * HD])
    wv = din("wv", [16, 128, KVLOC * HD])
    wp = din("wp", [HLOC, 128, D])
    qa = din("qa", [16, 128, RANK])
    ka = din("ka", [16, 128, RANK])
    va = din("va", [16, 128, RANK])
    pa = din("pa", [HLOC, 128, RANK])
    qb = din("qb", [RANK, HLOC * HD])
    kb = din("kb", [RANK, KVLOC * HD])
    vb = din("vb", [RANK, KVLOC * HD])
    pb = din("pb", [RANK, D])
    qg = din("qg", [1, HLOC])
    rope = din("rope", [64, T], BF16)
    ropeb = din("ropeb", [64, T], BF16)
    maskt = din("maskt", [128, 128])
    iden = din("iden", [128, 128], BF16)
    gt8 = din("gt8", [NIC, 128, D], U8)
    ut8 = din("ut8", [NIC, 128, D], U8)
    dt8 = din("dt8", [NIC, 128, D], U8)
    gsc = din("gsc", [128, NIC])
    usc = din("usc", [128, NIC])
    dscT = din("dscT", [128, D])
    epst = din("epst", [128, 2])
    out_d = nc.dram_tensor("out", [To, D], F32, kind="ExternalOutput").ap()
    mq_dram = nc.dram_tensor("mq_dram", [HLOC * NQC, 128], BF16).ap()
    cc_in = nc.dram_tensor("cc_in", [T, D], BF16)
    gbf_d = nc.dram_tensor("gbf_d", [NIC, 128, D], BF16).ap()
    ubf_d = nc.dram_tensor("ubf_d", [NIC, 128, D], BF16).ap()
    dbf_d = nc.dram_tensor("dbf_d", [NIC, 128, D], BF16).ap()
    cc_out = nc.dram_tensor("cc_out", [To, D], BF16)

    P = Planner()
    es0 = ExitStack()
    with es0:
        sems = {s: es0.enter_context(nc.semaphore(s)) for s in SEMS}
        SB = lambda nm, sh, d=F32: nc.sbuf_tensor(nm, sh, d)
        PS = lambda nm, sh, d=F32: nc.psum_tensor(nm, sh, d)

        eps_sb = es0.enter_context(SB("eps_sb", [128, 2]))
        iden_sb = es0.enter_context(SB("iden_sb", [128, 128], BF16))
        mask_sb = es0.enter_context(SB("mask_sb", [128, 128]))
        qg_sb = es0.enter_context(SB("qg_sb", [1, HLOC]))
        gsc_sb = es0.enter_context(SB("gsc_sb", [128, NIC]))
        usc_sb = es0.enter_context(SB("usc_sb", [128, NIC]))
        ones_sb = es0.enter_context(SB("ones_sb", [128, 1], BF16))
        nones_sb = es0.enter_context(SB("nones_sb", [128, 1], BF16))
        onesk1 = es0.enter_context(SB("onesk1", [1, 128], BF16))
        pa_bf = es0.enter_context(SB("pa_bf", [128, HLOC * RANK], BF16))
        pb_bf = es0.enter_context(SB("pb_bf", [RANK, D], BF16))

        es_qy = ExitStack()
        q_sb = es_qy.enter_context(SB("q_sb", [128, HLOC * T], BF16))
        y_sb = q_sb
        es_kv = ExitStack()
        k_sb = es_kv.enter_context(SB("k_sb", [128, KVLOC * T], BF16))
        v_sb = es_kv.enter_context(SB("v_sb", [128, (T // 128) * KVLOC * HD], BF16))

        es_p1 = ExitStack()
        ropeA_t = es_p1.enter_context(SB("ropeA_t", [64, 512], BF16))
        ropeB_t = es_p1.enter_context(SB("ropeB_t", [64, 512], BF16))
        qa_bf = es_p1.enter_context(SB("qa_bf", [128, 16 * RANK], BF16))
        ka_bf = es_p1.enter_context(SB("ka_bf", [128, 16 * RANK], BF16))
        va_bf = es_p1.enter_context(SB("va_bf", [128, 16 * RANK], BF16))
        qb_bf = es_p1.enter_context(SB("qb_bf", [RANK, HLOC * HD], BF16))
        kb_bf = es_p1.enter_context(SB("kb_bf", [RANK, KVLOC * HD], BF16))
        vb_bf = es_p1.enter_context(SB("vb_bf", [RANK, KVLOC * HD], BF16))
        xstage = [es_p1.enter_context(SB(f"xstage{i}", [128, 1024])) for i in range(2)]
        wq_bf = es_p1.enter_context(SB("wq_bf", [128, 16 * HLOC * HD], BF16))
        wk_bf = es_p1.enter_context(SB("wk_bf", [128, 16 * KVLOC * HD], BF16))
        wv_bf = es_p1.enter_context(SB("wv_bf", [128, 16 * KVLOC * HD], BF16))

        for dpair in [(eps_sb[:], epst), (mask_sb[:], maskt), (qg_sb[:], qg),
                      (gsc_sb[:], gsc), (usc_sb[:], usc),
                      (iden_sb[:], iden)]:
            P.dma("sq", dpair[0], dpair[1])
        P.op("dv", lambda e: e.memset(ones_sb[:], 1.0))
        P.op("dv", lambda e: e.memset(nones_sb[:], -1.0))
        P.op("dv", lambda e: e.memset(onesk1[:], 1.0), inc="sDV")

        jobs = []
        for fc in range(16):
            jobs.append((wq[fc], 128, HLOC * HD, wq_bf, fc * HLOC * HD))
        for i in range(4):
            jobs.append((wk.rearrange("c p n -> p c n")[:, i * 4:(i + 1) * 4, :],
                         128, 4 * KVLOC * HD, wk_bf, i * 4 * KVLOC * HD))
            jobs.append((wv.rearrange("c p n -> p c n")[:, i * 4:(i + 1) * 4, :],
                         128, 4 * KVLOC * HD, wv_bf, i * 4 * KVLOC * HD))
        jobs += [(qa.rearrange("c p n -> p c n"), 128, 16 * RANK, qa_bf, 0),
                 (ka.rearrange("c p n -> p c n"), 128, 16 * RANK, ka_bf, 0),
                 (va.rearrange("c p n -> p c n"), 128, 16 * RANK, va_bf, 0),
                 (pa.rearrange("c p n -> p c n"), 128, HLOC * RANK, pa_bf, 0),
                 (qb, RANK, HLOC * HD, qb_bf, 0),
                 (kb, RANK, KVLOC * HD, kb_bf, 0),
                 (vb, RANK, KVLOC * HD, vb_bf, 0),
                 (pb[:, 0:1024], RANK, 1024, pb_bf, 0),
                 (pb[:, 1024:2048], RANK, 1024, pb_bf, 1024)]
        xst_ev = [None, None]
        for j, (src, rows, cols, dst, off) in enumerate(jobs):
            buf = j % 2
            st = xstage[buf]
            if len(src.shape) == 3:
                d_ap = st[:, 0:cols].rearrange("p (c n) -> p c n", c=src.shape[1])
            else:
                d_ap = st[0:rows, 0:cols]
            v = P.dma("sq", d_ap, src, w=later(xst_ev[buf]), ch=f"c{buf}")
            xst_ev[buf] = P.op(
                "dv", lambda e, a=st[0:rows, 0:cols], b=dst[0:rows, off:off + cols]:
                e.tensor_copy(b, a), w=[v], inc="sDV")
        P.gate()

        x_fm = es_p1.enter_context(SB("x_fm", [128, 16 * 512], BF16))
        qsq = [es_p1.enter_context(SB(f"qsq{i}", [128, 512], BF16)) for i in range(2)]
        stat1 = es_p1.enter_context(SB("stat1", [1, 512]))
        stat2 = es_p1.enter_context(SB("stat2", [1, 512]))
        mqrow = es_p1.enter_context(SB("mqrow", [1, 512], BF16))
        mkbr = es_p1.enter_context(SB("mkbr", [1, 512], BF16))
        bcev = es_p1.enter_context(SB("bcev", [128, 512], BF16))
        qn_sb = [es_p1.enter_context(SB(f"qn{i}", [128, 512], BF16)) for i in range(2)]
        rs1 = es_p1.enter_context(SB("rs1", [64, 512], BF16))
        rs2 = es_p1.enter_context(SB("rs2", [64, 512], BF16))
        rsa = es_p1.enter_context(SB("rsa", [32, 512], BF16))
        rsb = es_p1.enter_context(SB("rsb", [32, 512], BF16))
        t1q = es_p1.enter_context(SB("t1q", [RANK, 512], BF16))
        t1k = es_p1.enter_context(SB("t1k", [RANK, 512], BF16))
        t1v = es_p1.enter_context(SB("t1v", [RANK, 512], BF16))
        es_pa = ExitStack()
        qk_ps = [es_pa.enter_context(PS(f"qk_ps{i}", [128, 512])) for i in range(3)]
        t1_ps = es_pa.enter_context(PS("t1_ps", [RANK, 512]))
        sq_ps = [es_pa.enter_context(PS("sq_ps0", [1, 512]))]
        bc_ps = es_pa.enter_context(PS("bc_ps", [128, 512]))
        v_ps = [es_pa.enter_context(PS(f"v_ps{i}", [128, 512])) for i in range(2)]

        xfm_rd = None
        qkfree = [None, None, None]
        t1free = None
        sqfree = [None]
        bcfree = None
        vpsfree = [None, None]
        qsqfree = [None, None]
        qnfree = [None, None]
        stat2_free = None
        mqrow_free = [None]

        def rms_stats(pi, seg, wpe):
            nonlocal stat2_free
            qb_ = pi % 2
            vsc = P.op("sc", lambda e, p=qk_ps[pi], q=qsq[qb_]: e.square(q[:], p[:]),
                       w=later(wpe, qsqfree[qb_]), inc="sSC")
            vpe = P.op("pe", lambda e, sp=sq_ps[seg], q=qsq[qb_]: e.matmul(
                sp[:], ones_sb[:], q[:], start=True, stop=True),
                w=later(vsc, sqfree[seg]), inc="sPE")
            qsqfree[qb_] = vpe
            v1 = P.op("sc", lambda e, sp=sq_ps[seg]: e.activation(
                stat1[:], sp[:], AF.Ln, bias=eps_sb[0:1, 0:1], scale=1.0 / HD),
                w=later(vpe), inc="sSC")
            sqfree[seg] = v1
            P.op("sc", lambda e: e.drain())
            v2 = P.op("sc", lambda e: e.activation(
                stat2[:], stat1[:], AF.Exp, scale=-0.5),
                w=later(stat2_free), inc="sSC")
            return v2

        rope_ev = [None, None]

        def rope_emit(qb_, dst, doff, pcol, wsrc):
            src = qn_sb[qb_]
            csn = ropeA_t[:, :]
            snc = ropeB_t[:, :]
            P.op("dv", lambda e: e.drain())
            P.op("dv", lambda e: e.tensor_tensor(rs1[:, :], src[0:64, :], csn, OP.mult),
                 w=later(wsrc, rope_ev[0]))
            P.op("dv", lambda e: e.tensor_tensor(rs2[:, :], src[0:64, :], snc, OP.mult))
            P.op("dv", lambda e: e.drain())
            P.op("dv", lambda e: e.tensor_copy(rsa[:, :], rs1[32:64, :]))
            P.op("dv", lambda e: e.tensor_copy(rsb[:, :], rs2[32:64, :]))
            P.op("dv", lambda e: e.drain())
            P.op("dv", lambda e: e.tensor_tensor(
                dst[0:32, doff:doff + 512], rs1[0:32, :], rsa[:, :], OP.add))
            vd = P.op("dv", lambda e: e.tensor_tensor(
                dst[32:64, doff:doff + 512], rsb[:, :], rs2[0:32, :], OP.subtract),
                inc="sDV")
            rope_ev[1] = vd
            vg = P.op("gp", lambda e: e.tensor_copy(
                dst[64:128, doff:doff + 512], src[64:128, :]), w=later(wsrc), inc="sGP")
            return [vd, vg]

        for p in range(NPASS):
            t0 = p * 512
            P.dma("sq", ropeA_t[:, :], rope[:, t0:t0 + 512],
                  w=later(rope_ev[1]), ch="c3")
            rope_ev[0] = P.dma("sq", ropeB_t[:, :], ropeb[:, t0:t0 + 512], ch="c3")
            for fc in range(16):
                buf = fc % 2
                st = xstage[buf]
                v = P.dma("sq", st[:, 0:512], xT[fc * 128:(fc + 1) * 128, t0:t0 + 512],
                          w=later(xst_ev[buf], xfm_rd if fc < 2 else None),
                          ch=f"c{buf}")
                xst_ev[buf] = P.op(
                    "dv", lambda e, a=st[:, 0:512], b=x_fm[:, fc * 512:(fc + 1) * 512]:
                    e.tensor_copy(b, a), w=[v], inc="sDV")
            v_xf = ("sDV", P.cnt["sDV"])
            for asb, tsb in [(qa_bf, t1q), (ka_bf, t1k), (va_bf, t1v)]:
                for fc in range(16):
                    vpe = P.op("pe", lambda e, fc=fc, a=asb: e.matmul(
                        t1_ps[:], a[:, fc * RANK:(fc + 1) * RANK],
                        x_fm[:, fc * 512:(fc + 1) * 512],
                        start=(fc == 0), stop=(fc == 15)),
                        w=later(v_xf, t1free) if fc == 0 else (), inc="sPE")
                t1free = P.op("sc", lambda e, t=tsb: e.copy(t[:], t1_ps[:]),
                              w=[vpe], inc="sSC")
            v_t1 = t1free
            for h in range(HLOC):
                pi = h % 3
                for fc in range(16):
                    P.op("pe", lambda e, fc=fc, h=h, pp=qk_ps[pi]: e.matmul(
                        pp[:], wq_bf[:, fc * HLOC * HD + h * HD: fc * HLOC * HD + (h + 1) * HD],
                        x_fm[:, fc * 512:(fc + 1) * 512],
                        start=(fc == 0), stop=False),
                        w=later(qkfree[pi], v_t1) if fc == 0 else (), inc="sPE")
                vpe = P.op("pe", lambda e, h=h, pp=qk_ps[pi]: e.matmul(
                    pp[:], qb_bf[:, h * HD:(h + 1) * HD], t1q[:],
                    start=False, stop=True), inc="sPE")
                v2 = rms_stats(pi, 0, vpe)
                vmq = P.op("dv", lambda e, h=h: e.tensor_scalar(
                    mqrow[:], stat2[:], qg_sb[0:1, h:h + 1], INV_SQRT_HD,
                    OP.mult, OP.mult), w=later(v2, mqrow_free[0]), inc="sDV")
                stat2_free = vmq
                mqrow_free[0] = P.dma("sc",
                      mq_dram[h * NQC + p * 4: h * NQC + (p + 1) * 4, :]
                      .rearrange("(o c) n -> o c n", o=1),
                      mqrow[0:1, :].rearrange("o (c n) -> o c n", c=4), w=[vmq],
                      ch="c2")
                qb_ = h % 2
                vq = P.op("sc", lambda e, pp=qk_ps[pi], q=qn_sb[qb_]: e.copy(q[:], pp[:]),
                          w=later(vpe, qnfree[qb_]), inc="sSC")
                qkfree[pi] = vq
                qnfree[qb_] = rope_emit(qb_, q_sb, h * T + t0, t0, vq)
            for kv in range(KVLOC):
                pi = (HLOC + kv) % 3
                for fc in range(16):
                    P.op("pe", lambda e, fc=fc, kv=kv, pp=qk_ps[pi]: e.matmul(
                        pp[:], wk_bf[:, fc * KVLOC * HD + kv * HD: fc * KVLOC * HD + (kv + 1) * HD],
                        x_fm[:, fc * 512:(fc + 1) * 512],
                        start=(fc == 0), stop=False),
                        w=later(qkfree[pi]) if fc == 0 else (), inc="sPE")
                vpe = P.op("pe", lambda e, kv=kv, pp=qk_ps[pi]: e.matmul(
                    pp[:], kb_bf[:, kv * HD:(kv + 1) * HD], t1k[:],
                    start=False, stop=True), inc="sPE")
                v2 = rms_stats(pi, 0, vpe)
                vmk = P.op("dv", lambda e: e.tensor_copy(mkbr[:], stat2[:]),
                           w=[v2], inc="sDV")
                stat2_free = vmk
                vbc = P.op("pe", lambda e: e.matmul(
                    bc_ps[:], onesk1[:], mkbr[:], start=True, stop=True),
                    w=later(vmk, bcfree), inc="sPE")
                vbs = P.op("sc", lambda e: e.copy(bcev[:], bc_ps[:]), w=[vbc], inc="sSC")
                bcfree = vbs
                qb_ = kv % 2
                vkn = P.op("dv", lambda e, pp=qk_ps[pi], q=qn_sb[qb_]: e.tensor_tensor(
                    q[:], pp[:], bcev[:], OP.mult),
                    w=later(vbs, qnfree[qb_]), inc="sDV")
                qkfree[pi] = vkn
                qnfree[qb_] = rope_emit(qb_, k_sb, kv * T + t0, t0, vkn)
            for tc in range(4):
                buf = tc % 2
                gchunk = p * 4 + tc
                for fc in range(16):
                    P.op("pe", lambda e, fc=fc, tc=tc, pp=v_ps[buf]: e.matmul(
                        pp[:, 0:KVLOC * HD],
                        x_fm[:, fc * 512 + tc * 128: fc * 512 + (tc + 1) * 128],
                        wv_bf[:, fc * KVLOC * HD:(fc + 1) * KVLOC * HD],
                        start=(fc == 0), stop=False),
                        w=later(vpsfree[buf]) if fc == 0 else (), inc="sPE")
                vpe = P.op("pe", lambda e, tc=tc, pp=v_ps[buf]: e.matmul(
                    pp[:, 0:KVLOC * HD], t1v[:, tc * 128:(tc + 1) * 128], vb_bf[:],
                    start=False, stop=True), inc="sPE")
                vpsfree[buf] = P.op(
                    "sc", lambda e, g=gchunk, pp=v_ps[buf]: e.copy(
                        v_sb[:, g * KVLOC * HD:(g + 1) * KVLOC * HD],
                        pp[:, 0:KVLOC * HD]), w=[vpe], inc="sSC")
            xfm_rd = ("sPE", P.cnt["sPE"])
        es_pa.close()
        es_p1.close()
        P.gate()

        es_p2 = ExitStack()
        w8st = [es_p2.enter_context(SB(f"w8st{i}", [128, D], U8)) for i in range(2)]
        wbfst = [es_p2.enter_context(SB(f"wbfst{i}", [128, D], BF16)) for i in range(2)]
        mqt_bf = es_p2.enter_context(SB("mqt_bf", [128, HLOC * NQC], BF16))
        mqt = es_p2.enter_context(SB("mqt", [128, HLOC * NQC]))
        s_sb = [es_p2.enter_context(SB(f"s_sb{i}", [128, T], BF16)) for i in range(3)]
        p_sb = [es_p2.enter_context(SB(f"p_sb{i}", [128, T], BF16)) for i in range(3)]
        pt_sb = [es_p2.enter_context(SB(f"pt_sb{i}", [128, 1024], BF16)) for i in range(2)]
        mxt = [es_p2.enter_context(SB(f"mxt{i}", [128, 8])) for i in range(3)]
        ngm = [es_p2.enter_context(SB(f"ngm{i}", [128, 1])) for i in range(3)]
        rsum = [es_p2.enter_context(SB(f"rsum{i}", [128, 1])) for i in range(3)]
        rinv = [es_p2.enter_context(SB(f"rinv{i}", [128, 1])) for i in range(3)]
        vsq_sb = es_p2.enter_context(SB("vsq_sb", [128, KVLOC * HD], BF16))
        sv_sb = es_p2.enter_context(SB("sv_sb", [128, 2 * KVLOC]))
        rv_sb = es_p2.enter_context(SB("rv_sb", [128, KVLOC]))
        vh_tok = es_p2.enter_context(SB("vh_tok", [128, KVLOC * HD], BF16))
        vh_fm = es_p2.enter_context(SB("vh_fm", [128, KVLOC * HD], BF16))
        ysb = [es_p2.enter_context(SB(f"ysb{i}", [128, 128], BF16)) for i in range(4)]
        zsb = es_p2.enter_context(SB("zsb", [128, 128], BF16))
        ndsb = es_p2.enter_context(SB("ndsb", [1, 128], BF16))
        prsb = es_p2.enter_context(SB("prsb", [128, 128], BF16))
        es_pb = ExitStack()
        s_ps = [es_pb.enter_context(PS(f"s_ps{i}", [128, 512])) for i in range(3)]
        pt_ps = [es_pb.enter_context(PS(f"pt_ps{i}", [128, 1024], BF16)) for i in range(2)]
        y_ps = [es_pb.enter_context(PS(f"y_ps{i}", [128, 512])) for i in range(2)]
        miscA = es_pb.enter_context(PS("miscA", [128, 512]))

        vt = P.op("sq", lambda e: e.dma_start_transpose(mqt_bf[:], mq_dram[:, :]),
                  w=[("c2", P.cnt["c2"])], inc="c3", n=16)
        P.op("dv", lambda e: e.tensor_copy(mqt[:], mqt_bf[:]), w=[vt], inc="sDV")

        w8cv = [None, None]
        wbf_out = [None, None]
        sps_free = [None, None, None]
        ptps_free = [None, None]
        ptsb_free = [None, None]
        yps_free = [None, None]
        ssb_free = [None] * 3
        psb_free = [None] * 3
        mxt_free = [None] * 3
        ngm_free = [None] * 3
        rs_free = [None] * 3
        ri_free = [None] * 3
        st2 = {"vhat": None, "vhtok": None, "vtr": None, "vhfm": None,
               "z": None, "nd": None, "pr": None, "mA": None, "svf": None}
        ysb_free = [None] * 4
        ptrot = [0]

        def attn_head(qc, kv, g):
            h = kv * 4 + g
            ext = (qc + 1) * 128
            ng = (ext + 511) // 512
            sb = h % 3
            mb = h % 3
            col = h * NQC + qc
            scans = None
            for j in range(ng):
                w = min(512, ext - j * 512)
                pj = j % 3
                vpe = P.op("pe", lambda e, j=j, w=w, pj=pj, h=h, kv=kv, qc=qc: e.matmul(
                    s_ps[pj][:, 0:w],
                    q_sb[:, h * T + qc * 128: h * T + (qc + 1) * 128],
                    k_sb[:, kv * T + j * 512: kv * T + j * 512 + w],
                    start=True, stop=True),
                    w=later(sps_free[pj], ssb_free[sb] if j == 0 else None,
                            mxt_free[mb] if j == 0 else None), inc="sPE")
                scans = P.op("dv", lambda e, j=j, w=w, pj=pj, sb=sb, mb=mb, col=col:
                             e.tensor_scalar(
                                 s_sb[sb][:, j * 512: j * 512 + w], s_ps[pj][:, 0:w],
                                 mqt[:, col:col + 1], None, OP.mult, OP.max,
                                 accum_out=mxt[mb][:, j:j + 1]),
                             w=[vpe], inc="sDV")
                sps_free[pj] = scans
            vmask = P.op("dv", lambda e, sb=sb, qc=qc: e.tensor_tensor(
                s_sb[sb][:, qc * 128:(qc + 1) * 128],
                s_sb[sb][:, qc * 128:(qc + 1) * 128], mask_sb[:], OP.add),
                w=later(scans), inc="sDV")
            vngm = P.op("dv", lambda e, mb=mb, ng=ng: e.reduce_max(
                ngm[mb][:, 0:1], mxt[mb][:, 0:ng], axis=AX, negate=True),
                w=later(scans, ngm_free[mb]), inc="sDV")
            mxt_free[mb] = vngm
            vexp = P.op("sc", lambda e, sb=sb, mb=mb, ext=ext: e.activation(
                p_sb[sb][:, 0:ext], s_sb[sb][:, 0:ext], AF.Exp,
                bias=ngm[mb][:, 0:1], accum_out=rsum[mb][:, 0:1]),
                w=later(vmask, vngm, psb_free[sb], rs_free[mb]), inc="sSC")
            ssb_free[sb] = vexp
            ngm_free[mb] = vexp
            return dict(qc=qc, kv=kv, g=g, h=h, ext=ext, sb=sb, mb=mb, vexp=vexp)

        def attn_norm(ctx):
            sb, mb, ext = ctx["sb"], ctx["mb"], ctx["ext"]
            vri = P.op("dv", lambda e, mb=mb: e.reciprocal(
                rinv[mb][:, 0:1], rsum[mb][:, 0:1]),
                w=later(ctx["vexp"], ri_free[mb]), inc="sDV")
            rs_free[mb] = vri
            vpn = P.op("dv", lambda e, sb=sb, mb=mb, ext=ext: e.tensor_scalar(
                p_sb[sb][:, 0:ext], p_sb[sb][:, 0:ext], rinv[mb][:, 0:1],
                None, OP.mult), w=[vri], inc="sDV")
            ri_free[mb] = vpn
            ctx["vpn"] = vpn

        def attn_ptpv(ctx):
            qc, kv, g, sb = ctx["qc"], ctx["kv"], ctx["g"], ctx["sb"]
            yb = kv % 2
            nch = qc + 1
            groups = []
            for c0 in range(0, nch, 8):
                groups.append((c0, min(8, nch - c0)))

            def emit_T(c0, cw):
                tb = ptrot[0] % 2
                ptrot[0] += 1
                vtp = None
                for ci in range(cw):
                    c = c0 + ci
                    vtp = P.op("pe", lambda e, c=c, ci=ci, tb=tb, sb=sb: e.transpose(
                        pt_ps[tb][:, ci * 128:(ci + 1) * 128],
                        p_sb[sb][:, c * 128:(c + 1) * 128], iden_sb[:]),
                        w=later(ctx["vpn"] if ci == 0 else None,
                                ptps_free[tb] if ci == 0 else None), inc="sPE")
                vev = P.op("sc", lambda e, tb=tb, cw=cw: e.copy(
                    pt_sb[tb][:, 0:cw * 128], pt_ps[tb][:, 0:cw * 128]),
                    w=later(vtp, ptsb_free[tb]), inc="sSC")
                ptps_free[tb] = vev
                return tb, vev

            def emit_PV(c0, cw, tb, vev):
                vpv = None
                for ci in range(cw):
                    c = c0 + ci
                    vpv = P.op("pe", lambda e, c=c, ci=ci, tb=tb, kv=kv, g=g, yb=yb, nch=nch:
                               e.matmul(
                                   y_ps[yb][:, g * 128:(g + 1) * 128],
                                   v_sb[:, c * KVLOC * HD + kv * HD: c * KVLOC * HD + (kv + 1) * HD],
                                   pt_sb[tb][:, ci * 128:(ci + 1) * 128],
                                   start=(c == 0), stop=(c == nch - 1)),
                               w=later(vev if ci == 0 else None,
                                       yps_free[yb] if (c == 0 and g == 0) else None),
                               inc="sPE")
                ptsb_free[tb] = vpv
                return vpv

            prev = None
            vpv = None
            for c0, cw in groups:
                tb, vev = emit_T(c0, cw)
                if prev is not None:
                    vpv = emit_PV(*prev)
                prev = (c0, cw, tb, vev)
            vpv = emit_PV(*prev)
            psb_free[sb] = vpv
            ctx["vpv"] = vpv
            ctx["yb"] = yb
            return ctx

        def attn_tail(ctx):
            qc, kv, g, h, yb = ctx["qc"], ctx["kv"], ctx["g"], ctx["h"], ctx["yb"]
            vye = P.op("sc", lambda e, yb=yb, g=g: e.copy(
                ysb[g][:], y_ps[yb][:, g * 128:(g + 1) * 128]),
                w=later(ctx["vpv"], ysb_free[g]), inc="sSC")
            vz = P.op("dv", lambda e, kv=kv, g=g: e.tensor_tensor(
                zsb[:], ysb[g][:], vh_fm[:, kv * HD:(kv + 1) * HD], OP.mult),
                w=later(vye, st2["z"], ctx["vhfm"]), inc="sDV")
            vnd = P.op("pe", lambda e: e.matmul(
                miscA[0:1, 128:256], nones_sb[:], zsb[:], start=True, stop=True),
                w=later(vz, st2["mA"]), inc="sPE")
            st2["z"] = vnd
            vnde = P.op("dv", lambda e: e.tensor_copy(ndsb[:], miscA[0:1, 128:256]),
                        w=later(vnd, st2["nd"]), inc="sDV")
            vbc2 = P.op("pe", lambda e: e.matmul(
                miscA[:, 0:128], onesk1[:], ndsb[:], start=True, stop=True),
                w=[vnde], inc="sPE")
            st2["nd"] = vbc2
            vpr = P.op("dv", lambda e, kv=kv: e.tensor_tensor(
                prsb[:], vh_fm[:, kv * HD:(kv + 1) * HD], miscA[:, 0:128], OP.mult),
                w=later(vbc2, st2["pr"]), inc="sDV")
            st2["mA"] = vpr
            vyo = P.op("dv", lambda e, h=h, qc=qc, g=g: e.tensor_tensor(
                y_sb[:, h * T + qc * 128: h * T + (qc + 1) * 128],
                ysb[g][:], prsb[:], OP.add), w=[vpr], inc="sDV")
            st2["pr"] = vyo
            ysb_free[g] = vyo
            if g == 3:
                yps_free[yb] = vye
            return vyo

        icpq_num = NIC * 3
        for qc in range(NQC):
            j0 = (qc * icpq_num) // NQC
            j1 = ((qc + 1) * icpq_num) // NQC
            for j in range(j0, j1):
                ic, mat = divmod(j, 3)
                srcd, dstd = [(gt8, gbf_d), (ut8, ubf_d), (dt8, dbf_d)][mat]
                bi = j % 2
                vin = P.dma("sq", w8st[bi][:], srcd[ic], w=later(w8cv[bi]),
                            ch=f"c{10 + bi}")
                vcv = P.op("dv", lambda e, bi=bi: e.tensor_scalar(
                    wbfst[bi][:], w8st[bi][:], -128.0, None, OP.add),
                    w=later(vin, wbf_out[bi]), inc="sDV")
                w8cv[bi] = vcv
                wbf_out[bi] = P.dma("sc", dstd[ic], wbfst[bi][:], w=[vcv],
                                    ch=f"c{6 + bi}")
            vv = P.op("sc", lambda e, qc=qc: e.square(
                vsq_sb[:], v_sb[:, qc * KVLOC * HD:(qc + 1) * KVLOC * HD]),
                w=later(st2["vhat"]), inc="sSC")
            vred = None
            for kv in range(KVLOC):
                vred = P.op("dv", lambda e, kv=kv: e.reduce_sum(
                    sv_sb[:, kv:kv + 1], vsq_sb[:, kv * HD:(kv + 1) * HD], axis=AX),
                    w=later(vv, st2["svf"]) if kv == 0 else (), inc="sDV")
            st2["vhat"] = vred
            P.op("dv", lambda e: e.drain())
            vfl = P.op("dv", lambda e: e.tensor_scalar(
                sv_sb[:, KVLOC:2 * KVLOC], sv_sb[:, 0:KVLOC], 1e-24, None, OP.max),
                inc="sDV")
            vln = P.op("sc", lambda e: e.activation(
                sv_sb[:, 0:KVLOC], sv_sb[:, KVLOC:2 * KVLOC], AF.Ln),
                w=[vfl], inc="sSC")
            P.op("sc", lambda e: e.drain())
            vrv = P.op("sc", lambda e: e.activation(
                rv_sb[:], sv_sb[:, 0:KVLOC], AF.Exp, scale=-0.5), inc="sSC")
            st2["svf"] = vrv
            vhs = []
            for kv in range(KVLOC):
                vhs.append(P.op("dv", lambda e, kv=kv, qc=qc: e.tensor_scalar(
                    vh_tok[:, kv * HD:(kv + 1) * HD],
                    v_sb[:, qc * KVLOC * HD + kv * HD: qc * KVLOC * HD + (kv + 1) * HD],
                    rv_sb[:, kv:kv + 1], None, OP.mult),
                    w=later(vrv, st2["vhtok"]) if kv == 0 else (), inc="sDV"))
            vts = None
            vtb = ptrot[0] % 2
            ptrot[0] += 1
            for kv in range(KVLOC):
                vts = P.op("pe", lambda e, kv=kv, vtb=vtb: e.transpose(
                    pt_ps[vtb][:, kv * HD:(kv + 1) * HD],
                    vh_tok[:, kv * HD:(kv + 1) * HD], iden_sb[:]),
                    w=later(vhs[kv], ptps_free[vtb] if kv == 0 else None),
                    inc="sPE")
            st2["vhtok"] = vts
            vhf = P.op("dv", lambda e, vtb=vtb: e.tensor_copy(
                vh_fm[:], pt_ps[vtb][:, 0:KVLOC * HD]),
                w=later(vts, st2["vhfm_free"] if "vhfm_free" in st2 else None),
                inc="sDV")
            ptps_free[vtb] = vhf
            ctxs = []
            for kv in range(KVLOC):
                for g in range(4):
                    if len(ctxs) >= 2:
                        attn_norm(ctxs[-2])
                    c = attn_head(qc, kv, g)
                    c["vhfm"] = vhf
                    ctxs.append(c)
                    if len(ctxs) >= 3:
                        attn_ptpv(ctxs[-3])
            attn_norm(ctxs[-2])
            attn_ptpv(ctxs[-2])
            attn_norm(ctxs[-1])
            attn_ptpv(ctxs[-1])
            for c in ctxs:
                attn_tail(c)
            st2["vhfm_free"] = st2["pr"]
        es_pb.close()
        es_p2.close()
        es_kv.close()
        P.gate()

        es_p3 = ExitStack()
        wp_bf = es_p3.enter_context(SB("wp_bf", [128, HLOC * D], BF16))
        wpst = [es_p3.enter_context(SB(f"wpst{i}", [128, D])) for i in range(2)]
        lr_sb = es_p3.enter_context(SB("lr_sb", [RANK, T], BF16))
        cst = [es_p3.enter_context(SB(f"cst{i}", [128, 512], BF16)) for i in range(2)]
        es_pc = ExitStack()
        c_ps = [es_pc.enter_context(PS(f"c_ps{i}", [128, 512])) for i in range(2)]
        lr_ps = es_pc.enter_context(PS("lr_ps", [RANK, 512]))

        wpst_free = [None, None]
        for fc in range(HLOC):
            buf = fc % 2
            v = P.dma("sq", wpst[buf][:], wp[fc], w=later(wpst_free[buf]),
                      ch=f"c{buf}")
            wpst_free[buf] = P.op(
                "dv", lambda e, fc=fc, buf=buf: e.tensor_copy(
                    wp_bf[:, fc * D:(fc + 1) * D], wpst[buf][:]), w=[v], inc="sDV")
        v_wp = ("sDV", P.cnt["sDV"])
        lr_free = None
        for tp in range(T // 512):
            for fc in range(HLOC):
                vpe = P.op("pe", lambda e, tp=tp, fc=fc: e.matmul(
                    lr_ps[:], pa_bf[:, fc * RANK:(fc + 1) * RANK],
                    y_sb[:, fc * T + tp * 512: fc * T + (tp + 1) * 512],
                    start=(fc == 0), stop=(fc == HLOC - 1)),
                    w=later(lr_free) if fc == 0 else (), inc="sPE")
            lr_free = P.op("sc", lambda e, tp=tp: e.copy(
                lr_sb[:, tp * 512:(tp + 1) * 512], lr_ps[:]), w=[vpe], inc="sSC")
        v_lr = lr_free
        cps_free = [None, None]
        cst_free = [None, None]
        vcc = None
        ntcf = T // 128
        tpq = ntcf // 8
        for q in range(4):
            for tc in [q * tpq + i for i in range(tpq)] + \
                      [ntcf // 2 + q * tpq + i for i in range(tpq)]:
                if tc < ntcf // 2:
                    r0 = q * tpq * 256 + (tc - q * tpq) * 128
                else:
                    r0 = q * tpq * 256 + tpq * 128 + (tc - ntcf // 2 - q * tpq) * 128
                for fog in range(4):
                    buf = fog % 2
                    for fc in range(HLOC):
                        P.op("pe", lambda e, tc=tc, fog=fog, fc=fc, buf=buf: e.matmul(
                            c_ps[buf][:], y_sb[:, fc * T + tc * 128: fc * T + (tc + 1) * 128],
                            wp_bf[:, fc * D + fog * 512: fc * D + (fog + 1) * 512],
                            start=(fc == 0), stop=False),
                            w=later(cps_free[buf], v_wp if tc == 0 else None,
                                    v_lr if tc == 0 else None) if fc == 0 else (),
                            inc="sPE")
                    vpe = P.op("pe", lambda e, tc=tc, fog=fog, buf=buf: e.matmul(
                        c_ps[buf][:], lr_sb[:, tc * 128:(tc + 1) * 128],
                        pb_bf[:, fog * 512:(fog + 1) * 512],
                        start=False, stop=True), inc="sPE")
                    vev = P.op("sc", lambda e, buf=buf: e.copy(cst[buf][:], c_ps[buf][:]),
                               w=later(vpe, cst_free[buf]), inc="sSC")
                    cps_free[buf] = vev
                    cst_free[buf] = P.dma(
                        "sc", cc_in[r0:r0 + 128, fog * 512:(fog + 1) * 512],
                        cst[buf][:], w=[vev], ch=f"c{4 + buf}")
            v_allst = [("c4", P.cnt["c4"]), ("c5", P.cnt["c5"])]
            vcc = P.op("gp", lambda e, q=q: e.collective_compute(
                "ReduceScatter", mybir.AluOpType.add,
                replica_groups=[[0, 1], [2, 3], [4, 5], [6, 7]],
                ins=[cc_in[q * (T // 4):(q + 1) * (T // 4), :].opt()],
                outs=[cc_out[q * (T // 8):(q + 1) * (T // 8), :].opt()]),
                w=v_allst, inc="sCC")
        es_pc.close()
        es_p3.close()
        es_qy.close()
        P.gate()

        es_p4 = ExitStack()
        h_fm = es0.enter_context(SB("h_fm", [128, 16 * To], BF16))
        att_t = [es_p4.enter_context(SB(f"att{i}", [128, D], BF16)) for i in range(2)]
        xo_t = [es_p4.enter_context(SB(f"xo{i}", [128, D])) for i in range(2)]
        r1_t = es_p4.enter_context(SB("r1", [128, D]))
        htm = es_p4.enter_context(SB("htm", [128, D], BF16))
        ln1 = LN(P, nc, es_p4, "ln1", eps_sb[:, 1:2])
        es_pd = ExitStack()
        btr_ps = [es_pd.enter_context(PS(f"btr{i}", [128, 1024], BF16)) for i in range(2)]

        att_free = [None, None]
        xo_free = [None, None]
        r1_free = None
        htm_free = None
        btr_free = [None, None]
        for tc in range(NTC):
            buf = tc % 2
            va = P.dma("sq", att_t[buf][:], cc_out[tc * 128:(tc + 1) * 128, :],
                       w=later(vcc if tc < 2 else None, att_free[buf]),
                       ch=f"c{buf}")
            vx = P.dma("sq", xo_t[buf][:], x_own[tc * 128:(tc + 1) * 128, :],
                       w=later(xo_free[buf]), ch=f"c{2 + buf}")
            vr = P.op("dv", lambda e, buf=buf: e.tensor_tensor(
                r1_t[:], att_t[buf][:], xo_t[buf][:], OP.add),
                w=later(va, vx, r1_free), inc="sDV")
            att_free[buf] = vr
            xo_free[buf] = vr
            vh, r1_readers = ln1.norm(r1_t[:], htm[:], [vr], [htm_free], D)
            r1_free = r1_readers
            vtp_last = None
            for half in range(2):
                tb = half
                vtp = None
                for fc8 in range(8):
                    fc = half * 8 + fc8
                    vtp = P.op("pe", lambda e, fc=fc, fc8=fc8, tb=tb: e.transpose(
                        btr_ps[tb][:, fc8 * 128:(fc8 + 1) * 128],
                        htm[:, fc * 128:(fc + 1) * 128], iden_sb[:]),
                        w=later(vh if fc8 == 0 else None,
                                btr_free[tb] if fc8 == 0 else None), inc="sPE")
                vtp_last = vtp
                evs = []
                for fc8 in range(8):
                    fc = half * 8 + fc8
                    evs.append(P.op("dv", lambda e, fc=fc, fc8=fc8, tb=tb, tc=tc:
                                    e.tensor_copy(
                                        h_fm[:, fc * To + tc * 128: fc * To + (tc + 1) * 128],
                                        btr_ps[tb][:, fc8 * 128:(fc8 + 1) * 128]),
                                    w=[vtp] if fc8 == 0 else (), inc="sDV"))
                btr_free[tb] = evs[-1]
            htm_free = ("sPE", vtp_last[1])
        es_pd.close()
        es_p4.close()
        P.gate()

        es_p5 = ExitStack()
        acc = es_p5.enter_context(SB("acc", [128, FTC * D]))
        dw_bf = es_p5.enter_context(SB("dw_bf", [128, 8 * D], BF16))
        hmid = es_p5.enter_context(SB("hmid", [128, 8 * FP], BF16))
        gw_bf = [es_p5.enter_context(SB(f"gw_bf{i}", [128, D], BF16)) for i in range(3)]
        uw_bf = [es_p5.enter_context(SB(f"uw_bf{i}", [128, D], BF16)) for i in range(3)]
        silu_sb = es_p5.enter_context(SB("silu_sb", [128, FP], BF16))
        dsc_sb = es_p5.enter_context(SB("dsc_sb", [128, D]))
        htm2 = es_p5.enter_context(SB("htm2", [128, D], BF16))
        r2_t = es_p5.enter_context(SB("r2", [128, D]))
        outst = [es_p5.enter_context(SB(f"outst{i}", [128, D])) for i in range(2)]
        ln2 = LN(P, nc, es_p5, "ln2", eps_sb[:, 1:2], alloc_sq=False)
        es_pe_ = ExitStack()
        g_ps = es_pe_.enter_context(PS("g_ps", [128, FP]))
        u_ps = es_pe_.enter_context(PS("u_ps", [128, FP]))
        d_ps = [es_pe_.enter_context(PS(f"d_ps{i}", [128, 512])) for i in range(2)]
        bt2_ps = [es_pe_.enter_context(PS(f"bt2{i}", [128, 1024], BF16)) for i in range(2)]

        v_dsc = P.dma("sq", dsc_sb[:], dscT, ch="c6")
        gwf = [None, None, None]
        uwf = [None, None, None]
        dw_cvt = [None] * 8
        dw_used = None
        gpsf = None
        upsf = None
        dpsf = [None, None]
        hm_cvt = [None] * 8
        hm_used = None
        siluf = None
        accf = [None] * (FTC * 4)
        bt2f = [None, None]
        htm2f = None
        r2f = None
        outf = [None, None]

        for p in range(NFP):
            tp0 = p * FP
            for o in range(NOCT):
                for i8 in range(8):
                    ic = o * 8 + i8
                    bi = ic % 3
                    vgc = P.dma("sq", gw_bf[bi][:], gbf_d[ic], w=later(gwf[bi]),
                                ch=f"c{bi}")
                    vuc = P.dma("sc", uw_bf[bi][:], ubf_d[ic], w=later(uwf[bi]),
                                ch=f"c{8 + bi}")
                    vdc = P.dma("sc", dw_bf[:, i8 * D:(i8 + 1) * D], dbf_d[ic],
                                w=later(dw_used if i8 == 0 else None),
                                ch=f"c{4 + (i8 % 2)}")
                    dw_cvt[i8] = vdc
                    vgp = None
                    for fc in range(16):
                        vgp = P.op("pe", lambda e, fc=fc, bi=bi, tp0=tp0: e.matmul(
                            g_ps[:], gw_bf[bi][:, fc * 128:(fc + 1) * 128],
                            h_fm[:, fc * To + tp0: fc * To + tp0 + FP],
                            start=(fc == 0), stop=(fc == 15)),
                            w=later(vgc, gpsf) if fc == 0 else (), inc="sPE")
                    vup = None
                    for fc in range(16):
                        vup = P.op("pe", lambda e, fc=fc, bi=bi, tp0=tp0: e.matmul(
                            u_ps[:], uw_bf[bi][:, fc * 128:(fc + 1) * 128],
                            h_fm[:, fc * To + tp0: fc * To + tp0 + FP],
                            start=(fc == 0), stop=(fc == 15)),
                            w=later(vuc, upsf) if fc == 0 else (), inc="sPE")
                    gwf[bi] = vgp
                    uwf[bi] = vup
                    vsl = P.op("sc", lambda e, ic=ic: e.activation(
                        silu_sb[:], g_ps[:], AF.Silu, scale=gsc_sb[:, ic:ic + 1]),
                        w=later(vgp, siluf), inc="sSC")
                    gpsf = vsl
                    vhm = P.op("dv", lambda e, ic=ic, i8=i8: e.scalar_tensor_tensor(
                        hmid[:, i8 * FP:(i8 + 1) * FP], u_ps[:], usc_sb[:, ic:ic + 1],
                        silu_sb[:], OP.mult, OP.mult),
                        w=later(vup, vsl, hm_used if i8 == 0 else None), inc="sDV")
                    upsf = vhm
                    siluf = vhm
                    hm_cvt[i8] = vhm
                hm_max = ("sDV", max(v[1] for v in hm_cvt))
                dw_max4 = max(v[1] for v in dw_cvt if v[0] == "c4")
                dw_max5 = max((v[1] for v in dw_cvt if v[0] == "c5"), default=0)
                dw_max = ("c4", dw_max4)
                dw_max_b = ("c5", dw_max5) if dw_max5 else None
                vdw_pe = None
                for tc in range(FTC):
                    for fg in range(4):
                        db = (tc * 4 + fg) % 2
                        vdp = None
                        for i8 in range(8):
                            vdp = P.op("pe", lambda e, i8=i8, tc=tc, fg=fg, db=db:
                                       e.matmul(
                                           d_ps[db][:],
                                           hmid[:, i8 * FP + tc * 128: i8 * FP + (tc + 1) * 128],
                                           dw_bf[:, i8 * D + fg * 512: i8 * D + (fg + 1) * 512],
                                           start=(i8 == 0), stop=(i8 == 7)),
                                       w=later(dpsf[db], hm_max, dw_max, dw_max_b)
                                       if i8 == 0 else (), inc="sPE")
                        vdw_pe = vdp
                        ai = tc * 4 + fg
                        if o == 0:
                            vac = P.op("dv", lambda e, tc=tc, fg=fg, db=db: e.tensor_scalar(
                                acc[:, tc * D + fg * 512: tc * D + (fg + 1) * 512],
                                d_ps[db][:], 1.0, None, OP.mult),
                                w=later(vdp, accf[ai]), inc="sDV")
                        else:
                            vac = P.op("dv", lambda e, tc=tc, fg=fg, db=db:
                                       e.scalar_tensor_tensor(
                                           acc[:, tc * D + fg * 512: tc * D + (fg + 1) * 512],
                                           d_ps[db][:], 1.0,
                                           acc[:, tc * D + fg * 512: tc * D + (fg + 1) * 512],
                                           OP.mult, OP.add),
                                       w=later(vdp, accf[ai]), inc="sDV")
                        dpsf[db] = vac
                        accf[ai] = vac
                dw_used = ("sPE", vdw_pe[1])
                hm_used = dw_used
            for tc in range(FTC):
                tb = tc % 2
                vht = None
                vtp_last = None
                for half in range(2):
                    vtp = None
                    for fc8 in range(8):
                        fc = half * 8 + fc8
                        vtp = P.op("pe", lambda e, fc=fc, fc8=fc8, tb=tb, tc=tc, tp0=tp0:
                                   e.transpose(
                                       bt2_ps[tb][:, fc8 * 128:(fc8 + 1) * 128],
                                       h_fm[:, fc * To + tp0 + tc * 128: fc * To + tp0 + (tc + 1) * 128],
                                       iden_sb[:]),
                                   w=later(bt2f[tb] if fc8 == 0 else None,
                                           htm2f if (half == 0 and fc8 == 0) else None),
                                   inc="sPE")
                    vtp_last = vtp
                    evs = []
                    for fc8 in range(8):
                        fc = half * 8 + fc8
                        evs.append(P.op("dv", lambda e, fc=fc, fc8=fc8, tb=tb:
                                        e.tensor_copy(
                                            htm2[:, fc * 128:(fc + 1) * 128],
                                            bt2_ps[tb][:, fc8 * 128:(fc8 + 1) * 128]),
                                        w=[vtp] if fc8 == 0 else (), inc="sDV"))
                    bt2f[tb] = evs[-1]
                    vht = evs[-1]
                ai0 = tc * 4
                vr2a = P.op("dv", lambda e, tc=tc: e.tensor_tensor(
                    r2_t[:], acc[:, tc * D:(tc + 1) * D], dsc_sb[:], OP.mult),
                    w=later(accf[ai0], accf[ai0 + 1], accf[ai0 + 2], accf[ai0 + 3],
                            r2f, v_dsc), inc="sDV")
                vr2 = P.op("gp", lambda e: e.tensor_tensor(
                    r2_t[:], r2_t[:], htm2[:], OP.add), w=later(vr2a, vht), inc="sGP")
                htm2f = vr2
                ob = tc % 2
                vout, r2_readers = ln2.norm(r2_t[:], outst[ob][:], [vr2],
                                            [outf[ob]], D, sq_ap=outst[ob][:])
                r2f = r2_readers + [vr2]
                outf[ob] = P.dma("sc", out_d[tp0 + tc * 128: tp0 + (tc + 1) * 128, :],
                                 outst[ob][:], w=[vout], ch=f"c{6 + ob}")
                for i in range(4):
                    accf[ai0 + i] = vr2a
        es_pe_.close()
        es_p5.close()
        P.op("sc", lambda e: e.nop(),
             w=[("sDs", P.cnt["sDs"]), ("c6", P.cnt["c6"]), ("c7", P.cnt["c7"])])

        block = es0.enter_context(nc.Block())
        P.emit(block, sems)
    return nc


def _u8_low_byte(w_int32):
    a = np.ascontiguousarray(w_int32)
    assert a.dtype == np.int32
    return a.view(np.uint8).reshape(a.shape + (4,))[..., 0]


def prep_core(c, T, I, x, Wq, Wk, Wv, Wproj, q_gain, q_A, q_B, k_A, k_B,
              v_A, v_B, p_A, p_B, gate_w, gate_scale, up_w, up_scale,
              down_w, down_scale):
    b, s = c // 2, c % 2
    To = T // 2
    NIC = I // 128
    f0 = 1024 * s
    h0 = 8 * s
    kv0 = 256 * s
    bf = ml_dtypes.bfloat16
    tile16 = lambda a: np.ascontiguousarray(a).reshape(16, 128, a.shape[1])
    m = {}
    m["xT"] = np.ascontiguousarray(x[b].T)
    m["x_own"] = np.ascontiguousarray(x[b, s * To:(s + 1) * To, :])
    m["wq"] = tile16(Wq[:, f0:f0 + 1024])
    m["wk"] = tile16(Wk[:, kv0:kv0 + 256])
    m["wv"] = tile16(Wv[:, kv0:kv0 + 256])
    m["wp"] = np.ascontiguousarray(Wproj[f0:f0 + 1024, :]).reshape(8, 128, D)
    m["qa"] = tile16(q_A)
    m["ka"] = tile16(k_A)
    m["va"] = tile16(v_A)
    m["pa"] = np.ascontiguousarray(p_A[f0:f0 + 1024, :]).reshape(8, 128, RANK)
    m["qb"] = np.ascontiguousarray(q_B[:, f0:f0 + 1024])
    m["kb"] = np.ascontiguousarray(k_B[:, kv0:kv0 + 256])
    m["vb"] = np.ascontiguousarray(v_B[:, kv0:kv0 + 256])
    m["pb"] = np.ascontiguousarray(p_B)
    m["qg"] = np.ascontiguousarray(q_gain[h0:h0 + 8]).reshape(1, 8)
    inv = 10000.0 ** (-(np.arange(32, dtype=np.float64)) / 32.0)
    ang = np.arange(T, dtype=np.float64)[:, None] * inv[None, :]
    cosT, sinT = np.cos(ang).T, np.sin(ang).T
    m["rope"] = np.concatenate([cosT, sinT], 0).astype(np.float32).astype(bf)
    m["ropeb"] = np.concatenate([sinT, cosT], 0).astype(np.float32).astype(bf)
    mk = np.zeros((128, 128), np.float32)
    mk[np.triu_indices(128, 1)] = -30000.0
    m["maskt"] = mk
    m["iden"] = np.eye(128, dtype=np.float32).astype(bf)
    gT = np.ascontiguousarray(_u8_low_byte(gate_w).T)
    uT = np.ascontiguousarray(_u8_low_byte(up_w).T)
    dT = np.ascontiguousarray(_u8_low_byte(down_w).T)
    m["gt8"] = np.ascontiguousarray(
        gT.reshape(16, 128, NIC, 128).transpose(2, 1, 0, 3).reshape(NIC, 128, D))
    m["ut8"] = np.ascontiguousarray(
        uT.reshape(16, 128, NIC, 128).transpose(2, 1, 0, 3).reshape(NIC, 128, D))
    m["dt8"] = np.ascontiguousarray(dT.reshape(NIC, 128, D))
    m["gsc"] = np.ascontiguousarray(gate_scale.reshape(NIC, 128).T)
    m["usc"] = np.ascontiguousarray(up_scale.reshape(NIC, 128).T)
    m["dscT"] = np.ascontiguousarray(np.broadcast_to(down_scale[None, :], (128, D)))
    ep = np.empty((128, 2), np.float32)
    ep[:, 0] = RMS_EPS
    ep[:, 1] = LN_EPS
    m["epst"] = ep
    return m


def run(x, ws, T, I, core_ids=None):
    in_maps = [prep_core(c, T, I, x, *ws) for c in range(8)]
    nc = build(T, I)
    res = run_bass_kernel_spmd(nc, in_maps, core_ids=list(range(8)))
    out = np.empty((4, T, D), np.float32)
    To = T // 2
    for c in range(8):
        b, s = c // 2, c % 2
        out[b, s * To:(s + 1) * To, :] = res.results[c]["out"]
    return out


def kernel(L0_attn_x, L0_attn_Wq, L0_attn_Wk, L0_attn_Wv, L0_attn_Wproj,
           L0_attn_q_gain, L0_attn_q_A, L0_attn_q_B, L0_attn_k_A, L0_attn_k_B,
           L0_attn_v_A, L0_attn_v_B, L0_attn_p_A, L0_attn_p_B,
           L0_ffn_gate_w, L0_ffn_gate_scale, L0_ffn_up_w, L0_ffn_up_scale,
           L0_ffn_down_w, L0_ffn_down_scale):
    x = np.asarray(L0_attn_x, dtype=np.float32)
    ws = (np.asarray(L0_attn_Wq, np.float32), np.asarray(L0_attn_Wk, np.float32),
          np.asarray(L0_attn_Wv, np.float32), np.asarray(L0_attn_Wproj, np.float32),
          np.asarray(L0_attn_q_gain, np.float32), np.asarray(L0_attn_q_A, np.float32),
          np.asarray(L0_attn_q_B, np.float32), np.asarray(L0_attn_k_A, np.float32),
          np.asarray(L0_attn_k_B, np.float32), np.asarray(L0_attn_v_A, np.float32),
          np.asarray(L0_attn_v_B, np.float32), np.asarray(L0_attn_p_A, np.float32),
          np.asarray(L0_attn_p_B, np.float32),
          np.asarray(L0_ffn_gate_w, np.int32), np.asarray(L0_ffn_gate_scale, np.float32),
          np.asarray(L0_ffn_up_w, np.int32), np.asarray(L0_ffn_up_scale, np.float32),
          np.asarray(L0_ffn_down_w, np.int32), np.asarray(L0_ffn_down_scale, np.float32))
    return run(x, ws, 4096, 8192)
